# version 19
# baseline (speedup 1.0000x reference)
import numpy as np
import ml_dtypes
from contextlib import ExitStack
import concourse.bass as bass
import concourse.mybir as mybir
from concourse.alu_op_type import AluOpType as ALU
from concourse.bass_utils import run_bass_kernel_spmd

F32 = mybir.dt.float32
BF16 = mybir.dt.bfloat16
AF = mybir.ActivationFunctionType
AX = mybir.AxisListType

T = 4096
D = 1024
NH, DH = 16, 64
QKVP = 7168
NCH = 8
CSH = 40.0
bf16 = ml_dtypes.bfloat16

ENGS = ("t", "s", "v", "g", "y")
DMA_QUEUES = ("y", "g", "s")
NSLOT = 8


class _ArView:

    def __init__(self, base, off, p, n):
        self.base, self.off, self.p, self.n = base, off, p, n

    def __getitem__(self, idx):
        if idx == slice(None):
            return self.base[0:self.p, self.off:self.off + self.n]
        if isinstance(idx, tuple):
            ps, cs = idx
        else:
            ps, cs = idx, slice(None)
        p0 = ps.start or 0
        p1 = self.p if ps.stop is None else ps.stop
        c0 = cs.start or 0
        c1 = self.n if cs.stop is None else cs.stop
        if cs.step in (None, 1):
            return self.base[p0:p1, self.off + c0:self.off + c1]
        return self.base[p0:p1, self.off + c0:self.off + c1:cs.step]


class Em:

    def __init__(self, name):
        self.name = name
        self.nc = bass.Bass("TRN2", target_bir_lowering=False)
        self.st = ExitStack()
        self.sems = {}
        self.cnt = {}
        self.ops = {e: [] for e in ENGS}
        self.writer = {}
        self.readers = {}
        self.pending = {}
        self.pending_keys = set()
        self.dmaslot = {q: 0 for q in DMA_QUEUES}
        self.waited = {}
        self.out_tokens = []
        self.pfx = ""
        self.arenas = {}
        self.shared_psum = False
        self.barrier_junk = None

    def sem(self, name):
        if name not in self.sems:
            self.sems[name] = self.st.enter_context(self.nc.semaphore(name))
            self.cnt[name] = 0
        return self.sems[name]

    def sb(self, name, shape, dtype):
        return self.st.enter_context(self.nc.sbuf_tensor(
            self.pfx + name, list(shape), dtype))

    def ps(self, name, shape, dtype=F32):
        if not self.shared_psum:
            return self.st.enter_context(self.nc.psum_tensor(
                self.pfx + name, list(shape), dtype))
        if dtype == BF16:
            return self.psum_banks["bf"]
        i = self.psum_next
        self.psum_next += 1
        assert i < 7, "out of shared f32 psum banks"
        return self.psum_banks[i]

    def arena_reset(self, pfx, psum_base=0):
        self.pfx = pfx
        self.arena_off = {k: 0 for k in self.arenas}
        self.psum_next = psum_base

    def mk_arenas(self):
        self.arenas = {
            "A1": self.st.enter_context(
                self.nc.sbuf_tensor("AR1", [128, 50176], BF16)),
            "A2": self.st.enter_context(
                self.nc.sbuf_tensor("AR2", [128, 29568], BF16)),
            "AF": self.st.enter_context(
                self.nc.sbuf_tensor("ARF", [128, 5632], F32)),
        }
        self.arena_cols = {"A1": 50176, "A2": 29568, "AF": 5632}
        self.psum_banks = {}
        for i in range(7):
            self.psum_banks[i] = self.st.enter_context(
                self.nc.psum_tensor(f"PSB{i}", [128, 512], F32))
        self.psum_banks["bf"] = self.st.enter_context(
            self.nc.psum_tensor("PSBbf", [128, 1024], BF16))
        self.shared_psum = True
        self.arena_off = {k: 0 for k in self.arenas}
        self.psum_next = 0

    def salloc(self, name, shape, dtype):
        if not self.arenas:
            return self.sb(name, shape, dtype)
        p, n = shape
        want = "AF" if dtype == F32 else None
        if want is None:
            a1left = self.arena_cols["A1"] - self.arena_off["A1"]
            want = "A1" if a1left >= n else "A2"
        off = self.arena_off[want]
        assert off + n <= self.arena_cols[want], \
            f"arena {want} overflow at {self.pfx}{name} (+{n} at {off})"
        self.arena_off[want] = off + n
        return _ArView(self.arenas[want], off, p, n)

    def barrier(self):
        snap = dict(self.cnt)
        for e in ("t", "s", "v", "g", "y"):
            tag = ("E" + e) if e != "y" else "Qy"
            deps = {s: c for s, c in snap.items()
                    if c > 0 and s != ("E" + e if e != "y" else None)}

            def run(h, deps=deps, tag=tag):
                for s, c in sorted(deps.items()):
                    if self.waited.get((tag, s), 0) < c:
                        h.wait_ge(self.sems[s], c)
                        self.waited[(tag, s)] = c
            self.ops[e].append(run)
        self.writer = {}
        self.readers = {}

    def din(self, name, shape, dtype):
        return self.nc.dram_tensor(name, list(shape), dtype, kind="ExternalInput").ap()

    def dout(self, name, shape, dtype):
        return self.nc.dram_tensor(name, list(shape), dtype, kind="ExternalOutput").ap()

    def _dep_sems(self, reads, writes):
        deps = {}

        def add(sc):
            if sc and deps.get(sc[0], 0) < sc[1]:
                deps[sc[0]] = sc[1]
        for k in reads:
            assert k not in self.pending_keys, f"read of pending key {k}"
            add(self.writer.get(k))
            if k.startswith("ps:"):
                for s, c in self.readers.get(k, {}).items():
                    add((s, c))
        for k in writes:
            add(self.writer.get(k))
            for s, c in self.readers.get(k, {}).items():
                add((s, c))
        return deps

    def _commit(self, eng_sem, reads, writes, inc):
        if inc == 0:
            self.pending.setdefault(eng_sem, []).append((reads, writes))
            self.pending_keys.update(writes)
            return
        c = self.cnt[eng_sem]
        pend = self.pending.pop(eng_sem, [])
        for r, w in pend + [(reads, writes)]:
            for k in r:
                self.readers.setdefault(k, {})[eng_sem] = c
            for k in w:
                self.writer[k] = (eng_sem, c)
                self.readers[k] = {}
                self.pending_keys.discard(k)

    def op(self, eng, fn, reads=(), writes=(), inc=1):
        assert eng in ("t", "s", "v", "g")
        reads, writes = [str(k) for k in reads], [str(k) for k in writes]
        deps = self._dep_sems(reads, writes)
        sem_name = "E" + eng
        self.sem(sem_name)
        self.cnt[sem_name] += inc
        my = self.cnt[sem_name]
        sems = self.sems
        waited = self.waited

        def run(h, deps=deps, fn=fn, inc=inc, sem_name=sem_name):
            for s, c in sorted(deps.items()):
                if waited.get((sem_name, s), 0) < c:
                    h.wait_ge(sems[s], c)
                    waited[(sem_name, s)] = c
            instr = fn(h)
            if inc:
                instr.then_inc(sems[sem_name], inc)
        self.ops[eng].append(run)
        self._commit(sem_name, reads, writes, inc)
        return (sem_name, my)

    def dma(self, queue, out, in_, reads=(), writes=(), is_out=False):
        assert queue in DMA_QUEUES
        reads, writes = [str(k) for k in reads], [str(k) for k in writes]
        slot = self.dmaslot[queue]
        self.dmaslot[queue] = (slot + 1) % NSLOT
        sem_name = f"D{queue}{slot}"
        self.sem(sem_name)
        deps = self._dep_sems(reads, writes)
        if self.cnt[sem_name] > 0:
            deps[sem_name] = max(deps.get(sem_name, 0), self.cnt[sem_name])
        self.cnt[sem_name] += 16
        my = self.cnt[sem_name]
        eng_tag = "Q" + queue
        sems = self.sems
        waited = self.waited

        def run(h, deps=deps, out=out, in_=in_, sem_name=sem_name, my=my):
            for s, c in sorted(deps.items()):
                if waited.get((eng_tag, s), 0) < c:
                    h.wait_ge(sems[s], c)
                    waited[(eng_tag, s)] = c
            h.dma_start(out=out, in_=in_).then_inc(sems[sem_name], 16)
        self.ops[queue].append(run)
        self._commit(sem_name, reads, writes, 16)
        tok = (sem_name, my)
        if is_out:
            self.out_tokens.append(tok)
        return tok

    def finalize(self):
        assert not self.pending_keys, f"pending at finalize: {self.pending_keys}"
        block = self.st.enter_context(self.nc.Block())
        handles = {"t": block.tensor, "s": block.scalar, "v": block.vector,
                   "g": block.gpsimd, "y": block.sync}
        toks = list(self.out_tokens)
        for e in ENGS:
            oplist = self.ops[e]
            if not oplist and e != "y":
                continue

            def section(h, oplist=oplist, e=e):
                for run in oplist:
                    run(h)
                if e == "y":
                    for s, c in toks:
                        h.wait_ge(self.sems[s], c)
            handles[e](section)
        self.st.close()
        return self.nc


def fm_ap(X, c0, nch, t0, nt):
    return X[128 * c0:128 * (c0 + nch), t0:t0 + nt].rearrange(
        "(c p) t -> p c t", p=128)


def host_prep(inputs):
    f32 = np.float32
    d = {}
    _ = inputs["L0_ffn_router_w"], inputs["L0_ffn_router_b"]
    x = np.asarray(inputs["L0_attn_x"], f32)
    d["XF"] = [np.ascontiguousarray(x[b].T) for b in range(8)]

    def pack(v, n):
        return np.ascontiguousarray(np.asarray(v, f32).reshape(n, 128).T)

    d["WINT"] = np.ascontiguousarray(np.asarray(inputs["L0_attn_W_in"], f32).T)
    d["LNG_IN"] = pack(inputs["L0_attn_in_ln_g"], 8)
    d["LNB_IN"] = pack(inputs["L0_attn_in_ln_b"], 8)
    d["MIDG"] = pack(inputs["L0_attn_mid_ln_g"], 56)
    d["MIDB"] = pack(inputs["L0_attn_mid_ln_b"], 56)
    d["SLOPES"] = np.asarray(inputs["L0_attn_slopes"], f32).reshape(1, 16)
    WoT = np.ascontiguousarray(np.asarray(inputs["L0_attn_W_out"], f32).T)
    d["WOT_O"] = np.ascontiguousarray(
        WoT[:1024].reshape(16, 64, 1024).transpose(1, 0, 2).reshape(64, 16 * 1024))
    d["WOT_PG"] = np.ascontiguousarray(WoT[1024:])
    d["OUTG"] = pack(inputs["L0_attn_out_ln_g"], 8)
    d["OUTB"] = pack(inputs["L0_attn_out_ln_b"], 8)
    d["W1"] = np.ascontiguousarray(np.asarray(inputs["L0_ffn_w1"], f32)[0])
    d["B1"] = pack(np.asarray(inputs["L0_ffn_b1"], f32)[0], 32)
    d["W2"] = np.ascontiguousarray(np.asarray(inputs["L0_ffn_w2"], f32)[0])
    d["B2"] = pack(np.asarray(inputs["L0_ffn_b2"], f32)[0], 8)
    d["GAM"] = pack(inputs["L0_ffn_gamma"], 8)
    d["BET"] = pack(inputs["L0_ffn_beta"], 8)
    d["WA"] = np.ascontiguousarray(np.asarray(inputs["L1_attn_Wa"], f32))
    d["BA"] = np.asarray(inputs["L1_attn_ba"], f32).reshape(16, 1)
    d["WE"] = np.ascontiguousarray(np.asarray(inputs["L1_attn_We"], f32))
    d["BE"] = np.asarray(inputs["L1_attn_be"], f32).reshape(16, 1)
    d["WI"] = np.ascontiguousarray(np.asarray(inputs["L1_attn_Wi"], f32))
    d["WO"] = np.ascontiguousarray(np.asarray(inputs["L1_attn_Wo"], f32))
    d["WV"] = np.ascontiguousarray(np.asarray(inputs["L1_attn_Wv"], f32))
    d["WOUT1"] = np.ascontiguousarray(
        np.asarray(inputs["L1_attn_Wout"], f32).reshape(16, 64, 1024)
        .transpose(1, 0, 2).reshape(64, 16 * 1024))
    d["CHI"] = np.asarray(inputs["L1_attn_chi"], f32).reshape(1, 16)
    d["BETA_P"] = np.asarray(inputs["L1_attn_beta_p"], f32).reshape(1, 16)
    d["IV_P"] = np.asarray(inputs["L1_attn_iv_p"], f32).reshape(1, 16)
    d["OV_P"] = np.asarray(inputs["L1_attn_ov_p"], f32).reshape(1, 16)
    d["FW1T"] = np.ascontiguousarray(np.asarray(inputs["L1_ffn_w1_w"], f32).T)
    d["FW1S"] = pack(inputs["L1_ffn_w1_s"], 32)
    d["FW2T"] = np.ascontiguousarray(np.asarray(inputs["L1_ffn_w2_w"], f32).T)
    d["FW2S"] = pack(inputs["L1_ffn_w2_s"], 32)
    d["FH1T"] = np.ascontiguousarray(np.asarray(inputs["L1_ffn_h1_w"], f32).T)
    d["FH1S"] = pack(inputs["L1_ffn_h1_s"], 32)
    d["FH2T"] = np.ascontiguousarray(np.asarray(inputs["L1_ffn_h2_w"], f32).T)
    d["FH2S"] = pack(inputs["L1_ffn_h2_s"], 32)
    d["FW3T"] = np.ascontiguousarray(np.asarray(inputs["L1_ffn_w3_w"], f32).T)
    d["FW3S"] = pack(inputs["L1_ffn_w3_s"], 8)
    d["IOTA128"] = np.arange(128, dtype=f32).reshape(128, 1)
    d["IOTAT"] = np.arange(T, dtype=f32).reshape(1, T)
    hi = np.zeros((8, 128, 16), f32)
    for dc in range(8):
        hi[dc, 0:64, 2 * dc] = 1.0
        hi[dc, 64:128, 2 * dc + 1] = 1.0
    d["HEADIND"] = np.ascontiguousarray(hi.transpose(1, 0, 2).reshape(128, 128))
    d["SIGN64"] = np.concatenate([np.ones(32, f32), -np.ones(32, f32)]).reshape(64, 1)
    masks = np.zeros((4, 128, 512), f32)
    keep = np.zeros((4, 128, 512), f32)
    pidx = np.arange(128)[:, None]
    jidx = np.arange(512)[None, :]
    for m in range(4):
        masks[m] = np.where(jidx > 128 * m + pidx, -1e10, 0.0)
        keep[m] = (128 * m + pidx <= jidx).astype(f32)
    d["PREMASK"] = np.ascontiguousarray(masks.reshape(4 * 128, 512)).astype(bf16)
    d["KEEPMASK"] = np.ascontiguousarray(keep.reshape(4 * 128, 512)).astype(bf16)
    maskt = np.zeros((4, 128, 512), f32)
    for m in range(4):
        maskt[m] = np.where(128 * m + pidx > jidx, -1e10, 0.0)
    d["PREMASKT"] = np.ascontiguousarray(maskt.reshape(4 * 128, 512)).astype(bf16)
    d["IDENT"] = np.eye(128, dtype=f32).astype(bf16)
    sh = np.zeros((128, 128), f32)
    sh[np.arange(127), np.arange(1, 128)] = 1.0
    d["SHIFT1"] = sh.astype(bf16)
    oh = np.zeros((128, 128), f32)
    oh[127, 0] = 1.0
    d["OH127"] = oh.astype(bf16)
    return d


def emit_ln(em, B, src, src_key, nch, tg, bc_key, tag, src_is_f32=False):
    oc_ = B["onescol"]
    if src_is_f32:
        for c in range(nch):
            sl = src[:, c * tg:(c + 1) * tg]
            cbb = B["cb"][c % 2]
            em.op("s", lambda s, sl=sl, cbb=cbb: s.copy(cbb[:, 0:tg], sl),
                  reads=[src_key], writes=[f"k:cb{c % 2}"])
            em.op("t", lambda t, cbb=cbb, c=c: t.matmul(
                B["psS1"][0:1, 0:tg], oc_[:], cbb[:, 0:tg],
                start=(c == 0), stop=(c == nch - 1)),
                reads=[f"k:cb{c % 2}", "k:ones"],
                writes=["ps:S1"] if c in (0, nch - 1) else [], inc=1)
            sqb = B["sq"][c % 2]
            em.op("s", lambda s, cbb=cbb, sqb=sqb: s.activation(
                sqb[:, 0:tg], cbb[:, 0:tg], AF.Square, bias=0.0, scale=1.0),
                reads=[f"k:cb{c % 2}"], writes=[f"k:sq{c % 2}"])
            em.op("t", lambda t, sqb=sqb, c=c: t.matmul(
                B["psS2"][0:1, 0:tg], oc_[:], sqb[:, 0:tg],
                start=(c == 0), stop=(c == nch - 1)),
                reads=[f"k:sq{c % 2}", "k:ones"],
                writes=["ps:S2"] if c in (0, nch - 1) else [], inc=1)
    else:
        for c in range(nch):
            sl = src[:, c * tg:(c + 1) * tg]
            em.op("t", lambda t, sl=sl, c=c: t.matmul(
                B["psS1"][0:1, 0:tg], oc_[:], sl,
                start=(c == 0), stop=(c == nch - 1)),
                reads=[src_key, "k:ones"],
                writes=["ps:S1"] if c in (0, nch - 1) else [],
                inc=1 if c == nch - 1 else 0)
        for c in range(nch):
            sl = src[:, c * tg:(c + 1) * tg]
            sqb = B["sq"][c % 2]
            em.op("s", lambda s, sl=sl, sqb=sqb: s.activation(
                sqb[:, 0:tg], sl, AF.Square, bias=0.0, scale=1.0),
                reads=[src_key], writes=[f"k:sq{c % 2}"])
            em.op("t", lambda t, sqb=sqb, c=c: t.matmul(
                B["psS2"][0:1, 0:tg], oc_[:], sqb[:, 0:tg],
                start=(c == 0), stop=(c == nch - 1)),
                reads=[f"k:sq{c % 2}", "k:ones"],
                writes=["ps:S2"] if c in (0, nch - 1) else [], inc=1)
    em.op("s", lambda s: s.copy(B["s1r"][0:1, 0:tg], B["psS1"][0:1, 0:tg]),
          reads=["ps:S1"], writes=["k:s1r"])
    em.op("s", lambda s: s.copy(B["s2r"][0:1, 0:tg], B["psS2"][0:1, 0:tg]),
          reads=["ps:S2"], writes=["k:s2r"])
    em.op("t", lambda t: t.matmul(B["psB"][0:128, 0:tg], B[bc_key][0:1, :],
                                  B["s1r"][0:1, 0:tg], start=True, stop=True),
          reads=["k:s1r", "k:" + bc_key], writes=["ps:B"])
    em.op("s", lambda s: s.copy(B["mub"][:, 0:tg], B["psB"][0:128, 0:tg]),
          reads=["ps:B"], writes=["k:mub"])
    em.op("t", lambda t: t.matmul(B["psB"][0:128, 0:tg], B[bc_key][0:1, :],
                                  B["s2r"][0:1, 0:tg], start=True, stop=True),
          reads=["k:s2r", "k:" + bc_key], writes=["ps:B"])
    em.op("s", lambda s: s.copy(B["m2b"][:, 0:tg], B["psB"][0:128, 0:tg]),
          reads=["ps:B"], writes=["k:m2b"])
    em.op("s", lambda s: s.activation(B["mu2"][:, 0:tg], B["mub"][:, 0:tg],
                                      AF.Square, bias=0.0, scale=1.0),
          reads=["k:mub"], writes=["k:mu2"])
    em.op("v", lambda v: v.tensor_tensor(B["varb"][:, 0:tg], B["m2b"][:, 0:tg],
                                         B["mu2"][:, 0:tg], ALU.subtract),
          reads=["k:m2b", "k:mu2"], writes=["k:varb"])
    em.op("v", lambda v: v.tensor_scalar(B["varb"][:, 0:tg], B["varb"][:, 0:tg],
                                         1e-5, None, ALU.add),
          reads=["k:varb"], writes=["k:varb"])
    em.op("s", lambda s: s.activation(B["varb"][:, 0:tg], B["varb"][:, 0:tg],
                                      AF.Ln, bias=0.0, scale=1.0),
          reads=["k:varb"], writes=["k:varb"])
    em.op("s", lambda s: s.activation(B["rstd"][:, 0:tg], B["varb"][:, 0:tg],
                                      AF.Exp, bias=0.0, scale=-0.5),
          reads=["k:varb"], writes=["k:rstd"])


def emit_ln_apply(em, B, src, src_key, c, tg, out, out_key, scale=None,
                  bias=None, out2=None, out2_key=None):
    sl = src[:, c * tg:(c + 1) * tg]
    t1 = B["t1"]
    em.op("v", lambda v: v.tensor_tensor(t1[:, 0:tg], sl, B["mub"][:, 0:tg],
                                         ALU.subtract),
          reads=[src_key, "k:mub"], writes=["k:t1"])
    if scale is None:
        em.op("v", lambda v: v.tensor_tensor(out, t1[:, 0:tg],
                                             B["rstd"][:, 0:tg], ALU.mult),
              reads=["k:t1", "k:rstd"], writes=[out_key])
        if out2 is not None:
            em.op("s", lambda s: s.copy(out2, out), reads=[out_key],
                  writes=[out2_key])
    else:
        em.op("v", lambda v: v.tensor_tensor(t1[:, 0:tg], t1[:, 0:tg],
                                             B["rstd"][:, 0:tg], ALU.mult),
              reads=["k:t1", "k:rstd"], writes=["k:t1"])
        em.op("v", lambda v: v.tensor_scalar(out, t1[:, 0:tg], scale, bias,
                                             ALU.mult, ALU.add),
              reads=["k:t1", "k:par", "k:parb", "k:parc", "k:pard"],
              writes=[out_key])


def mk_ln_bufs(em, tg, maxch):
    B = {}
    B["onescol"] = em.sb("onescol", [128, 1], BF16)
    B["bc8"] = em.sb("bc8", [1, 128], F32)
    B["bc56"] = em.sb("bc56", [1, 128], F32)
    sqT = em.sb("sqT", [128, 2 * tg], BF16)
    B["sqT"] = sqT
    B["sq"] = [_ArView(sqT, i * tg, 128, tg) for i in range(2)]
    B["cb"] = [em.sb(f"cbb{i}", [128, tg], BF16) for i in range(2)]
    B["s1r"] = em.sb("s1r", [1, tg], F32)
    B["s2r"] = em.sb("s2r", [1, tg], F32)
    B["mub"] = em.sb("mub", [128, tg], F32)
    B["m2b"] = em.sb("m2b", [128, tg], F32)
    B["varb"] = em.sb("varb", [128, tg], F32)
    B["rstd"] = em.sb("rstd", [128, tg], F32)
    B["mu2"] = em.sb("mu2", [128, tg], F32)
    B["t1"] = em.sb("t1", [128, tg], F32)
    B["psS1"] = em.ps("psS1", [1, 512])
    B["psS2"] = em.ps("psS2", [1, 512])
    B["psB"] = em.ps("psB", [128, 512])
    em.op("v", lambda v: v.memset(B["onescol"][:], 1.0), writes=["k:ones"])
    em.op("v", lambda v: v.memset(B["bc8"][:], 1.0 / 1024.0), writes=["k:bc8"])
    em.op("v", lambda v: v.memset(B["bc56"][:], 1.0 / 7168.0), writes=["k:bc56"])
    return B


def build_phase_a(TG=256, Tt=T, em=None, io=None, B=None):
    if em is None:
        em = Em("A")
        io = {
            "XF": em.din("XF", [D, Tt], F32),
            "WINT": em.din("WINT", [D, QKVP], F32),
            "LNG_IN": em.din("LNG_IN", [128, 8], F32),
            "LNB_IN": em.din("LNB_IN", [128, 8], F32),
            "MIDG": em.din("MIDG", [128, 56], F32),
            "MIDB": em.din("MIDB", [128, 56], F32),
            "SHIFT1": em.din("SHIFT1", [128, 128], BF16),
            "OH127": em.din("OH127", [128, 128], BF16),
            "IDENT": em.din("IDENT", [128, 128], BF16),
            "XNF": em.dout("XNF", [D, Tt], F32),
            "XNB": em.dout("XNB", [D, Tt], BF16),
            "QH": em.dout("QH", [NH, DH, Tt], BF16),
            "KH": em.dout("KH", [NH, DH, Tt], BF16),
            "VSC": em.dout("VSC", [128, (Tt // 128) * 1056], BF16),
            "PG": em.dout("PG", [4096, Tt], BF16),
        }
    NG = Tt // TG
    KTPG = TG // 128

    (XF, WINT, LNG_IN, LNB_IN, MIDG, MIDB, SHIFT1, OH127, IDENT, XNF, XNB, QH, KH, VSC, PG) = (io[k] for k in ("XF", "WINT", "LNG_IN", "LNB_IN", "MIDG", "MIDB", "SHIFT1", "OH127", "IDENT", "XNF", "XNB", "QH", "KH", "VSC", "PG",))
    winTa = em.salloc("winTa", [128, 7 * QKVP], BF16)
    winTb = em.salloc("winTb", [128, QKVP], BF16)

    def winT_sl(c, lo, hi):
        if c < 7:
            return winTa[:, QKVP * c + lo:QKVP * c + hi]
        return winTb[:, lo:hi]
    lng = em.sb("lng", [128, 8], F32)
    lnb = em.sb("lnb", [128, 8], F32)
    midg = em.sb("midg", [128, 56], F32)
    midb = em.sb("midb", [128, 56], F32)
    qg_s = em.sb("qg_s", [128, 8], F32)
    qb_s = em.sb("qb_s", [128, 8], F32)
    shm = em.sb("shm", [128, 128], BF16)
    ohm = em.sb("ohm", [128, 128], BF16)
    idn = em.sb("idn", [128, 128], BF16)
    xg = em.salloc("xg", [128, NCH * TG], F32)
    xnf = em.salloc("xnf", [128, NCH * TG], F32)
    xnb = em.salloc("xnb", [128, NCH * TG], BF16)
    x1 = em.salloc("x1", [128, NCH * (TG + 4)], BF16)
    hgrp = em.salloc("hgrp", [128, 56 * TG], BF16)
    stage = [em.salloc(f"stage{i}", [128, TG], BF16) for i in range(2)]
    vstg = em.salloc("vstg", [128, TG], BF16)
    vaugg = em.salloc("vaugg", [128, KTPG * 1056], BF16)
    gbuf = [em.salloc(f"gbuf{i}", [128, TG], BF16) for i in range(2)]
    pgo = [em.salloc(f"pgo{i}", [128, TG], BF16) for i in range(2)]

    if B is None:
        B = mk_ln_bufs(em, TG, 56)
    psW = [em.ps(f"psW{i}", [128, 512]) for i in range(3)]
    psT = em.ps("psT", [128, 1024], BF16)
    psSh = em.ps("psSh", [128, 512])

    em.dma("y", lng[:], LNG_IN, writes=["k:par"])
    em.dma("y", lnb[:], LNB_IN, writes=["k:parb"])
    em.dma("y", midg[:], MIDG, writes=["k:parc"])
    em.dma("y", midb[:], MIDB, writes=["k:pard"])
    em.dma("g", shm[:], SHIFT1, writes=["k:shm"])
    em.dma("g", ohm[:], OH127, writes=["k:ohm"])
    em.dma("g", idn[:], IDENT, writes=["k:idn"])
    for c in range(NCH):
        em.dma("g", winT_sl(c, 0, QKVP),
               WINT[128 * c:128 * (c + 1), :], writes=[f"k:winT{c}"])
    em.op("v", lambda v: v.tensor_scalar(qg_s[:], midg[:, 0:8], 0.125, None,
                                         ALU.mult),
          reads=["k:parc"], writes=["k:par"])
    em.op("v", lambda v: v.tensor_scalar(qb_s[:], midb[:, 0:8], 0.125, None,
                                         ALU.mult),
          reads=["k:pard"], writes=["k:par"])
    em.op("v", lambda v: v.memset(x1[:, :], 0.0), writes=["k:x1"])
    em.op("v", lambda v: v.memset(vaugg[:, :], 0.0), writes=["k:vaugg"])

    for g in range(NG):
        t0 = g * TG
        em.dma("y", xg[:], fm_ap(XF, 0, NCH, t0, TG), writes=["k:xg"])

        emit_ln(em, B, xg, "k:xg", NCH, TG, "bc8", "e",
                src_is_f32=True)
        for c in range(NCH):
            emit_ln_apply(em, B, xg, "k:xg", c, TG,
                          xnf[:, c * TG:(c + 1) * TG], "k:xnf",
                          out2=xnb[:, c * TG:(c + 1) * TG], out2_key="k:xnb")
        em.dma("y", fm_ap(XNF, 0, NCH, t0, TG), xnf[:], reads=["k:xnf"],
               is_out=True)
        em.dma("y", fm_ap(XNB, 0, NCH, t0, TG), xnb[:], reads=["k:xnb"],
               is_out=True)

        for c in (0, 1):
            base = c * (TG + 4)
            em.op("v", lambda v, base=base: v.tensor_copy(
                x1[:, base:base + 4], x1[:, base + TG:base + TG + 4]),
                reads=["k:x1"], writes=["k:x1"])
        emit_ln(em, B, xnb, "k:xnb", NCH, TG, "bc8", "i")
        for c in range(NCH):
            base = c * (TG + 4)
            emit_ln_apply(em, B, xnb, "k:xnb", c, TG,
                          x1[:, base + 4:base + 4 + TG], "k:x1",
                          scale=lng[:, c:c + 1], bias=lnb[:, c:c + 1])

        def rhs_for(c):
            base = c * (TG + 4)
            if c == 0:
                return x1[:, base + 3:base + 3 + TG]
            if c == 1:
                return x1[:, base + 1:base + 1 + TG]
            return x1[:, base + 4:base + 4 + TG]

        def a_stats(oc):
            if oc % 2 == 0:
                em.op("t", lambda t, oc=oc: t.matmul(
                    B["psS1"][0:1, 0:2 * TG], B["onescol"][:],
                    hgrp[:, oc * TG:(oc + 2) * TG],
                    start=(oc == 0), stop=(oc == 54)),
                    reads=["k:hgrp", "k:ones"],
                    writes=["ps:S1"] if oc in (0, 54) else [], inc=1)
            sc0 = (oc % 2) * TG
            em.op("v", lambda v, oc=oc, sc0=sc0: v.tensor_tensor(
                B["sqT"][0:128, sc0:sc0 + TG],
                hgrp[:, oc * TG:(oc + 1) * TG],
                hgrp[:, oc * TG:(oc + 1) * TG], ALU.mult),
                reads=["k:hgrp"], writes=[f"k:sq{oc % 2}"])
            if oc % 2 == 1:
                em.op("t", lambda t, oc=oc: t.matmul(
                    B["psS2"][0:1, 0:2 * TG], B["onescol"][:],
                    B["sqT"][0:128, 0:2 * TG],
                    start=(oc == 1), stop=(oc == 55)),
                    reads=["k:sq0", "k:sq1", "k:ones"],
                    writes=["ps:S2"] if oc in (1, 55) else [], inc=1)

        for oc in range(56):
            pw = psW[oc % 3]
            for c in range(NCH):
                lhs = winT_sl(c, 128 * oc, 128 * (oc + 1))
                em.op("t", lambda t, lhs=lhs, c=c, pw=pw: t.matmul(
                    pw[:, 0:TG], lhs, rhs_for(c), start=(c == 0),
                    stop=(c == NCH - 1)),
                    reads=["k:x1", f"k:winT{c}"],
                    writes=[f"ps:W{oc % 3}"] if c in (0, NCH - 1) else [],
                    inc=1 if c == NCH - 1 else 0)
            hsl = hgrp[:, oc * TG:(oc + 1) * TG]
            em.op("s", lambda s, hsl=hsl, pw=pw, oc=oc: s.activation(
                hsl, pw[:, 0:TG], AF.Relu, bias=0.0, scale=1.0),
                reads=[f"ps:W{oc % 3}"], writes=["k:hgrp"])
            if oc >= 1:
                a_stats(oc - 1)
        a_stats(55)
        em.op("s", lambda s: s.copy(B["s1r"][0:1, 0:2 * TG],
                                    B["psS1"][0:1, 0:2 * TG]),
              reads=["ps:S1"], writes=["k:s1r"])
        em.op("v", lambda v: v.tensor_tensor(
            B["s1r"][0:1, 0:TG], B["s1r"][0:1, 0:TG],
            B["s1r"][0:1, TG:2 * TG], ALU.add),
            reads=["k:s1r"], writes=["k:s1r"])
        em.op("s", lambda s: s.copy(B["s2r"][0:1, 0:2 * TG],
                                    B["psS2"][0:1, 0:2 * TG]),
              reads=["ps:S2"], writes=["k:s2r"])
        em.op("v", lambda v: v.tensor_tensor(
            B["s2r"][0:1, 0:TG], B["s2r"][0:1, 0:TG],
            B["s2r"][0:1, TG:2 * TG], ALU.add),
            reads=["k:s2r"], writes=["k:s2r"])
        em.op("t", lambda t: t.matmul(B["psB"][0:128, 0:TG], B["bc56"][0:1, :],
                                      B["s1r"][0:1, 0:TG], start=True, stop=True),
              reads=["k:s1r", "k:bc56"], writes=["ps:B"])
        em.op("s", lambda s: s.copy(B["mub"][:, 0:TG], B["psB"][0:128, 0:TG]),
              reads=["ps:B"], writes=["k:mub"])
        em.op("t", lambda t: t.matmul(B["psB"][0:128, 0:TG], B["bc56"][0:1, :],
                                      B["s2r"][0:1, 0:TG], start=True, stop=True),
              reads=["k:s2r", "k:bc56"], writes=["ps:B"])
        em.op("s", lambda s: s.copy(B["m2b"][:, 0:TG], B["psB"][0:128, 0:TG]),
              reads=["ps:B"], writes=["k:m2b"])
        em.op("s", lambda s: s.activation(B["mu2"][:, 0:TG], B["mub"][:, 0:TG],
                                          AF.Square, bias=0.0, scale=1.0),
              reads=["k:mub"], writes=["k:mu2"])
        em.op("v", lambda v: v.tensor_tensor(B["varb"][:, 0:TG],
                                             B["m2b"][:, 0:TG],
                                             B["mu2"][:, 0:TG], ALU.subtract),
              reads=["k:m2b", "k:mu2"], writes=["k:varb"])
        em.op("v", lambda v: v.tensor_scalar(B["varb"][:, 0:TG],
                                             B["varb"][:, 0:TG],
                                             1e-5, None, ALU.add),
              reads=["k:varb"], writes=["k:varb"])
        em.op("s", lambda s: s.activation(B["varb"][:, 0:TG], B["varb"][:, 0:TG],
                                          AF.Ln, bias=0.0, scale=1.0),
              reads=["k:varb"], writes=["k:varb"])
        em.op("s", lambda s: s.activation(B["rstd"][:, 0:TG], B["varb"][:, 0:TG],
                                          AF.Exp, bias=0.0, scale=-0.5),
              reads=["k:varb"], writes=["k:rstd"])

        def mid_apply(oc, dst, dkey, sc, bi):
            hsl = hgrp[:, oc * TG:(oc + 1) * TG]
            em.op("v", lambda v, hsl=hsl: v.tensor_tensor(
                B["t1"][:, 0:TG], hsl, B["mub"][:, 0:TG], ALU.subtract),
                reads=["k:hgrp", "k:mub"], writes=["k:t1"])
            em.op("v", lambda v: v.tensor_tensor(
                B["t1"][:, 0:TG], B["t1"][:, 0:TG], B["rstd"][:, 0:TG],
                ALU.mult),
                reads=["k:t1", "k:rstd"], writes=["k:t1"])
            em.op("v", lambda v, dst=dst, sc=sc, bi=bi: v.tensor_scalar(
                dst, B["t1"][:, 0:TG], sc, bi, ALU.mult, ALU.add),
                reads=["k:t1", "k:par", "k:parb", "k:parc", "k:pard"],
                writes=[dkey])

        for oc in range(8):
            sg = stage[oc % 2]
            mid_apply(oc, sg[:, 0:TG], f"k:stage{oc % 2}",
                      qg_s[:, oc:oc + 1], qb_s[:, oc:oc + 1])
            em.dma("y", QH[2 * oc, :, t0:t0 + TG], sg[0:64, 0:TG],
                   reads=[f"k:stage{oc % 2}"], is_out=True)
            em.dma("y", QH[2 * oc + 1, :, t0:t0 + TG], sg[64:128, 0:TG],
                   reads=[f"k:stage{oc % 2}"], is_out=True)
        for oc in range(8, 16):
            sg = stage[oc % 2]
            mid_apply(oc, sg[:, 0:TG], f"k:stage{oc % 2}",
                      midg[:, oc:oc + 1], midb[:, oc:oc + 1])
            h0 = 2 * (oc - 8)
            em.dma("y", KH[h0, :, t0:t0 + TG], sg[0:64, 0:TG],
                   reads=[f"k:stage{oc % 2}"], is_out=True)
            em.dma("y", KH[h0 + 1, :, t0:t0 + TG], sg[64:128, 0:TG],
                   reads=[f"k:stage{oc % 2}"], is_out=True)
        for oc in range(16, 24):
            mid_apply(oc, vstg[:, 0:TG], "k:vstg",
                      midg[:, oc:oc + 1], midb[:, oc:oc + 1])
            for ktl in range(KTPG):
                em.op("t", lambda t, ktl=ktl: t.transpose(
                    psT[:, 0:128], vstg[:, 128 * ktl:128 * (ktl + 1)], idn[:]),
                    reads=["k:vstg", "k:idn"], writes=["ps:T"])
                h0 = 2 * (oc - 16)
                for hh in range(2):
                    cbase = 1056 * ktl + 66 * (h0 + hh)
                    em.op("s", lambda s, cbase=cbase, hh=hh: s.copy(
                        vaugg[:, cbase:cbase + 64],
                        psT[:, 64 * hh:64 * (hh + 1)]),
                        reads=["ps:T"], writes=["k:vaugg"])
        em.dma("y", VSC[:, 1056 * KTPG * g:1056 * KTPG * (g + 1)], vaugg[:],
               reads=["k:vaugg"], is_out=True)

        for oc in range(24, 56):
            mid_apply(oc, hgrp[:, oc * TG:(oc + 1) * TG], "k:hgrp",
                      midg[:, oc:oc + 1], midb[:, oc:oc + 1])

        def emit_pg_sh(j, i):
            cur = hgrp[:, j * TG:(j + 1) * TG]
            prev = hgrp[:, (55 if j == 24 else j - 1) * TG:
                        (56 if j == 24 else j) * TG]
            pb = psSh[:, 0:TG] if i % 2 == 0 else psW[0][:, 0:TG]
            pkey = "ps:Sh" if i % 2 == 0 else "ps:W0"
            em.op("t", lambda t, cur=cur, pb=pb: t.matmul(
                pb, shm[:], cur, start=True, stop=False),
                reads=["k:hgrp", "k:shm"], writes=[pkey], inc=0)
            em.op("t", lambda t, prev=prev, pb=pb: t.matmul(
                pb, ohm[:], prev, start=False, stop=True),
                reads=["k:hgrp", "k:ohm"], writes=[pkey])
            return pb, pkey

        order = list(range(25, 56)) + [24]
        shq = []
        for i, j in enumerate(order):
            pb, pkey = emit_pg_sh(j, i)
            em.op("s", lambda s, j=j, i=i: s.activation(
                gbuf[i % 2][:, 0:TG], hgrp[:, j * TG:(j + 1) * TG],
                AF.Gelu, bias=0.0, scale=1.0),
                reads=["k:hgrp"], writes=[f"k:gbuf{i % 2}"])
            em.op("v", lambda v, i=i, pb=pb: v.tensor_tensor(
                pgo[i % 2][:, 0:TG], gbuf[i % 2][:, 0:TG], pb, ALU.mult),
                reads=[f"k:gbuf{i % 2}", pkey], writes=[f"k:pgo{i % 2}"])
            em.dma("y", PG[128 * (j - 24):128 * (j - 23), t0:t0 + TG],
                   pgo[i % 2][:, 0:TG], reads=[f"k:pgo{i % 2}"], is_out=True)

    return em



def build_phase_b(Tt=T, em=None, io=None, B=None):
    if em is None:
        em = Em("B")
        io = {
            "QHD": em.din("QH", [NH, DH, Tt], BF16),
            "KHD": em.din("KH", [NH, DH, Tt], BF16),
            "VSCD": em.din("VSC", [128, (Tt // 128) * 1056], BF16),
            "SLOPES": em.din("SLOPES", [1, 16], F32),
            "IOTA128": em.din("IOTA128", [128, 1], F32),
            "IOTAT": em.din("IOTAT", [1, Tt], F32),
            "PREMASK": em.din("PREMASK", [512, 512], BF16),
            "PREMASKT": em.din("PREMASKT", [512, 512], BF16),
            "IDENT": em.din("IDENT", [128, 128], BF16),
            "OH": em.dout("OH", [NH, DH, Tt], BF16),
            "MH": em.nc.dram_tensor("MH", [NH, Tt], BF16, kind="Internal").ap(),
        }
    NQT = Tt // 128
    NQG = Tt // 512

    (QHD, KHD, VSCD, SLOPES, IOTA128, IOTAT, PREMASK, PREMASKT, IDENT, OH, MH) = (io[k] for k in ("QHD", "KHD", "VSCD", "SLOPES", "IOTA128", "IOTAT", "PREMASK", "PREMASKT", "IDENT", "OH", "MH",))
    vsc = em.salloc("vsc", [128, (Tt // 128) * 1056], BF16)
    kb = [em.salloc(f"kb{i}", [66, Tt], BF16) for i in range(2)]
    qb = [em.salloc(f"qb{i}", [66, Tt], BF16) for i in range(2)]
    slp = em.sb("slp", [1, 16], F32)
    slpb = em.sb("slpb", [128, 16], F32)
    onesf = em.sb("onesf", [1, 128], F32)
    onesb64 = em.sb("onesb64", [128, 64], BF16)
    io128 = em.sb("io128", [128, 1], F32)
    iot = em.salloc("iot", [1, Tt], F32)
    alro = em.salloc("alro", [1, Tt], BF16)
    albc = em.salloc("albc", [128, NH * 32], F32)
    prem = em.salloc("prem", [128, 4 * 512], BF16)
    premt = em.salloc("premt", [128, 4 * 512], BF16)
    idn = em.sb("idn", [128, 128], BF16)
    rmax = em.salloc("rmax", [128, NQT], BF16)
    rtmp = em.sb("rtmp", [128, 4], BF16)
    mrow = em.salloc("mrow", [32, 128], BF16)
    abuf = em.salloc("abuf", [128, 4 * 512], BF16)
    zrs = em.salloc("zrs", [128, 512], F32)
    zrb = em.salloc("zrb", [128, 512], BF16)
    fbc = em.salloc("fbc", [64, 512], F32)
    obuf = em.salloc("obuf", [64, 512], BF16)

    psS = [em.ps(f"psS{i}", [128, 512]) for i in range(4)]
    psO = [em.ps(f"psO{i}", [128, 512]) for i in range(2)]
    psB = em.ps("psB", [128, 512])
    psTr = em.ps("psTr", [128, 1024], BF16)

    em.dma("y", vsc[:], VSCD, writes=["k:vsc"])
    em.dma("y", slp[:], SLOPES, writes=["k:slp"])
    em.dma("y", io128[:], IOTA128, writes=["k:io"])
    em.dma("y", iot[:], IOTAT, writes=["k:io"])
    em.dma("y", prem[:], PREMASK.rearrange("(m p) j -> p m j", p=128),
           writes=["k:prem"])
    em.dma("y", premt[:], PREMASKT.rearrange("(m p) j -> p m j", p=128),
           writes=["k:premt"])
    em.dma("y", idn[:], IDENT, writes=["k:idn"])
    em.op("v", lambda v: v.memset(onesf[:], 1.0), writes=["k:ones"])
    em.op("v", lambda v: v.memset(onesb64[:], 1.0), writes=["k:ones64"])
    for i in range(2):
        em.op("v", lambda v, i=i: v.memset(kb[i][64:65, :], 1.0),
              writes=[f"k:kb{i}"])
        em.op("v", lambda v, i=i: v.memset(qb[i][64:66, :], 1.0),
              writes=[f"k:qb{i}"])
    nt = (Tt // 128) * 16
    em.op("v", lambda v: v.memset(vsc[:, 64:1056 * (Tt // 128):66], 1.0),
          reads=[], writes=["k:vsc"])
    em.op("t", lambda t: t.matmul(psB[0:128, 0:16], onesf[0:1, :], slp[0:1, :],
                                  start=True, stop=True),
          reads=["k:ones", "k:slp"], writes=["ps:B"])
    em.op("s", lambda s: s.copy(slpb[:], psB[0:128, 0:16]),
          reads=["ps:B"], writes=["k:slpb"])
    for h in range(NH):
        for kt in range(Tt // 128):
            em.op("v", lambda v, h=h, kt=kt: v.tensor_scalar(
                albc[:, 32 * h + kt:32 * h + kt + 1], io128[:],
                float(128 * kt), slpb[:, h:h + 1], ALU.add, ALU.mult),
                reads=["k:io", "k:slpb"], writes=["k:albc"], inc=0)
    em.op("v", lambda v: v.tensor_copy(rtmp[:, 0:1], io128[:]),
          reads=["k:io"], writes=["k:rtmp0"])

    def b_prepass(h):
        par = h % 2
        kbu, qbu = kb[par], qb[par]
        kkey, qkey = f"k:kb{par}", f"k:qb{par}"
        em.dma("y", kbu[0:64, :], KHD[h], reads=[], writes=[kkey])
        em.dma("y", qbu[0:64, :], QHD[h], reads=[], writes=[qkey])
        em.op("v", lambda v, h=h: v.tensor_scalar(
            alro[:], iot[:], slp[0:1, h:h + 1], None, ALU.mult),
            reads=["k:io", "k:slp", "k:alro"], writes=["k:alro"])
        em.dma("s", kbu[65:66, :], alro[0:1, :], reads=["k:alro"],
               writes=[kkey])
        em.op("v", lambda v, qbu=qbu: v.memset(qbu[64:65, :], 0.0),
              writes=[qkey])
        for qt in range(NQT):
            for kg in range(qt // 4 + 1):
                ip = (qt * 9 + kg) % 4
                pp = psS[ip]
                diag = (kg == qt // 4)
                em.op("t", lambda t, pp=pp, qt=qt, kg=kg, qbu=qbu, kbu=kbu,
                      diag=diag:
                      t.matmul(pp[:, 0:512],
                               qbu[0:66, 128 * qt:128 * (qt + 1)],
                               kbu[0:66, 512 * kg:512 * (kg + 1)],
                               start=True, stop=not diag),
                      reads=[qkey, kkey], writes=[f"ps:S{ip}"],
                      inc=0 if diag else 1)
                if diag:
                    m = qt % 4
                    em.op("t", lambda t, pp=pp, m=m: t.matmul(
                        pp[:, 0:512], idn[:],
                        prem[:, 512 * m:512 * (m + 1)],
                        start=False, stop=True),
                        reads=["k:idn", "k:prem"],
                        writes=[f"ps:S{ip}"])
                if kg == 0:
                    em.op("v", lambda v, pp=pp, qt=qt: v.reduce_max(
                        rmax[:, qt:qt + 1], pp[:, 0:512], axis=AX.X),
                        reads=[f"ps:S{ip}"], writes=["k:rmax"])
                else:
                    em.op("v", lambda v, pp=pp, qt=qt: v.reduce_max(
                        rtmp[:, qt % 4:qt % 4 + 1], pp[:, 0:512], axis=AX.X),
                        reads=[f"ps:S{ip}"], writes=[f"k:rtmp{qt % 4}"])
                    em.op("v", lambda v, qt=qt: v.tensor_tensor(
                        rmax[:, qt:qt + 1], rmax[:, qt:qt + 1],
                        rtmp[:, qt % 4:qt % 4 + 1], ALU.max),
                        reads=[f"k:rtmp{qt % 4}", "k:rmax"], writes=["k:rmax"])
        em.op("t", lambda t: t.transpose(psTr[0:NQT, 0:128], rmax[:, 0:NQT],
                                         idn[:]),
              reads=["k:rmax", "k:idn"], writes=["ps:Tr"])
        em.op("s", lambda s: s.activation(mrow[0:NQT, :], psTr[0:NQT, 0:128],
                                          AF.Copy, bias=CSH, scale=-1.0),
              reads=["ps:Tr"], writes=["k:mrow"])
        em.dma("y", MH[h:h + 1, :].rearrange("o (a b) -> (o a) b", b=128),
               mrow[0:NQT, :], reads=["k:mrow"], writes=["k:MH"])
        em.dma("y", qbu[64:65, :], MH[h:h + 1, :], reads=["k:MH"],
               writes=[qkey])

    def b_realpass(h):
        par = h % 2
        kbu, qbu = kb[par], qb[par]
        kkey, qkey = f"k:kb{par}", f"k:qb{par}"
        for qg in range(NQG):
            nkt = 4 * qg + 4
            LAG = 2
            pso = psO[qg % 2]
            pokey = f"ps:O{qg % 2}"

            def emit_o(kt, nkt=None, h=h, pso=pso, pokey=pokey):
                asl = abuf[:, 512 * (kt % 4):512 * (kt % 4 + 1)]
                em.op("t", lambda t, asl=asl, kt=kt, nkt=nkt, h=h, pso=pso:
                      t.matmul(
                    pso[0:66, 0:512], vsc[:, 1056 * kt + 66 * h:
                                          1056 * kt + 66 * h + 66],
                    asl, start=(kt == 0), stop=(kt == nkt - 1)),
                    reads=[f"k:ab{kt % 4}", "k:vsc"],
                    writes=[pokey] if kt in (0, nkt - 1) else [],
                    inc=1 if kt == nkt - 1 else 0)

            for kt in range(nkt):
                ss = psS[kt % 4]
                diag = kt >= 4 * qg
                em.op("t", lambda t, ss=ss, kt=kt, qg=qg, kbu=kbu, qbu=qbu,
                      diag=diag:
                      t.matmul(ss[:, 0:512],
                               kbu[0:65, 128 * kt:128 * (kt + 1)],
                               qbu[0:65, 512 * qg:512 * (qg + 1)],
                               start=True, stop=not diag),
                      reads=[kkey, qkey], writes=[f"ps:S{kt % 4}"],
                      inc=0 if diag else 1)
                if diag:
                    m = kt - 4 * qg
                    em.op("t", lambda t, ss=ss, m=m: t.matmul(
                        ss[:, 0:512], idn[:],
                        premt[:, 512 * m:512 * (m + 1)],
                        start=False, stop=True),
                        reads=["k:idn", "k:premt"],
                        writes=[f"ps:S{kt % 4}"])
                asl = abuf[:, 512 * (kt % 4):512 * (kt % 4 + 1)]
                em.op("s", lambda s, ss=ss, asl=asl, h=h, kt=kt: s.activation(
                    asl, ss[:, 0:512], AF.Exp,
                    bias=albc[:, 32 * h + kt:32 * h + kt + 1], scale=1.0),
                    reads=[f"ps:S{kt % 4}", "k:albc"],
                    writes=[f"k:ab{kt % 4}"])
                if kt >= LAG:
                    emit_o(kt - LAG, nkt)
            for kt in range(max(0, nkt - LAG), nkt):
                emit_o(kt, nkt)
            em.op("s", lambda s, pso=pso: s.activation(
                zrs[64:65, 0:512], pso[64:65, 0:512], AF.Ln,
                bias=0.0, scale=2.0 ** -24),
                  reads=[pokey], writes=["k:zrs"])
            em.op("s", lambda s: s.activation(
                zrb[64:65, 0:512], zrs[64:65, 0:512], AF.Exp,
                bias=0.0, scale=-1.0),
                  reads=["k:zrs"], writes=["k:zrb"])
            em.op("t", lambda t: t.matmul(psB[0:64, 0:512],
                                          onesb64[64:65, 0:64],
                                          zrb[64:65, 0:512],
                                          start=True, stop=True),
                  reads=["k:zrb", "k:ones64"], writes=["ps:B"])
            em.op("s", lambda s: s.activation(
                fbc[0:64, :], psB[0:64, 0:512], AF.Copy,
                bias=0.0, scale=2.0 ** -24),
                  reads=["ps:B"], writes=["k:fbc"])
            em.op("v", lambda v, pso=pso: v.tensor_tensor(
                obuf[:], pso[0:64, 0:512], fbc[:], ALU.mult),
                  reads=[pokey, "k:fbc"], writes=["k:obuf"])
            em.dma("y", OH[h, :, 512 * qg:512 * (qg + 1)], obuf[:],
                   reads=["k:obuf"], is_out=True)

    b_prepass(0)
    for h in range(NH):
        if h + 1 < NH:
            b_prepass(h + 1)
        b_realpass(h)
    return em
def build_phase_c(TG=256, Tt=T, em=None, io=None, B=None):
    if em is None:
        em = Em("C")
        io = {
            "OHD": em.din("OH", [NH, DH, Tt], BF16),
            "PGD": em.din("PG", [4096, Tt], BF16),
            "WOT_O": em.din("WOT_O", [64, 16 * 1024], F32),
            "WOT_PG": em.din("WOT_PG", [4096, 1024], F32),
            "OUTG": em.din("OUTG", [128, 8], F32),
            "OUTB": em.din("OUTB", [128, 8], F32),
            "XFD": em.din("XF", [D, Tt], F32),
            "XMID": em.dout("XMID", [D, Tt], F32),
        }
    NG = Tt // TG

    (OHD, PGD, WOT_O, WOT_PG, OUTG, OUTB, XFD, XMID) = (io[k] for k in ("OHD", "PGD", "WOT_O", "WOT_PG", "OUTG", "OUTB", "XFD", "XMID",))
    woto = em.salloc("woto", [64, 16 * 1024], BF16)
    wotp = em.salloc("wotp", [128, 32 * 1024], BF16)
    outg = em.sb("outg", [128, 8], F32)
    outb = em.sb("outb", [128, 8], F32)
    ohg = em.salloc("ohg", [64, 16 * TG], BF16)
    pgg = em.salloc("pgg", [128, 32 * TG], BF16)
    wg = em.salloc("wg", [128, NCH * TG], F32)
    wgb = em.salloc("wgb", [128, NCH * TG], BF16)
    xg = em.salloc("xg", [128, TG], F32)
    t2 = em.salloc("t2", [128, TG], F32)
    if B is None:
        B = mk_ln_bufs(em, TG, 8)
    psA = [em.ps(f"psA{i}", [128, 512]) for i in range(2)]
    psP2 = [em.ps(f"psP2{i}", [128, 512]) for i in range(2)]

    for i in range(4):
        em.dma("g", woto[:, 4096 * i:4096 * (i + 1)],
               WOT_O[:, 4096 * i:4096 * (i + 1)], writes=[f"k:woto{i}"])
    for f in range(32):
        em.dma("g", wotp[:, 1024 * f:1024 * (f + 1)],
               WOT_PG[128 * f:128 * (f + 1), :], writes=[f"k:wotpf{f}"])
    em.dma("y", outg[:], OUTG, writes=["k:par"])
    em.dma("y", outb[:], OUTB, writes=["k:parb"])

    for g in range(NG):
        t0 = g * TG
        for h in range(NH):
            em.dma("y", ohg[:, TG * h:TG * (h + 1)], OHD[h, :, t0:t0 + TG],
                   writes=["k:ohg"])
        em.dma("y", pgg[:], fm_ap(PGD, 0, 32, t0, TG), writes=["k:pgg"])
        for dc in range(NCH):
            pa, pp2 = psA[dc % 2], psP2[dc % 2]
            for h in range(NH):
                em.op("t", lambda t, h=h, dc=dc, pa=pa: t.matmul(
                    pa[0:128, 0:TG],
                    woto[0:64, 1024 * h + 128 * dc:1024 * h + 128 * (dc + 1)],
                    ohg[0:64, TG * h:TG * (h + 1)],
                    start=(h == 0), stop=(h == NH - 1)),
                    reads=["k:ohg", f"k:woto{h // 4}"],
                    writes=[f"ps:A{dc % 2}"] if h in (0, NH - 1) else [],
                    inc=1 if h == NH - 1 else 0)
            for f in range(32):
                em.op("t", lambda t, f=f, dc=dc, pp2=pp2: t.matmul(
                    pp2[:, 0:TG],
                    wotp[:, 1024 * f + 128 * dc:1024 * f + 128 * (dc + 1)],
                    pgg[:, TG * f:TG * (f + 1)],
                    start=(f == 0), stop=(f == 31)),
                    reads=["k:pgg", f"k:wotpf{f}"],
                    writes=[f"ps:P2{dc % 2}"] if f in (0, 31) else [],
                    inc=1 if f == 31 else 0)
            em.op("s", lambda s, pp2=pp2, dc=dc: s.copy(t2[:, 0:TG],
                                                          pp2[:, 0:TG]),
                  reads=[f"ps:P2{dc % 2}"], writes=["k:t2"])
            wsl = wg[:, dc * TG:(dc + 1) * TG]
            em.op("v", lambda v, wsl=wsl, pa=pa, dc=dc: v.tensor_tensor(
                wsl, pa[0:128, 0:TG], t2[:, 0:TG], ALU.add),
                reads=[f"ps:A{dc % 2}", "k:t2"], writes=["k:wg"])
            em.op("s", lambda s, dc=dc, wsl=wsl: s.copy(
                wgb[:, dc * TG:(dc + 1) * TG], wsl),
                reads=["k:wg"], writes=["k:wgb"])
        emit_ln(em, B, wgb, "k:wgb", NCH, TG, "bc8", "o")
        for dc in range(NCH):
            wsl = wg[:, dc * TG:(dc + 1) * TG]
            em.op("v", lambda v, wsl=wsl: v.tensor_tensor(
                B["t1"][:, 0:TG], wsl, B["mub"][:, 0:TG], ALU.subtract),
                reads=["k:wg", "k:mub"], writes=["k:t1"])
            em.op("v", lambda v: v.tensor_tensor(
                B["t1"][:, 0:TG], B["t1"][:, 0:TG], B["rstd"][:, 0:TG],
                ALU.mult),
                reads=["k:t1", "k:rstd"], writes=["k:t1"])
            em.op("v", lambda v, dc=dc: v.tensor_scalar(
                B["t1"][:, 0:TG], B["t1"][:, 0:TG], outg[:, dc:dc + 1],
                outb[:, dc:dc + 1], ALU.mult, ALU.add),
                reads=["k:t1", "k:par", "k:parb"], writes=["k:t1"])
            em.dma("y", xg[:, 0:TG], fm_ap(XFD, dc, 1, t0, TG),
                   writes=["k:xg"])
            em.op("v", lambda v, dc=dc, wsl=wsl: v.tensor_tensor(
                wsl, B["t1"][:, 0:TG], xg[:, 0:TG], ALU.add),
                reads=["k:t1", "k:xg"], writes=["k:wg"])
        em.dma("y", fm_ap(XMID, 0, NCH, t0, TG), wg[:], reads=["k:wg"],
               is_out=True)
    return em


def build_phase_d1(TG=256, Tt=T, em=None, io=None, B=None):
    if em is None:
        em = Em("D1")
        io = {"XNB": em.din("XNB", [D, Tt], BF16),
              "W1": em.din("W1", [D, 4096], F32),
              "B1": em.din("B1", [128, 32], F32),
              "HF": em.dout("HF", [4096, Tt], BF16)}
    NG = Tt // TG
    XNBD, W1D, B1D, HFD = io["XNB"], io["W1"], io["B1"], io["HF"]
    w1 = em.salloc("w1", [128, NCH * 4096], BF16)
    b1 = em.sb("b1", [128, 32], F32)
    xnbg = em.salloc("xnbg", [128, NCH * TG], BF16)
    hbuf = em.salloc("hbuf", [128, TG], BF16)
    psH = [em.ps(f"psH{i}", [128, 512]) for i in range(4)]
    for c in range(NCH):
        em.dma("g", w1[:, 4096 * c:4096 * (c + 1)],
               W1D[128 * c:128 * (c + 1), :], writes=[f"k:w1c{c}"])
    em.dma("y", b1[:], B1D, writes=["k:par"])
    for g in range(NG):
        t0 = g * TG
        em.dma("y", xnbg[:], fm_ap(XNBD, 0, NCH, t0, TG), writes=["k:xnbg"])
        for f in range(32):
            pw = psH[f % 4]
            for c in range(NCH):
                em.op("t", lambda t, f=f, c=c, pw=pw: t.matmul(
                    pw[:, 0:TG],
                    w1[:, 4096 * c + 128 * f:4096 * c + 128 * (f + 1)],
                    xnbg[:, c * TG:(c + 1) * TG],
                    start=(c == 0), stop=(c == NCH - 1)),
                    reads=["k:xnbg", f"k:w1c{c}"],
                    writes=[f"ps:H{f % 4}"] if c in (0, NCH - 1) else [],
                    inc=1 if c == NCH - 1 else 0)
            em.op("s", lambda s, f=f, pw=pw: s.activation(
                hbuf[:, 0:TG], pw[:, 0:TG], AF.Relu,
                bias=b1[:, f:f + 1], scale=1.0),
                reads=[f"ps:H{f % 4}", "k:par"], writes=["k:hbuf"])
            em.dma("y", HFD[128 * f:128 * (f + 1), t0:t0 + TG], hbuf[:, 0:TG],
                   reads=["k:hbuf"], is_out=True)
    return em


def build_phase_d2(TG=256, Tt=T, em=None, io=None, B=None):
    if em is None:
        em = Em("D2")
        io = {"HF": em.din("HF", [4096, Tt], BF16),
              "XNF": em.din("XNF", [D, Tt], F32),
              "W2": em.din("W2", [4096, D], F32),
              "B2": em.din("B2", [128, 8], F32),
              "GAM": em.din("GAM", [128, 8], F32),
              "BET": em.din("BET", [128, 8], F32),
              "XMID": em.din("XMID", [D, Tt], F32),
              "X1R": em.dout("X1R", [D, Tt], F32)}
    NG = Tt // TG
    HFD, XNFD, W2D, B2D, GAMD, BETD, XMIDD, X1R = (io[k] for k in (
        "HF", "XNF", "W2", "B2", "GAM", "BET", "XMID", "X1R"))
    w2 = em.salloc("w2", [128, 32 * 1024], BF16)
    b2 = em.sb("b2", [128, 8], F32)
    gam = em.sb("gam", [128, 8], F32)
    bet = em.sb("bet", [128, 8], F32)
    hfg = em.salloc("hfg", [128, 32 * TG], BF16)
    xnfg = em.salloc("xnfg", [128, TG], F32)
    xmg = em.salloc("xmg", [128, TG], F32)
    yg = em.salloc("yg", [128, NCH * TG], F32)
    ygb = em.salloc("ygb", [128, NCH * TG], BF16)
    if B is None:
        B = mk_ln_bufs(em, TG, 8)
    psO2 = [em.ps(f"psO2{i}", [128, 512]) for i in range(2)]
    for f in range(32):
        em.dma("g", w2[:, 1024 * f:1024 * (f + 1)],
               W2D[128 * f:128 * (f + 1), :], writes=[f"k:w2f{f}"])
    em.dma("y", b2[:], B2D, writes=["k:parb"])
    em.dma("y", gam[:], GAMD, writes=["k:parc"])
    em.dma("y", bet[:], BETD, writes=["k:pard"])
    for g in range(NG):
        t0 = g * TG
        em.dma("y", hfg[:], fm_ap(HFD, 0, 32, t0, TG), writes=["k:hfg"])
        for dc in range(NCH):
            for f in range(32):
                em.op("t", lambda t, f=f, dc=dc: t.matmul(
                    psO2[dc % 2][:, 0:TG],
                    w2[:, 1024 * f + 128 * dc:1024 * f + 128 * (dc + 1)],
                    hfg[:, f * TG:(f + 1) * TG],
                    start=(f == 0), stop=(f == 31)),
                    reads=["k:hfg", f"k:w2f{f}"],
                    writes=[f"ps:O2{dc % 2}"] if f in (0, 31) else [],
                    inc=1 if f == 31 else 0)
            ysl = yg[:, dc * TG:(dc + 1) * TG]
            em.dma("y", xnfg[:, 0:TG], fm_ap(XNFD, dc, 1, t0, TG),
                   writes=["k:xnfg"])
            em.op("v", lambda v, ysl=ysl, dc=dc: v.scalar_tensor_tensor(
                ysl, psO2[dc % 2][:, 0:TG], b2[:, dc:dc + 1],
                xnfg[:, 0:TG], ALU.add, ALU.add),
                reads=[f"ps:O2{dc % 2}", "k:parb", "k:xnfg"], writes=["k:yg"])
            em.op("s", lambda s, ysl=ysl, dc=dc: s.copy(
                ygb[:, dc * TG:(dc + 1) * TG], ysl),
                reads=["k:yg"], writes=["k:ygb"])
        emit_ln(em, B, ygb, "k:ygb", NCH, TG, "bc8", "f")
        for dc in range(NCH):
            ysl = yg[:, dc * TG:(dc + 1) * TG]
            em.op("v", lambda v, ysl=ysl: v.tensor_tensor(
                B["t1"][:, 0:TG], ysl, B["mub"][:, 0:TG], ALU.subtract),
                reads=["k:yg", "k:mub"], writes=["k:t1"])
            em.op("v", lambda v: v.tensor_tensor(
                B["t1"][:, 0:TG], B["t1"][:, 0:TG], B["rstd"][:, 0:TG],
                ALU.mult),
                reads=["k:t1", "k:rstd"], writes=["k:t1"])
            em.op("v", lambda v, dc=dc: v.tensor_scalar(
                B["t1"][:, 0:TG], B["t1"][:, 0:TG], gam[:, dc:dc + 1],
                bet[:, dc:dc + 1], ALU.mult, ALU.add),
                reads=["k:t1", "k:parc", "k:pard"], writes=["k:t1"])
            em.dma("y", xmg[:, 0:TG], fm_ap(XMIDD, dc, 1, t0, TG),
                   writes=["k:xmg"])
            em.op("v", lambda v, ysl=ysl, dc=dc: v.tensor_tensor(
                ysl, B["t1"][:, 0:TG], xmg[:, 0:TG], ALU.add),
                reads=["k:t1", "k:xmg"], writes=["k:yg"])
        em.dma("y", fm_ap(X1R, 0, NCH, t0, TG), yg[:], reads=["k:yg"],
               is_out=True)
    return em

def build_phase_e(TG=256, Tt=T, em=None, io=None, B=None):
    if em is None:
        em = Em("E")
        io = {
            "X1RD": em.din("X1R", [D, Tt], F32),
            "WID": em.din("WI", [D, D], F32),
            "WOD": em.din("WO", [D, D], F32),
            "WVD": em.din("WV", [D, D], F32),
            "WAD": em.din("WA", [D, 16], F32),
            "WED": em.din("WE", [D, 16], F32),
            "BAD": em.din("BA", [16, 1], F32),
            "BED": em.din("BE", [16, 1], F32),
            "HEADIND": em.din("HEADIND", [128, 128], F32),
            "IDENT": em.din("IDENT", [128, 128], BF16),
            "XN1F": em.dout("XN1F", [D, Tt], F32),
            "XN1B": em.dout("XN1B", [D, Tt], BF16),
            "Q1H": em.dout("Q1H", [NH, DH, Tt], BF16),
            "K1H": em.dout("K1H", [NH, DH, Tt], BF16),
            "VSC1": em.dout("VSC1", [128, (Tt // 128) * 1056], BF16),
            "M1H": em.dout("M1H", [NH, Tt], BF16),
            "APH": em.dout("APH", [NH, Tt], F32),
        }
    NG = Tt // TG
    KTPG = TG // 128

    (X1RD, WID, WOD, WVD, WAD, WED, BAD, BED, HEADIND, IDENT, XN1F, XN1B, Q1H, K1H, VSC1, M1H, APH) = (io[k] for k in ("X1RD", "WID", "WOD", "WVD", "WAD", "WED", "BAD", "BED", "HEADIND", "IDENT", "XN1F", "XN1B", "Q1H", "K1H", "VSC1", "M1H", "APH",))
    wi = em.salloc("wi", [128, NCH * 1024], BF16)
    wo = em.salloc("wo", [128, NCH * 1024], BF16)
    wv = em.salloc("wv", [128, NCH * 1024], BF16)
    wa = em.salloc("wa", [128, NCH * 16], BF16)
    we = em.salloc("we", [128, NCH * 16], BF16)
    hind = em.sb("hind", [128, 128], BF16)
    idn = em.sb("idn", [128, 128], BF16)
    ba = em.sb("ba", [16, 1], F32)
    be = em.sb("be", [16, 1], F32)
    xg = em.salloc("xg", [128, NCH * TG], F32)
    xn1f = em.salloc("xn1f", [128, NCH * TG], F32)
    xn1b = em.salloc("xn1b", [128, NCH * TG], BF16)
    stage = em.salloc("stage", [128, TG], BF16)
    sqs = em.salloc("sqs", [128, TG], BF16)
    apg = em.salloc("apg", [16, TG], F32)
    exg = em.salloc("exg", [16, TG], BF16)
    ext = em.sb("ext", [128, KTPG * 16], F32)
    vsc1g = em.salloc("vsc1g", [128, KTPG * 1056], BF16)
    sq16 = em.salloc("sq16", [16, Tt], BF16)
    sk16 = em.salloc("sk16", [16, Tt], BF16)
    kmx = em.sb("kmx", [16, 1], F32)
    nkm = em.sb("nkm", [16, 1], F32)
    srt = em.salloc("srt", [16, Tt], BF16)
    m1r = em.salloc("m1r", [16, Tt], BF16)
    if B is None:
        B = mk_ln_bufs(em, TG, 8)
    psM = [em.ps(f"psM{i}", [128, 512]) for i in range(2)]
    psAE = em.ps("psAE", [16, 512])
    psSq = em.ps("psSq", [16, 512])
    psTe = em.ps("psTe", [128, 1024], BF16)

    for c in range(NCH):
        em.dma("g", wi[:, 1024 * c:1024 * (c + 1)],
               WID[128 * c:128 * (c + 1), :], writes=["k:wi"])
        em.dma("g", wo[:, 1024 * c:1024 * (c + 1)],
               WOD[128 * c:128 * (c + 1), :], writes=["k:wo"])
        em.dma("g", wv[:, 1024 * c:1024 * (c + 1)],
               WVD[128 * c:128 * (c + 1), :], writes=["k:wv"])
        em.dma("g", wa[:, 16 * c:16 * (c + 1)],
               WAD[128 * c:128 * (c + 1), :], writes=["k:wa"])
        em.dma("g", we[:, 16 * c:16 * (c + 1)],
               WED[128 * c:128 * (c + 1), :], writes=["k:we"])
    em.dma("g", hind[:], HEADIND, writes=["k:hind"])
    em.dma("g", idn[:], IDENT, writes=["k:idn"])
    em.dma("y", ba[:], BAD, writes=["k:par"])
    em.dma("y", be[:], BED, writes=["k:parb"])
    em.op("v", lambda v: v.memset(vsc1g[:], 0.0), writes=["k:vsc1g"])
    em.op("v", lambda v: v.memset(
        vsc1g[:, 65:1056 * KTPG:66], 1.0), writes=["k:vsc1g"])

    for g in range(NG):
        t0 = g * TG
        em.dma("y", xg[:], fm_ap(X1RD, 0, NCH, t0, TG), writes=["k:xg"])
        emit_ln(em, B, xg, "k:xg", NCH, TG, "bc8", "e",
                src_is_f32=True)
        for c in range(NCH):
            emit_ln_apply(em, B, xg, "k:xg", c, TG,
                          xn1f[:, c * TG:(c + 1) * TG], "k:xn1f",
                          out2=xn1b[:, c * TG:(c + 1) * TG], out2_key="k:xn1b")
        em.dma("y", fm_ap(XN1F, 0, NCH, t0, TG), xn1f[:], reads=["k:xn1f"],
               is_out=True)
        em.dma("y", fm_ap(XN1B, 0, NCH, t0, TG), xn1b[:], reads=["k:xn1b"],
               is_out=True)
        for w_, bb, dst, dkey, wkey in ((wa, ba, apg, "k:apg", "k:wa"),
                                        (we, be, exg, "k:exg", "k:we")):
            for c in range(NCH):
                em.op("t", lambda t, w_=w_, c=c: t.matmul(
                    psAE[0:16, 0:TG], w_[:, 16 * c:16 * (c + 1)],
                    xn1b[:, c * TG:(c + 1) * TG],
                    start=(c == 0), stop=(c == NCH - 1)),
                    reads=["k:xn1b", wkey],
                    writes=["ps:AE"] if c in (0, NCH - 1) else [],
                    inc=1 if c == NCH - 1 else 0)
            em.op("s", lambda s, bb=bb, dst=dst, dkey=dkey: s.activation(
                dst[0:16, 0:TG], psAE[0:16, 0:TG], AF.Sigmoid,
                bias=bb[0:16, 0:1], scale=1.0),
                reads=["ps:AE", "k:par", "k:parb"], writes=[dkey])
        em.dma("y", APH[0:16, t0:t0 + TG], apg[0:16, 0:TG], reads=["k:apg"],
               is_out=True)
        for tt in range(KTPG):
            em.op("t", lambda t, tt=tt: t.transpose(
                psTe[0:128, 0:16], exg[0:16, 128 * tt:128 * (tt + 1)],
                idn[0:16, 0:16]),
                reads=["k:exg", "k:idn"], writes=["ps:Te"])
            em.op("s", lambda s, tt=tt: s.copy(
                ext[:, 16 * tt:16 * (tt + 1)], psTe[0:128, 0:16]),
                reads=["ps:Te"], writes=["k:ext"])
        for which, wmat, wkey, DST, SQ16 in (
                ("q", wi, "k:wi", Q1H, sq16), ("k", wo, "k:wo", K1H, sk16)):
            for dc in range(NCH):
                pm = psM[dc % 2]
                for c in range(NCH):
                    em.op("t", lambda t, wmat=wmat, c=c, dc=dc, pm=pm:
                          t.matmul(pm[:, 0:TG],
                                   wmat[:, 1024 * c + 128 * dc:
                                        1024 * c + 128 * (dc + 1)],
                                   xn1b[:, c * TG:(c + 1) * TG],
                                   start=(c == 0), stop=(c == NCH - 1)),
                          reads=["k:xn1b", wkey],
                          writes=[f"ps:M{dc % 2}"] if c in (0, NCH - 1) else [],
                          inc=1 if c == NCH - 1 else 0)
                sc = 0.125 if which == "q" else 1.0
                em.op("s", lambda s, pm=pm, sc=sc: s.activation(
                    stage[:, 0:TG], pm[:, 0:TG], AF.Copy, bias=0.0, scale=sc),
                    reads=[f"ps:M{dc % 2}"], writes=["k:stage"])
                em.dma("y", DST[2 * dc, :, t0:t0 + TG], stage[0:64, 0:TG],
                       reads=["k:stage"], is_out=True)
                em.dma("y", DST[2 * dc + 1, :, t0:t0 + TG], stage[64:128, 0:TG],
                       reads=["k:stage"], is_out=True)
                em.op("s", lambda s: s.activation(
                    sqs[:, 0:TG], stage[:, 0:TG], AF.Square, bias=0.0,
                    scale=1.0),
                    reads=["k:stage"], writes=["k:sqs"])
                em.op("t", lambda t, dc=dc: t.matmul(
                    psSq[0:16, 0:TG], hind[:, 16 * dc:16 * (dc + 1)],
                    sqs[:, 0:TG], start=(dc == 0), stop=(dc == NCH - 1)),
                    reads=["k:sqs", "k:hind"],
                    writes=["ps:Sq"] if dc in (0, NCH - 1) else [], inc=1)
            em.op("s", lambda s, SQ16=SQ16, t0=t0: s.copy(
                SQ16[0:16, t0:t0 + TG], psSq[0:16, 0:TG]),
                reads=["ps:Sq"], writes=["k:sq16"])
        for tt in range(KTPG):
            for vcg in range(2):
                pm = psM[vcg]
                for c in range(NCH):
                    em.op("t", lambda t, c=c, tt=tt, vcg=vcg, pm=pm: t.matmul(
                        pm[:, 0:512],
                        xn1b[:, c * TG + 128 * tt:c * TG + 128 * (tt + 1)],
                        wv[:, 1024 * c + 512 * vcg:1024 * c + 512 * (vcg + 1)],
                        start=(c == 0), stop=(c == NCH - 1)),
                        reads=["k:xn1b", "k:wv"],
                        writes=[f"ps:M{vcg}"] if c in (0, NCH - 1) else [],
                        inc=1 if c == NCH - 1 else 0)
                for j in range(8):
                    h = 8 * vcg + j
                    em.op("v", lambda v, tt=tt, h=h, j=j, pm=pm: v.tensor_scalar(
                        vsc1g[:, 1056 * tt + 66 * h:1056 * tt + 66 * h + 64],
                        pm[:, 64 * j:64 * (j + 1)],
                        ext[:, 16 * tt + h:16 * tt + h + 1], None, ALU.mult),
                        reads=[f"ps:M{vcg}", "k:ext"], writes=["k:vsc1g"])
                    em.op("s", lambda s, tt=tt, h=h: s.copy(
                        vsc1g[:, 1056 * tt + 66 * h + 64:
                              1056 * tt + 66 * h + 65],
                        ext[:, 16 * tt + h:16 * tt + h + 1]),
                        reads=["k:ext"], writes=["k:vsc1g"])
        em.dma("y", VSC1[:, 1056 * KTPG * g:1056 * KTPG * (g + 1)], vsc1g[:],
               reads=["k:vsc1g"], is_out=True)
    em.op("v", lambda v: v.reduce_max(kmx[:], sk16[0:16, :], axis=AX.X),
          reads=["k:sq16"], writes=["k:kmx"])
    em.op("s", lambda s: s.activation(kmx[:], kmx[:], AF.Sqrt, bias=0.0,
                                      scale=1.0),
          reads=["k:kmx"], writes=["k:kmx"])
    em.op("v", lambda v: v.tensor_scalar(nkm[:], kmx[:], -1.0, None, ALU.mult),
          reads=["k:kmx"], writes=["k:nkm"])
    em.op("s", lambda s: s.activation(srt[0:16, :], sq16[0:16, :], AF.Sqrt,
                                      bias=0.0, scale=1.0),
          reads=["k:sq16"], writes=["k:srt"])
    em.op("v", lambda v: v.tensor_scalar(m1r[0:16, :], srt[0:16, :], nkm[:],
                                         CSH, ALU.mult, ALU.add),
          reads=["k:srt", "k:nkm"], writes=["k:m1r"])
    em.dma("y", M1H[0:16, :], m1r[0:16, :], reads=["k:m1r"], is_out=True)
    return em


def build_phase_f(Tt=T, em=None, io=None, B=None):
    if em is None:
        em = Em("F")
        io = {
            "Q1HD": em.din("Q1H", [NH, DH, Tt], BF16),
            "K1HD": em.din("K1H", [NH, DH, Tt], BF16),
            "VSC1D": em.din("VSC1", [128, NKT * 1056], BF16),
            "M1HD": em.din("M1H", [NH, Tt], BF16),
            "APHD": em.din("APH", [NH, Tt], F32),
            "O1H": em.dout("O1H", [NH, DH, Tt], BF16),
        }
    NQG = Tt // 512
    NKT = Tt // 128

    (Q1HD, K1HD, VSC1D, M1HD, APHD, O1H) = (io[k] for k in ("Q1HD", "K1HD", "VSC1D", "M1HD", "APHD", "O1H",))
    vsc = em.salloc("vsc", [128, NKT * 1056], BF16)
    kb = [em.salloc(f"kb{i}", [65, Tt], BF16) for i in range(2)]
    qb = [em.salloc(f"qb{i}", [65, Tt], BF16) for i in range(2)]
    wb = em.salloc("wb", [128, 512], F32)
    ap0 = em.salloc("ap0", [1, 512], F32)
    onesc = em.sb("onesc", [128, 1], BF16)
    onesr = em.sb("onesr", [1, 64], BF16)
    abuf = em.salloc("abuf", [128, 4 * 512], BF16)
    szb = em.salloc("szb", [128, 512], BF16)
    rr = em.salloc("rr", [1, 512], F32)
    frow = em.salloc("frow", [1, 512], BF16)
    fbc = em.salloc("fbc", [64, 512], BF16)
    obuf = em.salloc("obuf", [64, 512], BF16)
    psS = [em.ps(f"psS{i}", [128, 512]) for i in range(3)]
    psO = [em.ps(f"psO{i}", [128, 512]) for i in range(2)]
    psB = em.ps("psB", [128, 512])
    psD = psB

    em.dma("y", vsc[:], VSC1D, writes=["k:vsc"])
    em.op("v", lambda v: v.memset(onesc[:], 1.0), writes=["k:ones"])
    em.op("v", lambda v: v.memset(onesr[:], 1.0), writes=["k:onesr"])
    em.op("v", lambda v: v.memset(wb[64:66, :], 1e-8), writes=["k:wb"])
    for i in range(2):
        em.op("v", lambda v, i=i: v.memset(kb[i][64:65, :], 1.0),
              writes=[f"k:kb{i}"])

    for h in range(NH):
        par = h % 2
        kbu, qbu = kb[par], qb[par]
        kkey, qkey = f"k:kb{par}", f"k:qb{par}"
        em.dma("y", kbu[0:64, :], K1HD[h], writes=[kkey])
        em.dma("y", qbu[0:64, :], Q1HD[h], writes=[qkey])
        em.dma("y", qbu[64:65, :], M1HD[h:h + 1, :], writes=[qkey])

        for qg in range(NQG):
            LAG = 2
            pso = psO[qg % 2]
            pokey = f"ps:O{qg % 2}"

            def emit_o(kt, h=h, pso=pso, pokey=pokey):
                asl = abuf[:, 512 * (kt % 3):512 * (kt % 3 + 1)]
                em.op("t", lambda t, asl=asl, kt=kt, h=h, pso=pso: t.matmul(
                    pso[0:66, 0:512],
                    vsc[:, 1056 * kt + 66 * h:1056 * kt + 66 * h + 66],
                    asl, start=(kt == 0), stop=(kt == NKT - 1)),
                    reads=[f"k:ab{kt % 3}", "k:vsc"],
                    writes=[pokey] if kt in (0, NKT - 1) else [],
                    inc=1 if kt == NKT - 1 else 0)

            for kt in range(NKT):
                ss = psS[kt % 3]
                em.op("t", lambda t, ss=ss, kt=kt, qg=qg, kbu=kbu, qbu=qbu:
                      t.matmul(ss[:, 0:512],
                               kbu[0:65, 128 * kt:128 * (kt + 1)],
                               qbu[0:65, 512 * qg:512 * (qg + 1)],
                               start=True, stop=True),
                      reads=[kkey, qkey], writes=[f"ps:S{kt % 3}"])
                asl = abuf[:, 512 * (kt % 3):512 * (kt % 3 + 1)]
                em.op("s", lambda s, ss=ss, asl=asl: s.activation(
                    asl, ss[:, 0:512], AF.Exp, bias=0.0, scale=1.0),
                    reads=[f"ps:S{kt % 3}"], writes=[f"k:ab{kt % 3}"])
                if kt >= LAG:
                    emit_o(kt - LAG)
            for kt in range(NKT - LAG, NKT):
                emit_o(kt)
            em.dma("s", wb[64:65, 0:512],
                   APHD[h:h + 1, 512 * qg:512 * (qg + 1)], writes=["k:wb"])
            em.dma("s", ap0[0:1, 0:512],
                   APHD[h:h + 1, 512 * qg:512 * (qg + 1)], writes=["k:ap0"])
            em.op("s", lambda s, pso=pso: s.copy(
                szb[64:66, 0:512], pso[64:66, 0:512]),
                  reads=[pokey], writes=["k:szb"])
            em.op("v", lambda v, qg=qg: v.tensor_tensor(
                szb[64:66, 0:512], szb[64:66, 0:512],
                wb[64:66, 0:512], ALU.mult),
                reads=["k:szb", "k:wb"], writes=["k:szb"])
            em.op("t", lambda t: t.matmul(psB[0:1, 0:512], onesc[64:66, 0:1],
                                          szb[64:66, 0:512],
                                          start=True, stop=True),
                  reads=["k:szb", "k:ones"], writes=["ps:B"])
            em.op("v", lambda v: v.reciprocal(rr[0:1, :], psB[0:1, 0:512]),
                  reads=["ps:B"], writes=["k:rr"])
            em.op("v", lambda v, qg=qg: v.tensor_tensor(
                rr[0:1, :], rr[0:1, :], ap0[0:1, 0:512], ALU.mult),
                reads=["k:rr", "k:ap0"], writes=["k:rr"])
            em.op("s", lambda s: s.copy(frow[0:1, :], rr[0:1, :]),
                  reads=["k:rr"], writes=["k:frow"])
            em.op("t", lambda t: t.matmul(psB[0:64, 0:512], onesr[0:1, 0:64],
                                          frow[0:1, 0:512],
                                          start=True, stop=True),
                  reads=["k:frow", "k:onesr"], writes=["ps:B"])
            em.op("s", lambda s: s.copy(fbc[0:64, :], psB[0:64, 0:512]),
                  reads=["ps:B"], writes=["k:fbc"])
            em.op("v", lambda v, pso=pso: v.tensor_tensor(
                obuf[:], pso[0:64, 0:512], fbc[:], ALU.mult),
                  reads=[pokey, "k:fbc"], writes=["k:obuf"])
            em.dma("y", O1H[h, :, 512 * qg:512 * (qg + 1)], obuf[:],
                   reads=["k:obuf"], is_out=True)
    return em


def build_phase_g(TG=256, Tt=T, em=None, io=None, B=None):
    if em is None:
        em = Em("G")
        io = {
            "O1HD": em.din("O1H", [NH, DH, Tt], BF16),
            "WOUT1D": em.din("WOUT1", [64, 16 * 1024], F32),
            "BETA_P": em.din("BETA_P", [1, 16], F32),
            "IV_P": em.din("IV_P", [1, 16], F32),
            "OV_P": em.din("OV_P", [1, 16], F32),
            "CHI": em.din("CHI", [1, 16], F32),
            "SIGN64": em.din("SIGN64", [64, 1], F32),
            "X1RD": em.din("X1R", [D, Tt], F32),
            "XMID1": em.dout("XMID1", [D, Tt], F32),
        }
    NG = Tt // TG

    (O1HD, WOUT1D, BETA_P, IV_P, OV_P, CHI, SIGN64, X1RD, XMID1) = (io[k] for k in ("O1HD", "WOUT1D", "BETA_P", "IV_P", "OV_P", "CHI", "SIGN64", "X1RD", "XMID1",))
    wo1 = em.salloc("wo1", [64, 1024], F32)
    w1o = em.salloc("w1o", [64, 16 * 1024], BF16)
    sgn = em.sb("sgn", [64, 1], F32)
    rows = em.sb("rows", [1, 16 * 6], F32)
    onesf = em.sb("onesf", [1, 128], F32)
    acol = em.sb("acol", [64, 16], F32)
    bcol = em.sb("bcol", [64, 16], F32)
    rt = em.salloc("rt", [64, 1024], F32)
    t1g = em.salloc("t1g", [64, 1024], F32)
    ohg = em.salloc("ohg", [64, 16 * TG], BF16)
    xg = em.salloc("xg", [128, TG], F32)
    og = em.salloc("og", [128, NCH * TG], F32)
    psA = [em.ps(f"psA{i}", [128, 512]) for i in range(2)]
    psBc = em.ps("psBc", [128, 512])

    em.dma("y", sgn[:], SIGN64, writes=["k:sgn"])
    em.dma("y", rows[0:1, 0:16], BETA_P, writes=["k:rows"])
    em.dma("y", rows[0:1, 16:32], IV_P, writes=["k:rows2"])
    em.dma("y", rows[0:1, 32:48], OV_P, writes=["k:rows3"])
    em.dma("y", rows[0:1, 48:64], CHI, writes=["k:rows4"])
    em.op("v", lambda v: v.memset(onesf[:], 1.0), writes=["k:onesf"])
    RK = ["k:rows", "k:rows2", "k:rows3", "k:rows4"]
    em.op("s", lambda s: s.activation(rows[0:1, 0:48], rows[0:1, 0:48],
                                      AF.Sigmoid, bias=0.0, scale=1.0),
          reads=RK, writes=RK)
    em.op("s", lambda s: s.activation(rows[0:1, 48:64], rows[0:1, 48:64],
                                      AF.Tanh, bias=0.0, scale=1.0),
          reads=RK, writes=RK)
    import math
    em.op("v", lambda v: v.tensor_scalar(rows[0:1, 64:80], rows[0:1, 0:16],
                                         -math.pi, math.pi / 2.0,
                                         ALU.mult, ALU.add),
          reads=RK, writes=["k:rows5"])
    em.op("s", lambda s: s.activation(rows[0:1, 64:80], rows[0:1, 64:80],
                                      AF.Sin, bias=0.0, scale=1.0),
          reads=["k:rows5"], writes=["k:rows5"])
    em.op("v", lambda v: v.tensor_scalar(rows[0:1, 80:96], rows[0:1, 0:16],
                                         math.pi, None, ALU.mult),
          reads=RK, writes=["k:rows6"])
    em.op("s", lambda s: s.activation(rows[0:1, 80:96], rows[0:1, 80:96],
                                      AF.Sin, bias=0.0, scale=1.0),
          reads=["k:rows6"], writes=["k:rows6"])
    for dst, key in ((64, "k:rows5"), (80, "k:rows6")):
        for off in (16, 32, 48):
            em.op("v", lambda v, dst=dst, off=off: v.tensor_tensor(
                rows[0:1, dst:dst + 16], rows[0:1, dst:dst + 16],
                rows[0:1, off:off + 16], ALU.mult),
                reads=RK + [key], writes=[key])
    em.op("t", lambda t: t.matmul(psBc[0:64, 0:16], onesf[0:1, 0:64],
                                  rows[0:1, 64:80], start=True, stop=True),
          reads=["k:rows5", "k:onesf"], writes=["ps:Bc"])
    em.op("s", lambda s: s.copy(acol[:], psBc[0:64, 0:16]),
          reads=["ps:Bc"], writes=["k:acol"])
    em.op("t", lambda t: t.matmul(psBc[0:64, 0:16], onesf[0:1, 0:64],
                                  rows[0:1, 80:96], start=True, stop=True),
          reads=["k:rows6", "k:onesf"], writes=["ps:Bc"])
    em.op("s", lambda s: s.copy(bcol[:], psBc[0:64, 0:16]),
          reads=["ps:Bc"], writes=["k:bcol"])
    em.op("v", lambda v: v.tensor_scalar(bcol[:], bcol[:], sgn[:], None,
                                         ALU.mult),
          reads=["k:bcol", "k:sgn"], writes=["k:bcol"])
    for h in range(NH):
        em.dma("y", wo1[:], WOUT1D[:, 1024 * h:1024 * (h + 1)],
               writes=["k:wo1"])
        em.op("s", lambda s: s.copy(rt[0:32, 0:1024], wo1[32:64, 0:1024]),
              reads=["k:wo1"], writes=["k:rt"])
        em.op("s", lambda s: s.copy(rt[32:64, 0:1024], wo1[0:32, 0:1024]),
              reads=["k:wo1"], writes=["k:rt"])
        em.op("v", lambda v, h=h: v.tensor_scalar(
            t1g[:, 0:1024], wo1[0:64, 0:1024], acol[:, h:h + 1], None,
            ALU.mult),
            reads=["k:wo1", "k:acol"], writes=["k:t1g"])
        em.op("v", lambda v, h=h: v.scalar_tensor_tensor(
            w1o[0:64, 1024 * h:1024 * (h + 1)], rt[0:64, 0:1024],
            bcol[:, h:h + 1], t1g[:, 0:1024], ALU.mult, ALU.add),
            reads=["k:rt", "k:bcol", "k:t1g"], writes=["k:w1o2"])
    for g in range(NG):
        t0 = g * TG
        for h in range(NH):
            em.dma("y", ohg[:, TG * h:TG * (h + 1)], O1HD[h, :, t0:t0 + TG],
                   writes=["k:ohg"])
        for dc in range(NCH):
            pa = psA[dc % 2]
            for h in range(NH):
                em.op("t", lambda t, h=h, dc=dc, pa=pa: t.matmul(
                    pa[0:128, 0:TG],
                    w1o[0:64, 1024 * h + 128 * dc:1024 * h + 128 * (dc + 1)],
                    ohg[0:64, TG * h:TG * (h + 1)],
                    start=(h == 0), stop=(h == NH - 1)),
                    reads=["k:ohg", "k:w1o2"],
                    writes=[f"ps:A{dc % 2}"] if h in (0, NH - 1) else [],
                    inc=1 if h == NH - 1 else 0)
            em.dma("y", xg[:, 0:TG], fm_ap(X1RD, dc, 1, t0, TG),
                   writes=["k:xg"])
            em.op("v", lambda v, dc=dc, pa=pa: v.tensor_tensor(
                og[:, dc * TG:(dc + 1) * TG], pa[0:128, 0:TG],
                xg[:, 0:TG], ALU.add),
                reads=[f"ps:A{dc % 2}", "k:xg"], writes=["k:og"])
        em.dma("y", fm_ap(XMID1, 0, NCH, t0, TG), og[:], reads=["k:og"],
               is_out=True)
    return em


def build_phase_h1a(TG=256, Tt=T, em=None, io=None, B=None):
    if em is None:
        em = Em("H1a")
        io = {"XN1B": em.din("XN1B", [D, Tt], BF16),
              "FW1T": em.din("FW1T", [D, 4096], F32),
              "FW1S": em.din("FW1S", [128, 32], F32),
              "G1S": em.dout("G1S", [4096, Tt], BF16)}
    NG = Tt // TG
    XN1BD, FW1T, FW1S, G1S = io["XN1B"], io["FW1T"], io["FW1S"], io["G1S"]
    w1 = em.salloc("w1", [128, NCH * 4096], BF16)
    s1 = em.sb("s1", [128, 32], F32)
    xnbg = em.salloc("xnbg", [128, NCH * TG], BF16)
    g1 = em.salloc("g1", [128, TG], BF16)
    psH = [em.ps(f"psH{i}", [128, 512]) for i in range(4)]
    for c in range(NCH):
        em.dma("g", w1[:, 4096 * c:4096 * (c + 1)],
               FW1T[128 * c:128 * (c + 1), :], writes=[f"k:w1c{c}"])
    em.dma("y", s1[:], FW1S, writes=["k:par"])
    for g in range(NG):
        t0 = g * TG
        em.dma("y", xnbg[:], fm_ap(XN1BD, 0, NCH, t0, TG), writes=["k:xnbg"])
        for f in range(32):
            pw = psH[f % 4]
            for c in range(NCH):
                em.op("t", lambda t, c=c, f=f, pw=pw: t.matmul(
                    pw[:, 0:TG],
                    w1[:, 4096 * c + 128 * f:4096 * c + 128 * (f + 1)],
                    xnbg[:, c * TG:(c + 1) * TG],
                    start=(c == 0), stop=(c == NCH - 1)),
                    reads=["k:xnbg", f"k:w1c{c}"],
                    writes=[f"ps:H{f % 4}"] if c in (0, NCH - 1) else [],
                    inc=1 if c == NCH - 1 else 0)
            em.op("s", lambda s, f=f, pw=pw: s.activation(
                g1[:, 0:TG], pw[:, 0:TG], AF.Silu, bias=0.0,
                scale=s1[:, f:f + 1]),
                reads=[f"ps:H{f % 4}", "k:par"], writes=["k:g1"])
            em.dma("y", G1S[128 * f:128 * (f + 1), t0:t0 + TG], g1[:, 0:TG],
                   reads=["k:g1"], is_out=True)
    return em


def build_phase_h1b(TG=256, Tt=T, em=None, io=None, B=None):
    if em is None:
        em = Em("H1b")
        io = {"XN1B": em.din("XN1B", [D, Tt], BF16),
              "G1S": em.din("G1S", [4096, Tt], BF16),
              "FW2T": em.din("FW2T", [D, 4096], F32),
              "FW2S": em.din("FW2S", [128, 32], F32),
              "H1S": em.dout("H1S", [4096, Tt], BF16)}
    NG = Tt // TG
    XN1BD, G1SD, FW2T, FW2S, H1S = (io[k] for k in (
        "XN1B", "G1S", "FW2T", "FW2S", "H1S"))
    w2 = em.salloc("w2", [128, NCH * 4096], BF16)
    s2 = em.sb("s2", [128, 32], F32)
    xnbg = em.salloc("xnbg", [128, NCH * TG], BF16)
    g1g = em.salloc("g1g", [128, 32 * TG], BF16)
    g2 = em.salloc("g2", [128, TG], F32)
    hbuf = em.salloc("hbuf", [128, TG], BF16)
    psH = [em.ps(f"psH{i}", [128, 512]) for i in range(2)]
    for c in range(NCH):
        em.dma("g", w2[:, 4096 * c:4096 * (c + 1)],
               FW2T[128 * c:128 * (c + 1), :], writes=[f"k:w2c{c}"])
    em.dma("y", s2[:], FW2S, writes=["k:par"])
    for g in range(NG):
        t0 = g * TG
        em.dma("y", xnbg[:], fm_ap(XN1BD, 0, NCH, t0, TG), writes=["k:xnbg"])
        em.dma("y", g1g[:], fm_ap(G1SD, 0, 32, t0, TG), writes=["k:g1g"])
        for f in range(32):
            pw = psH[f % 2]
            for c in range(NCH):
                em.op("t", lambda t, c=c, f=f, pw=pw: t.matmul(
                    pw[:, 0:TG],
                    w2[:, 4096 * c + 128 * f:4096 * c + 128 * (f + 1)],
                    xnbg[:, c * TG:(c + 1) * TG],
                    start=(c == 0), stop=(c == NCH - 1)),
                    reads=["k:xnbg", f"k:w2c{c}"],
                    writes=[f"ps:H{f % 2}"] if c in (0, NCH - 1) else [],
                    inc=1 if c == NCH - 1 else 0)
            em.op("v", lambda v, f=f, pw=pw: v.tensor_scalar(
                g2[:, 0:TG], pw[:, 0:TG], s2[:, f:f + 1], None, ALU.mult),
                reads=[f"ps:H{f % 2}", "k:par"], writes=["k:g2"])
            em.op("v", lambda v, f=f: v.tensor_tensor(
                hbuf[:, 0:TG], g1g[:, f * TG:(f + 1) * TG], g2[:, 0:TG],
                ALU.mult),
                reads=["k:g1g", "k:g2"], writes=["k:hbuf"])
            em.dma("y", H1S[128 * f:128 * (f + 1), t0:t0 + TG], hbuf[:, 0:TG],
                   reads=["k:hbuf"], is_out=True)
    return em

def build_phase_h2(name, TG=256, Tt=T, em=None, io=None, B=None):
    _mk = em is None
    if _mk:
        em = Em(name)
    NG = Tt // TG
    if _mk:
        io = {"HIN": em.din("HIN", [4096, Tt], BF16),
              "WT": em.din("WT", [4096, 4096], F32),
              "SC": em.din("SC", [128, 32], F32),
              "HOUT": em.dout("HOUT", [4096, Tt], BF16)}
    HIN, WT, SC, HOUT = io["HIN"], io["WT"], io["SC"], io["HOUT"]
    wblk = [em.salloc(f"wblk{i}", [128, 32 * 512], BF16) for i in range(2)]
    sc = em.sb("sc", [128, 32], F32)
    hing = [em.salloc(f"hing{i}", [128, 32 * TG], BF16) for i in range(2)]
    hbuf = em.salloc("hbuf", [128, TG], BF16)
    psH = [em.ps(f"psH{i}", [128, 512]) for i in range(4)]
    em.dma("y", sc[:], SC, writes=["k:par"])
    for ob in range(8):
        wb_ = wblk[ob % 2]
        wkey = f"k:wb{ob % 2}"
        for c in range(32):
            em.dma("g", wb_[:, 512 * c:512 * (c + 1)],
                   WT[128 * c:128 * (c + 1), 512 * ob:512 * (ob + 1)],
                   writes=[wkey])
        for g in range(NG):
            t0 = g * TG
            hg = hing[g % 2]
            hkey = f"k:hg{g % 2}"
            em.dma("y", hg[:], fm_ap(HIN, 0, 32, t0, TG), writes=[hkey])
            for sub in range(4):
                pp = psH[sub % 4]
                for c in range(32):
                    em.op("t", lambda t, c=c, sub=sub, wb_=wb_, hg=hg, pp=pp:
                          t.matmul(pp[:, 0:TG],
                                   wb_[:, 512 * c + 128 * sub:
                                       512 * c + 128 * (sub + 1)],
                                   hg[:, c * TG:(c + 1) * TG],
                                   start=(c == 0), stop=(c == 31)),
                          reads=[hkey, wkey],
                          writes=[f"ps:H{sub % 4}"] if c in (0, 31) else [],
                          inc=1 if c == 31 else 0)
                fo = 4 * ob + sub
                em.op("s", lambda s, pp=pp, fo=fo, sub=sub: s.activation(
                    hbuf[:, 0:TG], pp[:, 0:TG], AF.Gelu, bias=0.0,
                    scale=sc[:, fo:fo + 1]),
                    reads=[f"ps:H{sub % 4}", "k:par"], writes=["k:hbuf"])
                em.dma("y", HOUT[512 * ob + 128 * sub:512 * ob + 128 * (sub + 1),
                                 t0:t0 + TG], hbuf[:, 0:TG],
                       reads=["k:hbuf"], is_out=True)
    return em


def build_phase_h4(TG=256, Tt=T, em=None, io=None, B=None):
    if em is None:
        em = Em("H4")
        io = {
            "H3S": em.din("H3S", [4096, Tt], BF16),
            "FW3T": em.din("FW3T", [4096, D], F32),
            "FW3S": em.din("FW3S", [128, 8], F32),
            "XMID1D": em.din("XMID1", [D, Tt], F32),
            "YF": em.dout("YF", [D, Tt], F32),
        }
    NG = Tt // TG
    (H3S, FW3T, FW3S, XMID1D, YF) = (io[k] for k in ("H3S", "FW3T", "FW3S", "XMID1D", "YF",))
    w3 = em.salloc("w3", [128, 32 * 1024], BF16)
    s3 = em.sb("s3", [128, 8], F32)
    hg = em.salloc("hg", [128, 32 * TG], BF16)
    xmg = em.salloc("xmg", [128, TG], F32)
    og = em.salloc("og", [128, NCH * TG], F32)
    psH = [em.ps(f"psH{i}", [128, 512]) for i in range(4)]
    for f in range(32):
        em.dma("g", w3[:, 1024 * f:1024 * (f + 1)],
               FW3T[128 * f:128 * (f + 1), :], writes=[f"k:w3f{f}"])
    em.dma("y", s3[:], FW3S, writes=["k:par"])
    for g in range(NG):
        t0 = g * TG
        em.dma("y", hg[:], fm_ap(H3S, 0, 32, t0, TG), writes=["k:hg"])
        for dc in range(NCH):
            pp = psH[dc % 4]
            for f in range(32):
                em.op("t", lambda t, f=f, dc=dc, pp=pp: t.matmul(
                    pp[:, 0:TG],
                    w3[:, 1024 * f + 128 * dc:1024 * f + 128 * (dc + 1)],
                    hg[:, f * TG:(f + 1) * TG],
                    start=(f == 0), stop=(f == 31)),
                    reads=["k:hg", f"k:w3f{f}"],
                    writes=[f"ps:H{dc % 4}"] if f in (0, 31) else [],
                    inc=1 if f == 31 else 0)
            em.dma("y", xmg[:, 0:TG], fm_ap(XMID1D, dc, 1, t0, TG),
                   writes=["k:xmg"])
            em.op("v", lambda v, dc=dc, pp=pp: v.scalar_tensor_tensor(
                og[:, dc * TG:(dc + 1) * TG], pp[:, 0:TG], s3[:, dc:dc + 1],
                xmg[:, 0:TG], ALU.mult, ALU.add),
                reads=[f"ps:H{dc % 4}", "k:par", "k:xmg"], writes=["k:og"])
        em.dma("y", fm_ap(YF, 0, NCH, t0, TG), og[:], reads=["k:og"],
               is_out=True)
    return em

SHARED_INPUTS = [
    ("WINT", [D, QKVP], F32), ("LNG_IN", [128, 8], F32),
    ("LNB_IN", [128, 8], F32), ("MIDG", [128, 56], F32),
    ("MIDB", [128, 56], F32), ("SHIFT1", [128, 128], BF16),
    ("OH127", [128, 128], BF16), ("IDENT", [128, 128], BF16),
    ("SLOPES", [1, 16], F32), ("IOTA128", [128, 1], F32),
    ("IOTAT", [1, T], F32), ("PREMASK", [512, 512], BF16),
    ("PREMASKT", [512, 512], BF16), ("WOT_O", [64, 16 * 1024], F32),
    ("WOT_PG", [4096, 1024], F32), ("OUTG", [128, 8], F32),
    ("OUTB", [128, 8], F32), ("W1", [D, 4096], F32), ("B1", [128, 32], F32),
    ("W2", [4096, D], F32), ("B2", [128, 8], F32), ("GAM", [128, 8], F32),
    ("BET", [128, 8], F32), ("WI", [D, D], F32), ("WO", [D, D], F32),
    ("WV", [D, D], F32), ("WA", [D, 16], F32), ("WE", [D, 16], F32),
    ("BA", [16, 1], F32), ("BE", [16, 1], F32), ("HEADIND", [128, 128], F32),
    ("WOUT1", [64, 16 * 1024], F32), ("BETA_P", [1, 16], F32),
    ("IV_P", [1, 16], F32), ("OV_P", [1, 16], F32), ("CHI", [1, 16], F32),
    ("SIGN64", [64, 1], F32), ("FW1T", [D, 4096], F32),
    ("FW1S", [128, 32], F32), ("FW2T", [D, 4096], F32),
    ("FW2S", [128, 32], F32), ("FH1T", [4096, 4096], F32),
    ("FH1S", [128, 32], F32), ("FH2T", [4096, 4096], F32),
    ("FH2S", [128, 32], F32), ("FW3T", [4096, D], F32),
    ("FW3S", [128, 8], F32),
]

SCRATCH = [
    ("XNF", [D, T], F32), ("XNB", [D, T], BF16),
    ("QH", [NH, DH, T], BF16), ("KH", [NH, DH, T], BF16),
    ("VSC", [128, (T // 128) * 1056], BF16), ("PG", [4096, T], BF16),
    ("MH", [NH, T], BF16), ("OH", [NH, DH, T], BF16),
    ("XMID", [D, T], F32), ("HF", [4096, T], BF16), ("X1R", [D, T], F32),
    ("XN1F", [D, T], F32), ("XN1B", [D, T], BF16),
    ("Q1H", [NH, DH, T], BF16), ("K1H", [NH, DH, T], BF16),
    ("VSC1", [128, (T // 128) * 1056], BF16), ("M1H", [NH, T], BF16),
    ("APH", [NH, T], F32), ("O1H", [NH, DH, T], BF16),
    ("XMID1", [D, T], F32), ("G1S", [4096, T], BF16),
    ("H1S", [4096, T], BF16), ("H2S", [4096, T], BF16),
    ("H3S", [4096, T], BF16),
]


def build_fused():
    em = Em("FUSED")
    em.mk_arenas()
    io = {"XF": em.din("XF", [D, T], F32)}
    for nm, shp, dt in SHARED_INPUTS:
        io[nm] = em.din(nm, shp, dt)
    for nm, shp, dt in SCRATCH:
        io[nm] = em.nc.dram_tensor(nm, shp, dt, kind="Internal").ap()
    io["YF"] = em.dout("YF", [D, T], F32)
    Bsh = mk_ln_bufs(em, 512, 56)

    def ph(pfx, base, fn, **kw):
        em.arena_reset(pfx, psum_base=base)
        fn(em=em, io=dict(io, **kw.pop("io_map", {})), **kw)
        em.barrier()

    ph("A:", 3, build_phase_a, B=Bsh)
    ph("B:", 0, build_phase_b,
       io_map=dict(QHD=io["QH"], KHD=io["KH"], VSCD=io["VSC"]))
    ph("C:", 3, build_phase_c, TG=512, B=Bsh,
       io_map=dict(OHD=io["OH"], PGD=io["PG"], XFD=io["XF"]))
    ph("D1:", 0, build_phase_d1, TG=512)
    ph("D2:", 3, build_phase_d2, TG=512, B=Bsh)
    ph("E:", 3, build_phase_e, B=Bsh,
       io_map=dict(X1RD=io["X1R"], WID=io["WI"], WOD=io["WO"], WVD=io["WV"],
                   WAD=io["WA"], WED=io["WE"], BAD=io["BA"], BED=io["BE"]))
    ph("F:", 0, build_phase_f,
       io_map=dict(Q1HD=io["Q1H"], K1HD=io["K1H"], VSC1D=io["VSC1"],
                   M1HD=io["M1H"], APHD=io["APH"]))
    ph("G:", 0, build_phase_g,
       io_map=dict(O1HD=io["O1H"], WOUT1D=io["WOUT1"], X1RD=io["X1R"]))
    ph("H1a:", 0, build_phase_h1a, TG=512)
    ph("H1b:", 0, build_phase_h1b, TG=512)
    em.arena_reset("H2:", 0)
    build_phase_h2("H2x", TG=512, em=em, io=dict(HIN=io["H1S"], WT=io["FH1T"],
                                                 SC=io["FH1S"], HOUT=io["H2S"]))
    em.barrier()
    em.arena_reset("H3:", 0)
    build_phase_h2("H3x", TG=512, em=em, io=dict(HIN=io["H2S"], WT=io["FH2T"],
                                                 SC=io["FH2S"], HOUT=io["H3S"]))
    em.barrier()
    em.arena_reset("H4:", 0)
    build_phase_h4(TG=512, em=em, io=dict(io, XMID1D=io["XMID1"]))
    return em


def kernel(**inputs):
    H = host_prep(inputs)
    NC = 8
    em = build_fused()
    nc = em.finalize()
    shared = {nm: H[nm] for nm, _, _ in SHARED_INPUTS}
    maps = [dict(shared, XF=H["XF"][b]) for b in range(NC)]
    res = run_bass_kernel_spmd(nc, maps, core_ids=list(range(NC)))
    return np.stack([np.ascontiguousarray(np.asarray(res.results[b]["YF"]).T)
                     for b in range(NC)])


# version 20
# speedup vs baseline: 1.0697x; 1.0697x over previous
import numpy as np
import ml_dtypes
from contextlib import ExitStack
import concourse.bass as bass
import concourse.mybir as mybir
from concourse.alu_op_type import AluOpType as ALU
from concourse.bass_utils import run_bass_kernel_spmd

F32 = mybir.dt.float32
BF16 = mybir.dt.bfloat16
AF = mybir.ActivationFunctionType
AX = mybir.AxisListType

T = 4096
D = 1024
NH, DH = 16, 64
QKVP = 7168
NCH = 8
CSH = 40.0
bf16 = ml_dtypes.bfloat16

ENGS = ("t", "s", "v", "g", "y")
DMA_QUEUES = ("y", "g", "s")
NSLOT = 8


class _ArView:

    def __init__(self, base, off, p, n):
        self.base, self.off, self.p, self.n = base, off, p, n

    def __getitem__(self, idx):
        if idx == slice(None):
            return self.base[0:self.p, self.off:self.off + self.n]
        if isinstance(idx, tuple):
            ps, cs = idx
        else:
            ps, cs = idx, slice(None)
        p0 = ps.start or 0
        p1 = self.p if ps.stop is None else ps.stop
        c0 = cs.start or 0
        c1 = self.n if cs.stop is None else cs.stop
        if cs.step in (None, 1):
            return self.base[p0:p1, self.off + c0:self.off + c1]
        return self.base[p0:p1, self.off + c0:self.off + c1:cs.step]


class Em:

    def __init__(self, name):
        self.name = name
        self.nc = bass.Bass("TRN2", target_bir_lowering=False)
        self.st = ExitStack()
        self.sems = {}
        self.cnt = {}
        self.ops = {e: [] for e in ENGS}
        self.writer = {}
        self.readers = {}
        self.pending = {}
        self.pending_keys = set()
        self.dmaslot = {q: 0 for q in DMA_QUEUES}
        self.waited = {}
        self.out_tokens = []
        self.pfx = ""
        self.arenas = {}
        self.shared_psum = False
        self.barrier_junk = None

    def sem(self, name):
        if name not in self.sems:
            self.sems[name] = self.st.enter_context(self.nc.semaphore(name))
            self.cnt[name] = 0
        return self.sems[name]

    def sb(self, name, shape, dtype):
        return self.st.enter_context(self.nc.sbuf_tensor(
            self.pfx + name, list(shape), dtype))

    def ps(self, name, shape, dtype=F32):
        if not self.shared_psum:
            return self.st.enter_context(self.nc.psum_tensor(
                self.pfx + name, list(shape), dtype))
        if dtype == BF16:
            return self.psum_banks["bf"]
        i = self.psum_next
        self.psum_next += 1
        assert i < 7, "out of shared f32 psum banks"
        return self.psum_banks[i]

    def arena_reset(self, pfx, psum_base=0):
        self.pfx = pfx
        self.arena_off = {k: 0 for k in self.arenas}
        self.psum_next = psum_base

    def mk_arenas(self):
        self.arenas = {
            "A1": self.st.enter_context(
                self.nc.sbuf_tensor("AR1", [128, 50176], BF16)),
            "A2": self.st.enter_context(
                self.nc.sbuf_tensor("AR2", [128, 29568], BF16)),
            "AF": self.st.enter_context(
                self.nc.sbuf_tensor("ARF", [128, 5632], F32)),
        }
        self.arena_cols = {"A1": 50176, "A2": 29568, "AF": 5632}
        self.psum_banks = {}
        for i in range(7):
            self.psum_banks[i] = self.st.enter_context(
                self.nc.psum_tensor(f"PSB{i}", [128, 512], F32))
        self.psum_banks["bf"] = self.st.enter_context(
            self.nc.psum_tensor("PSBbf", [128, 1024], BF16))
        self.shared_psum = True
        self.arena_off = {k: 0 for k in self.arenas}
        self.psum_next = 0

    def salloc(self, name, shape, dtype):
        if not self.arenas:
            return self.sb(name, shape, dtype)
        p, n = shape
        want = "AF" if dtype == F32 else None
        if want is None:
            a1left = self.arena_cols["A1"] - self.arena_off["A1"]
            want = "A1" if a1left >= n else "A2"
        off = self.arena_off[want]
        assert off + n <= self.arena_cols[want], \
            f"arena {want} overflow at {self.pfx}{name} (+{n} at {off})"
        self.arena_off[want] = off + n
        return _ArView(self.arenas[want], off, p, n)

    def barrier(self):
        snap = dict(self.cnt)
        for e in ("t", "s", "v", "g", "y"):
            tag = ("E" + e) if e != "y" else "Qy"
            deps = {s: c for s, c in snap.items()
                    if c > 0 and s != ("E" + e if e != "y" else None)}

            def run(h, deps=deps, tag=tag):
                for s, c in sorted(deps.items()):
                    if self.waited.get((tag, s), 0) < c:
                        h.wait_ge(self.sems[s], c)
                        self.waited[(tag, s)] = c
            self.ops[e].append(run)
        self.writer = {}
        self.readers = {}

    def din(self, name, shape, dtype):
        return self.nc.dram_tensor(name, list(shape), dtype, kind="ExternalInput").ap()

    def dout(self, name, shape, dtype):
        return self.nc.dram_tensor(name, list(shape), dtype, kind="ExternalOutput").ap()

    def _dep_sems(self, reads, writes):
        deps = {}

        def add(sc):
            if sc and deps.get(sc[0], 0) < sc[1]:
                deps[sc[0]] = sc[1]
        for k in reads:
            assert k not in self.pending_keys, f"read of pending key {k}"
            add(self.writer.get(k))
            if k.startswith("ps:"):
                for s, c in self.readers.get(k, {}).items():
                    add((s, c))
        for k in writes:
            add(self.writer.get(k))
            for s, c in self.readers.get(k, {}).items():
                add((s, c))
        return deps

    def _commit(self, eng_sem, reads, writes, inc):
        if inc == 0:
            self.pending.setdefault(eng_sem, []).append((reads, writes))
            self.pending_keys.update(writes)
            return
        c = self.cnt[eng_sem]
        pend = self.pending.pop(eng_sem, [])
        for r, w in pend + [(reads, writes)]:
            for k in r:
                self.readers.setdefault(k, {})[eng_sem] = c
            for k in w:
                self.writer[k] = (eng_sem, c)
                self.readers[k] = {}
                self.pending_keys.discard(k)

    def op(self, eng, fn, reads=(), writes=(), inc=1):
        assert eng in ("t", "s", "v", "g")
        reads, writes = [str(k) for k in reads], [str(k) for k in writes]
        deps = self._dep_sems(reads, writes)
        sem_name = "E" + eng
        self.sem(sem_name)
        self.cnt[sem_name] += inc
        my = self.cnt[sem_name]
        sems = self.sems
        waited = self.waited

        def run(h, deps=deps, fn=fn, inc=inc, sem_name=sem_name):
            for s, c in sorted(deps.items()):
                if waited.get((sem_name, s), 0) < c:
                    h.wait_ge(sems[s], c)
                    waited[(sem_name, s)] = c
            instr = fn(h)
            if inc:
                instr.then_inc(sems[sem_name], inc)
        self.ops[eng].append(run)
        self._commit(sem_name, reads, writes, inc)
        return (sem_name, my)

    def dma(self, queue, out, in_, reads=(), writes=(), is_out=False):
        assert queue in DMA_QUEUES
        reads, writes = [str(k) for k in reads], [str(k) for k in writes]
        slot = self.dmaslot[queue]
        self.dmaslot[queue] = (slot + 1) % NSLOT
        sem_name = f"D{queue}{slot}"
        self.sem(sem_name)
        deps = self._dep_sems(reads, writes)
        if self.cnt[sem_name] > 0:
            deps[sem_name] = max(deps.get(sem_name, 0), self.cnt[sem_name])
        self.cnt[sem_name] += 16
        my = self.cnt[sem_name]
        eng_tag = "Q" + queue
        sems = self.sems
        waited = self.waited

        def run(h, deps=deps, out=out, in_=in_, sem_name=sem_name, my=my):
            for s, c in sorted(deps.items()):
                if waited.get((eng_tag, s), 0) < c:
                    h.wait_ge(sems[s], c)
                    waited[(eng_tag, s)] = c
            h.dma_start(out=out, in_=in_).then_inc(sems[sem_name], 16)
        self.ops[queue].append(run)
        self._commit(sem_name, reads, writes, 16)
        tok = (sem_name, my)
        if is_out:
            self.out_tokens.append(tok)
        return tok

    def finalize(self):
        assert not self.pending_keys, f"pending at finalize: {self.pending_keys}"
        block = self.st.enter_context(self.nc.Block())
        handles = {"t": block.tensor, "s": block.scalar, "v": block.vector,
                   "g": block.gpsimd, "y": block.sync}
        toks = list(self.out_tokens)
        for e in ENGS:
            oplist = self.ops[e]
            if not oplist and e != "y":
                continue

            def section(h, oplist=oplist, e=e):
                for run in oplist:
                    run(h)
                if e == "y":
                    for s, c in toks:
                        h.wait_ge(self.sems[s], c)
            handles[e](section)
        self.st.close()
        return self.nc


def fm_ap(X, c0, nch, t0, nt):
    return X[128 * c0:128 * (c0 + nch), t0:t0 + nt].rearrange(
        "(c p) t -> p c t", p=128)


def host_prep(inputs):
    f32 = np.float32
    d = {}
    _ = inputs["L0_ffn_router_w"], inputs["L0_ffn_router_b"]
    x = np.asarray(inputs["L0_attn_x"], f32)
    d["XF"] = [np.ascontiguousarray(x[b].T) for b in range(8)]

    def pack(v, n):
        return np.ascontiguousarray(np.asarray(v, f32).reshape(n, 128).T)

    d["WINT"] = np.ascontiguousarray(np.asarray(inputs["L0_attn_W_in"], f32).T)
    d["LNG_IN"] = pack(inputs["L0_attn_in_ln_g"], 8)
    d["LNB_IN"] = pack(inputs["L0_attn_in_ln_b"], 8)
    d["MIDG"] = pack(inputs["L0_attn_mid_ln_g"], 56)
    d["MIDB"] = pack(inputs["L0_attn_mid_ln_b"], 56)
    d["SLOPES"] = np.asarray(inputs["L0_attn_slopes"], f32).reshape(1, 16)
    WoT = np.ascontiguousarray(np.asarray(inputs["L0_attn_W_out"], f32).T)
    d["WOT_O"] = np.ascontiguousarray(
        WoT[:1024].reshape(16, 64, 1024).transpose(1, 0, 2).reshape(64, 16 * 1024))
    d["WOT_PG"] = np.ascontiguousarray(WoT[1024:])
    d["OUTG"] = pack(inputs["L0_attn_out_ln_g"], 8)
    d["OUTB"] = pack(inputs["L0_attn_out_ln_b"], 8)
    d["W1"] = np.ascontiguousarray(np.asarray(inputs["L0_ffn_w1"], f32)[0])
    d["B1"] = pack(np.asarray(inputs["L0_ffn_b1"], f32)[0], 32)
    d["W2"] = np.ascontiguousarray(np.asarray(inputs["L0_ffn_w2"], f32)[0])
    d["B2"] = pack(np.asarray(inputs["L0_ffn_b2"], f32)[0], 8)
    d["GAM"] = pack(inputs["L0_ffn_gamma"], 8)
    d["BET"] = pack(inputs["L0_ffn_beta"], 8)
    d["WA"] = np.ascontiguousarray(np.asarray(inputs["L1_attn_Wa"], f32))
    d["BA"] = np.asarray(inputs["L1_attn_ba"], f32).reshape(16, 1)
    d["WE"] = np.ascontiguousarray(np.asarray(inputs["L1_attn_We"], f32))
    d["BE"] = np.asarray(inputs["L1_attn_be"], f32).reshape(16, 1)
    d["WI"] = np.ascontiguousarray(np.asarray(inputs["L1_attn_Wi"], f32))
    d["WO"] = np.ascontiguousarray(np.asarray(inputs["L1_attn_Wo"], f32))
    d["WV"] = np.ascontiguousarray(np.asarray(inputs["L1_attn_Wv"], f32))
    d["WOUT1"] = np.ascontiguousarray(
        np.asarray(inputs["L1_attn_Wout"], f32).reshape(16, 64, 1024)
        .transpose(1, 0, 2).reshape(64, 16 * 1024))
    d["CHI"] = np.asarray(inputs["L1_attn_chi"], f32).reshape(1, 16)
    d["BETA_P"] = np.asarray(inputs["L1_attn_beta_p"], f32).reshape(1, 16)
    d["IV_P"] = np.asarray(inputs["L1_attn_iv_p"], f32).reshape(1, 16)
    d["OV_P"] = np.asarray(inputs["L1_attn_ov_p"], f32).reshape(1, 16)
    d["FW1T"] = np.ascontiguousarray(np.asarray(inputs["L1_ffn_w1_w"], f32).T)
    d["FW1S"] = pack(inputs["L1_ffn_w1_s"], 32)
    d["FW2T"] = np.ascontiguousarray(np.asarray(inputs["L1_ffn_w2_w"], f32).T)
    d["FW2S"] = pack(inputs["L1_ffn_w2_s"], 32)
    d["FH1T"] = np.ascontiguousarray(np.asarray(inputs["L1_ffn_h1_w"], f32).T)
    d["FH1S"] = pack(inputs["L1_ffn_h1_s"], 32)
    d["FH2T"] = np.ascontiguousarray(np.asarray(inputs["L1_ffn_h2_w"], f32).T)
    d["FH2S"] = pack(inputs["L1_ffn_h2_s"], 32)
    d["FW3T"] = np.ascontiguousarray(np.asarray(inputs["L1_ffn_w3_w"], f32).T)
    d["FW3S"] = pack(inputs["L1_ffn_w3_s"], 8)
    d["IOTA128"] = np.arange(128, dtype=f32).reshape(128, 1)
    d["IOTAT"] = np.arange(T, dtype=f32).reshape(1, T)
    hi = np.zeros((8, 128, 16), f32)
    for dc in range(8):
        hi[dc, 0:64, 2 * dc] = 1.0
        hi[dc, 64:128, 2 * dc + 1] = 1.0
    d["HEADIND"] = np.ascontiguousarray(hi.transpose(1, 0, 2).reshape(128, 128))
    d["SIGN64"] = np.concatenate([np.ones(32, f32), -np.ones(32, f32)]).reshape(64, 1)
    masks = np.zeros((4, 128, 512), f32)
    keep = np.zeros((4, 128, 512), f32)
    pidx = np.arange(128)[:, None]
    jidx = np.arange(512)[None, :]
    for m in range(4):
        masks[m] = np.where(jidx > 128 * m + pidx, -1e10, 0.0)
        keep[m] = (128 * m + pidx <= jidx).astype(f32)
    d["PREMASK"] = np.ascontiguousarray(masks.reshape(4 * 128, 512)).astype(bf16)
    d["KEEPMASK"] = np.ascontiguousarray(keep.reshape(4 * 128, 512)).astype(bf16)
    maskt = np.zeros((4, 128, 512), f32)
    for m in range(4):
        maskt[m] = np.where(128 * m + pidx > jidx, -1e10, 0.0)
    d["PREMASKT"] = np.ascontiguousarray(maskt.reshape(4 * 128, 512)).astype(bf16)
    d["IDENT"] = np.eye(128, dtype=f32).astype(bf16)
    sh = np.zeros((128, 128), f32)
    sh[np.arange(127), np.arange(1, 128)] = 1.0
    d["SHIFT1"] = sh.astype(bf16)
    oh = np.zeros((128, 128), f32)
    oh[127, 0] = 1.0
    d["OH127"] = oh.astype(bf16)
    return d


def emit_ln(em, B, src, src_key, nch, tg, bc_key, tag, src_is_f32=False):
    oc_ = B["onescol"]
    if src_is_f32:
        for c in range(nch):
            sl = src[:, c * tg:(c + 1) * tg]
            cbb = B["cb"][c % 2]
            em.op("s", lambda s, sl=sl, cbb=cbb: s.copy(cbb[:, 0:tg], sl),
                  reads=[src_key], writes=[f"k:cb{c % 2}"])
            em.op("t", lambda t, cbb=cbb, c=c: t.matmul(
                B["psS1"][0:1, 0:tg], oc_[:], cbb[:, 0:tg],
                start=(c == 0), stop=(c == nch - 1)),
                reads=[f"k:cb{c % 2}", "k:ones"],
                writes=["ps:S1"] if c in (0, nch - 1) else [], inc=1)
            sqb = B["sq"][c % 2]
            em.op("s", lambda s, cbb=cbb, sqb=sqb: s.activation(
                sqb[:, 0:tg], cbb[:, 0:tg], AF.Square, bias=0.0, scale=1.0),
                reads=[f"k:cb{c % 2}"], writes=[f"k:sq{c % 2}"])
            em.op("t", lambda t, sqb=sqb, c=c: t.matmul(
                B["psS2"][0:1, 0:tg], oc_[:], sqb[:, 0:tg],
                start=(c == 0), stop=(c == nch - 1)),
                reads=[f"k:sq{c % 2}", "k:ones"],
                writes=["ps:S2"] if c in (0, nch - 1) else [], inc=1)
    else:
        for c in range(nch):
            sl = src[:, c * tg:(c + 1) * tg]
            em.op("t", lambda t, sl=sl, c=c: t.matmul(
                B["psS1"][0:1, 0:tg], oc_[:], sl,
                start=(c == 0), stop=(c == nch - 1)),
                reads=[src_key, "k:ones"],
                writes=["ps:S1"] if c in (0, nch - 1) else [],
                inc=1 if c == nch - 1 else 0)
        for c in range(nch):
            sl = src[:, c * tg:(c + 1) * tg]
            sqb = B["sq"][c % 2]
            em.op("s", lambda s, sl=sl, sqb=sqb: s.activation(
                sqb[:, 0:tg], sl, AF.Square, bias=0.0, scale=1.0),
                reads=[src_key], writes=[f"k:sq{c % 2}"])
            em.op("t", lambda t, sqb=sqb, c=c: t.matmul(
                B["psS2"][0:1, 0:tg], oc_[:], sqb[:, 0:tg],
                start=(c == 0), stop=(c == nch - 1)),
                reads=[f"k:sq{c % 2}", "k:ones"],
                writes=["ps:S2"] if c in (0, nch - 1) else [], inc=1)
    em.op("s", lambda s: s.copy(B["s1r"][0:1, 0:tg], B["psS1"][0:1, 0:tg]),
          reads=["ps:S1"], writes=["k:s1r"])
    em.op("s", lambda s: s.copy(B["s2r"][0:1, 0:tg], B["psS2"][0:1, 0:tg]),
          reads=["ps:S2"], writes=["k:s2r"])
    em.op("t", lambda t: t.matmul(B["psB"][0:128, 0:tg], B[bc_key][0:1, :],
                                  B["s1r"][0:1, 0:tg], start=True, stop=True),
          reads=["k:s1r", "k:" + bc_key], writes=["ps:B"])
    em.op("s", lambda s: s.copy(B["mub"][:, 0:tg], B["psB"][0:128, 0:tg]),
          reads=["ps:B"], writes=["k:mub"])
    em.op("t", lambda t: t.matmul(B["psB"][0:128, 0:tg], B[bc_key][0:1, :],
                                  B["s2r"][0:1, 0:tg], start=True, stop=True),
          reads=["k:s2r", "k:" + bc_key], writes=["ps:B"])
    em.op("s", lambda s: s.copy(B["m2b"][:, 0:tg], B["psB"][0:128, 0:tg]),
          reads=["ps:B"], writes=["k:m2b"])
    em.op("s", lambda s: s.activation(B["mu2"][:, 0:tg], B["mub"][:, 0:tg],
                                      AF.Square, bias=0.0, scale=1.0),
          reads=["k:mub"], writes=["k:mu2"])
    em.op("v", lambda v: v.tensor_tensor(B["varb"][:, 0:tg], B["m2b"][:, 0:tg],
                                         B["mu2"][:, 0:tg], ALU.subtract),
          reads=["k:m2b", "k:mu2"], writes=["k:varb"])
    em.op("v", lambda v: v.tensor_scalar(B["varb"][:, 0:tg], B["varb"][:, 0:tg],
                                         1e-5, None, ALU.add),
          reads=["k:varb"], writes=["k:varb"])
    em.op("s", lambda s: s.activation(B["varb"][:, 0:tg], B["varb"][:, 0:tg],
                                      AF.Ln, bias=0.0, scale=1.0),
          reads=["k:varb"], writes=["k:varb"])
    em.op("s", lambda s: s.activation(B["rstd"][:, 0:tg], B["varb"][:, 0:tg],
                                      AF.Exp, bias=0.0, scale=-0.5),
          reads=["k:varb"], writes=["k:rstd"])


def emit_ln_apply(em, B, src, src_key, c, tg, out, out_key, scale=None,
                  bias=None, out2=None, out2_key=None):
    sl = src[:, c * tg:(c + 1) * tg]
    t1 = B["t1"]
    em.op("v", lambda v: v.tensor_tensor(t1[:, 0:tg], sl, B["mub"][:, 0:tg],
                                         ALU.subtract),
          reads=[src_key, "k:mub"], writes=["k:t1"])
    if scale is None:
        em.op("v", lambda v: v.tensor_tensor(out, t1[:, 0:tg],
                                             B["rstd"][:, 0:tg], ALU.mult),
              reads=["k:t1", "k:rstd"], writes=[out_key])
        if out2 is not None:
            em.op("s", lambda s: s.copy(out2, out), reads=[out_key],
                  writes=[out2_key])
    else:
        em.op("v", lambda v: v.tensor_tensor(t1[:, 0:tg], t1[:, 0:tg],
                                             B["rstd"][:, 0:tg], ALU.mult),
              reads=["k:t1", "k:rstd"], writes=["k:t1"])
        em.op("v", lambda v: v.tensor_scalar(out, t1[:, 0:tg], scale, bias,
                                             ALU.mult, ALU.add),
              reads=["k:t1", "k:par", "k:parb", "k:parc", "k:pard"],
              writes=[out_key])


def mk_ln_bufs(em, tg, maxch):
    B = {}
    B["onescol"] = em.sb("onescol", [128, 1], BF16)
    B["bc8"] = em.sb("bc8", [1, 128], F32)
    B["bc56"] = em.sb("bc56", [1, 128], F32)
    sqT = em.sb("sqT", [128, 2 * tg], BF16)
    B["sqT"] = sqT
    B["sq"] = [_ArView(sqT, i * tg, 128, tg) for i in range(2)]
    B["cb"] = [em.sb(f"cbb{i}", [128, tg], BF16) for i in range(2)]
    B["s1r"] = em.sb("s1r", [1, tg], F32)
    B["s2r"] = em.sb("s2r", [1, tg], F32)
    B["mub"] = em.sb("mub", [128, tg], F32)
    B["m2b"] = em.sb("m2b", [128, tg], F32)
    B["varb"] = em.sb("varb", [128, tg], F32)
    B["rstd"] = em.sb("rstd", [128, tg], F32)
    B["mu2"] = em.sb("mu2", [128, tg], F32)
    B["t1"] = em.sb("t1", [128, tg], F32)
    B["psS1"] = em.ps("psS1", [1, 512])
    B["psS2"] = em.ps("psS2", [1, 512])
    B["psB"] = em.ps("psB", [128, 512])
    em.op("v", lambda v: v.memset(B["onescol"][:], 1.0), writes=["k:ones"])
    em.op("v", lambda v: v.memset(B["bc8"][:], 1.0 / 1024.0), writes=["k:bc8"])
    em.op("v", lambda v: v.memset(B["bc56"][:], 1.0 / 7168.0), writes=["k:bc56"])
    return B


def build_phase_a(TG=256, Tt=T, em=None, io=None, B=None):
    if em is None:
        em = Em("A")
        io = {
            "XF": em.din("XF", [D, Tt], F32),
            "WINT": em.din("WINT", [D, QKVP], F32),
            "LNG_IN": em.din("LNG_IN", [128, 8], F32),
            "LNB_IN": em.din("LNB_IN", [128, 8], F32),
            "MIDG": em.din("MIDG", [128, 56], F32),
            "MIDB": em.din("MIDB", [128, 56], F32),
            "SHIFT1": em.din("SHIFT1", [128, 128], BF16),
            "OH127": em.din("OH127", [128, 128], BF16),
            "IDENT": em.din("IDENT", [128, 128], BF16),
            "XNF": em.dout("XNF", [D, Tt], F32),
            "XNB": em.dout("XNB", [D, Tt], BF16),
            "QH": em.dout("QH", [NH, DH, Tt], BF16),
            "KH": em.dout("KH", [NH, DH, Tt], BF16),
            "VSC": em.dout("VSC", [128, (Tt // 128) * 1056], BF16),
            "PG": em.dout("PG", [4096, Tt], BF16),
        }
    NG = Tt // TG
    KTPG = TG // 128

    (XF, WINT, LNG_IN, LNB_IN, MIDG, MIDB, SHIFT1, OH127, IDENT, XNF, XNB, QH, KH, VSC, PG) = (io[k] for k in ("XF", "WINT", "LNG_IN", "LNB_IN", "MIDG", "MIDB", "SHIFT1", "OH127", "IDENT", "XNF", "XNB", "QH", "KH", "VSC", "PG",))
    winTa = em.salloc("winTa", [128, 7 * QKVP], BF16)
    winTb = em.salloc("winTb", [128, QKVP], BF16)

    def winT_sl(c, lo, hi):
        if c < 7:
            return winTa[:, QKVP * c + lo:QKVP * c + hi]
        return winTb[:, lo:hi]
    lng = em.sb("lng", [128, 8], F32)
    lnb = em.sb("lnb", [128, 8], F32)
    midg = em.sb("midg", [128, 56], F32)
    midb = em.sb("midb", [128, 56], F32)
    qg_s = em.sb("qg_s", [128, 8], F32)
    qb_s = em.sb("qb_s", [128, 8], F32)
    shm = em.sb("shm", [128, 128], BF16)
    ohm = em.sb("ohm", [128, 128], BF16)
    idn = em.sb("idn", [128, 128], BF16)
    xg = em.salloc("xg", [128, NCH * TG], F32)
    xnf = em.salloc("xnf", [128, NCH * TG], F32)
    xnb = em.salloc("xnb", [128, NCH * TG], BF16)
    x1 = em.salloc("x1", [128, NCH * (TG + 4)], BF16)
    hgrp = em.salloc("hgrp", [128, 56 * TG], BF16)
    stage = [em.salloc(f"stage{i}", [128, TG], BF16) for i in range(2)]
    vstg = em.salloc("vstg", [128, TG], BF16)
    vaugg = em.salloc("vaugg", [128, KTPG * 1056], BF16)
    gbuf = [em.salloc(f"gbuf{i}", [128, TG], BF16) for i in range(2)]
    pgo = [em.salloc(f"pgo{i}", [128, TG], BF16) for i in range(2)]

    if B is None:
        B = mk_ln_bufs(em, TG, 56)
    psW = [em.ps(f"psW{i}", [128, 512]) for i in range(3)]
    psT = em.ps("psT", [128, 1024], BF16)
    psSh = em.ps("psSh", [128, 512])

    em.dma("y", lng[:], LNG_IN, writes=["k:par"])
    em.dma("y", lnb[:], LNB_IN, writes=["k:parb"])
    em.dma("y", midg[:], MIDG, writes=["k:parc"])
    em.dma("y", midb[:], MIDB, writes=["k:pard"])
    em.dma("g", shm[:], SHIFT1, writes=["k:shm"])
    em.dma("g", ohm[:], OH127, writes=["k:ohm"])
    em.dma("g", idn[:], IDENT, writes=["k:idn"])
    for c in range(NCH):
        em.dma("g", winT_sl(c, 0, QKVP),
               WINT[128 * c:128 * (c + 1), :], writes=[f"k:winT{c}"])
    em.op("v", lambda v: v.tensor_scalar(qg_s[:], midg[:, 0:8], 0.125, None,
                                         ALU.mult),
          reads=["k:parc"], writes=["k:par"])
    em.op("v", lambda v: v.tensor_scalar(qb_s[:], midb[:, 0:8], 0.125, None,
                                         ALU.mult),
          reads=["k:pard"], writes=["k:par"])
    em.op("v", lambda v: v.memset(x1[:, :], 0.0), writes=["k:x1"])
    em.op("v", lambda v: v.memset(vaugg[:, :], 0.0), writes=["k:vaugg"])

    for g in range(NG):
        t0 = g * TG
        em.dma("y", xg[:], fm_ap(XF, 0, NCH, t0, TG), writes=["k:xg"])

        emit_ln(em, B, xg, "k:xg", NCH, TG, "bc8", "e",
                src_is_f32=True)
        for c in range(NCH):
            emit_ln_apply(em, B, xg, "k:xg", c, TG,
                          xnf[:, c * TG:(c + 1) * TG], "k:xnf",
                          out2=xnb[:, c * TG:(c + 1) * TG], out2_key="k:xnb")
        em.dma("y", fm_ap(XNF, 0, NCH, t0, TG), xnf[:], reads=["k:xnf"],
               is_out=True)
        em.dma("y", fm_ap(XNB, 0, NCH, t0, TG), xnb[:], reads=["k:xnb"],
               is_out=True)

        for c in (0, 1):
            base = c * (TG + 4)
            em.op("v", lambda v, base=base: v.tensor_copy(
                x1[:, base:base + 4], x1[:, base + TG:base + TG + 4]),
                reads=["k:x1"], writes=["k:x1"])
        emit_ln(em, B, xnb, "k:xnb", NCH, TG, "bc8", "i")
        for c in range(NCH):
            base = c * (TG + 4)
            emit_ln_apply(em, B, xnb, "k:xnb", c, TG,
                          x1[:, base + 4:base + 4 + TG], "k:x1",
                          scale=lng[:, c:c + 1], bias=lnb[:, c:c + 1])

        def rhs_for(c):
            base = c * (TG + 4)
            if c == 0:
                return x1[:, base + 3:base + 3 + TG]
            if c == 1:
                return x1[:, base + 1:base + 1 + TG]
            return x1[:, base + 4:base + 4 + TG]

        def a_stats(oc):
            if oc % 2 == 0:
                em.op("t", lambda t, oc=oc: t.matmul(
                    B["psS1"][0:1, 0:2 * TG], B["onescol"][:],
                    hgrp[:, oc * TG:(oc + 2) * TG],
                    start=(oc == 0), stop=(oc == 54)),
                    reads=["k:hgrp", "k:ones"],
                    writes=["ps:S1"] if oc in (0, 54) else [], inc=1)
            sc0 = (oc % 2) * TG
            em.op("v", lambda v, oc=oc, sc0=sc0: v.tensor_tensor(
                B["sqT"][0:128, sc0:sc0 + TG],
                hgrp[:, oc * TG:(oc + 1) * TG],
                hgrp[:, oc * TG:(oc + 1) * TG], ALU.mult),
                reads=["k:hgrp"], writes=[f"k:sq{oc % 2}"])
            if oc % 2 == 1:
                em.op("t", lambda t, oc=oc: t.matmul(
                    B["psS2"][0:1, 0:2 * TG], B["onescol"][:],
                    B["sqT"][0:128, 0:2 * TG],
                    start=(oc == 1), stop=(oc == 55)),
                    reads=["k:sq0", "k:sq1", "k:ones"],
                    writes=["ps:S2"] if oc in (1, 55) else [], inc=1)

        for oc in range(56):
            pw = psW[oc % 3]
            for c in range(NCH):
                lhs = winT_sl(c, 128 * oc, 128 * (oc + 1))
                em.op("t", lambda t, lhs=lhs, c=c, pw=pw: t.matmul(
                    pw[:, 0:TG], lhs, rhs_for(c), start=(c == 0),
                    stop=(c == NCH - 1)),
                    reads=["k:x1", f"k:winT{c}"],
                    writes=[f"ps:W{oc % 3}"] if c in (0, NCH - 1) else [],
                    inc=1 if c == NCH - 1 else 0)
            hsl = hgrp[:, oc * TG:(oc + 1) * TG]
            em.op("s", lambda s, hsl=hsl, pw=pw, oc=oc: s.activation(
                hsl, pw[:, 0:TG], AF.Relu, bias=0.0, scale=1.0),
                reads=[f"ps:W{oc % 3}"], writes=["k:hgrp"])
            if oc >= 1:
                a_stats(oc - 1)
        a_stats(55)
        em.op("s", lambda s: s.copy(B["s1r"][0:1, 0:2 * TG],
                                    B["psS1"][0:1, 0:2 * TG]),
              reads=["ps:S1"], writes=["k:s1r"])
        em.op("v", lambda v: v.tensor_tensor(
            B["s1r"][0:1, 0:TG], B["s1r"][0:1, 0:TG],
            B["s1r"][0:1, TG:2 * TG], ALU.add),
            reads=["k:s1r"], writes=["k:s1r"])
        em.op("s", lambda s: s.copy(B["s2r"][0:1, 0:2 * TG],
                                    B["psS2"][0:1, 0:2 * TG]),
              reads=["ps:S2"], writes=["k:s2r"])
        em.op("v", lambda v: v.tensor_tensor(
            B["s2r"][0:1, 0:TG], B["s2r"][0:1, 0:TG],
            B["s2r"][0:1, TG:2 * TG], ALU.add),
            reads=["k:s2r"], writes=["k:s2r"])
        em.op("t", lambda t: t.matmul(B["psB"][0:128, 0:TG], B["bc56"][0:1, :],
                                      B["s1r"][0:1, 0:TG], start=True, stop=True),
              reads=["k:s1r", "k:bc56"], writes=["ps:B"])
        em.op("s", lambda s: s.copy(B["mub"][:, 0:TG], B["psB"][0:128, 0:TG]),
              reads=["ps:B"], writes=["k:mub"])
        em.op("t", lambda t: t.matmul(B["psB"][0:128, 0:TG], B["bc56"][0:1, :],
                                      B["s2r"][0:1, 0:TG], start=True, stop=True),
              reads=["k:s2r", "k:bc56"], writes=["ps:B"])
        em.op("s", lambda s: s.copy(B["m2b"][:, 0:TG], B["psB"][0:128, 0:TG]),
              reads=["ps:B"], writes=["k:m2b"])
        em.op("s", lambda s: s.activation(B["mu2"][:, 0:TG], B["mub"][:, 0:TG],
                                          AF.Square, bias=0.0, scale=1.0),
              reads=["k:mub"], writes=["k:mu2"])
        em.op("v", lambda v: v.tensor_tensor(B["varb"][:, 0:TG],
                                             B["m2b"][:, 0:TG],
                                             B["mu2"][:, 0:TG], ALU.subtract),
              reads=["k:m2b", "k:mu2"], writes=["k:varb"])
        em.op("v", lambda v: v.tensor_scalar(B["varb"][:, 0:TG],
                                             B["varb"][:, 0:TG],
                                             1e-5, None, ALU.add),
              reads=["k:varb"], writes=["k:varb"])
        em.op("s", lambda s: s.activation(B["varb"][:, 0:TG], B["varb"][:, 0:TG],
                                          AF.Ln, bias=0.0, scale=1.0),
              reads=["k:varb"], writes=["k:varb"])
        em.op("s", lambda s: s.activation(B["rstd"][:, 0:TG], B["varb"][:, 0:TG],
                                          AF.Exp, bias=0.0, scale=-0.5),
              reads=["k:varb"], writes=["k:rstd"])

        def mid_apply(oc, dst, dkey, sc, bi):
            hsl = hgrp[:, oc * TG:(oc + 1) * TG]
            em.op("v", lambda v, hsl=hsl: v.tensor_tensor(
                B["t1"][:, 0:TG], hsl, B["mub"][:, 0:TG], ALU.subtract),
                reads=["k:hgrp", "k:mub"], writes=["k:t1"])
            em.op("v", lambda v: v.tensor_tensor(
                B["t1"][:, 0:TG], B["t1"][:, 0:TG], B["rstd"][:, 0:TG],
                ALU.mult),
                reads=["k:t1", "k:rstd"], writes=["k:t1"])
            em.op("v", lambda v, dst=dst, sc=sc, bi=bi: v.tensor_scalar(
                dst, B["t1"][:, 0:TG], sc, bi, ALU.mult, ALU.add),
                reads=["k:t1", "k:par", "k:parb", "k:parc", "k:pard"],
                writes=[dkey])

        for oc in range(8):
            sg = stage[oc % 2]
            mid_apply(oc, sg[:, 0:TG], f"k:stage{oc % 2}",
                      qg_s[:, oc:oc + 1], qb_s[:, oc:oc + 1])
            em.dma("y", QH[2 * oc, :, t0:t0 + TG], sg[0:64, 0:TG],
                   reads=[f"k:stage{oc % 2}"], is_out=True)
            em.dma("y", QH[2 * oc + 1, :, t0:t0 + TG], sg[64:128, 0:TG],
                   reads=[f"k:stage{oc % 2}"], is_out=True)
        for oc in range(8, 16):
            sg = stage[oc % 2]
            mid_apply(oc, sg[:, 0:TG], f"k:stage{oc % 2}",
                      midg[:, oc:oc + 1], midb[:, oc:oc + 1])
            h0 = 2 * (oc - 8)
            em.dma("y", KH[h0, :, t0:t0 + TG], sg[0:64, 0:TG],
                   reads=[f"k:stage{oc % 2}"], is_out=True)
            em.dma("y", KH[h0 + 1, :, t0:t0 + TG], sg[64:128, 0:TG],
                   reads=[f"k:stage{oc % 2}"], is_out=True)
        for oc in range(16, 24):
            mid_apply(oc, vstg[:, 0:TG], "k:vstg",
                      midg[:, oc:oc + 1], midb[:, oc:oc + 1])
            for ktl in range(KTPG):
                em.op("t", lambda t, ktl=ktl: t.transpose(
                    psT[:, 0:128], vstg[:, 128 * ktl:128 * (ktl + 1)], idn[:]),
                    reads=["k:vstg", "k:idn"], writes=["ps:T"])
                h0 = 2 * (oc - 16)
                for hh in range(2):
                    cbase = 1056 * ktl + 66 * (h0 + hh)
                    em.op("s", lambda s, cbase=cbase, hh=hh: s.copy(
                        vaugg[:, cbase:cbase + 64],
                        psT[:, 64 * hh:64 * (hh + 1)]),
                        reads=["ps:T"], writes=["k:vaugg"])
        em.dma("y", VSC[:, 1056 * KTPG * g:1056 * KTPG * (g + 1)], vaugg[:],
               reads=["k:vaugg"], is_out=True)

        for oc in range(24, 56):
            mid_apply(oc, hgrp[:, oc * TG:(oc + 1) * TG], "k:hgrp",
                      midg[:, oc:oc + 1], midb[:, oc:oc + 1])

        def emit_pg_sh(j, i):
            cur = hgrp[:, j * TG:(j + 1) * TG]
            prev = hgrp[:, (55 if j == 24 else j - 1) * TG:
                        (56 if j == 24 else j) * TG]
            pb = psSh[:, 0:TG] if i % 2 == 0 else psW[0][:, 0:TG]
            pkey = "ps:Sh" if i % 2 == 0 else "ps:W0"
            em.op("t", lambda t, cur=cur, pb=pb: t.matmul(
                pb, shm[:], cur, start=True, stop=False),
                reads=["k:hgrp", "k:shm"], writes=[pkey], inc=0)
            em.op("t", lambda t, prev=prev, pb=pb: t.matmul(
                pb, ohm[:], prev, start=False, stop=True),
                reads=["k:hgrp", "k:ohm"], writes=[pkey])
            return pb, pkey

        order = list(range(25, 56)) + [24]
        shq = []
        for i, j in enumerate(order):
            pb, pkey = emit_pg_sh(j, i)
            em.op("s", lambda s, j=j, i=i: s.activation(
                gbuf[i % 2][:, 0:TG], hgrp[:, j * TG:(j + 1) * TG],
                AF.Gelu, bias=0.0, scale=1.0),
                reads=["k:hgrp"], writes=[f"k:gbuf{i % 2}"])
            em.op("v", lambda v, i=i, pb=pb: v.tensor_tensor(
                pgo[i % 2][:, 0:TG], gbuf[i % 2][:, 0:TG], pb, ALU.mult),
                reads=[f"k:gbuf{i % 2}", pkey], writes=[f"k:pgo{i % 2}"])
            em.dma("y", PG[128 * (j - 24):128 * (j - 23), t0:t0 + TG],
                   pgo[i % 2][:, 0:TG], reads=[f"k:pgo{i % 2}"], is_out=True)

    return em



def build_phase_b(Tt=T, em=None, io=None, B=None):
    if em is None:
        em = Em("B")
        io = {
            "QHD": em.din("QH", [NH, DH, Tt], BF16),
            "KHD": em.din("KH", [NH, DH, Tt], BF16),
            "VSCD": em.din("VSC", [128, (Tt // 128) * 1056], BF16),
            "SLOPES": em.din("SLOPES", [1, 16], F32),
            "IOTA128": em.din("IOTA128", [128, 1], F32),
            "IOTAT": em.din("IOTAT", [1, Tt], F32),
            "PREMASK": em.din("PREMASK", [512, 512], BF16),
            "PREMASKT": em.din("PREMASKT", [512, 512], BF16),
            "IDENT": em.din("IDENT", [128, 128], BF16),
            "OH": em.dout("OH", [NH, DH, Tt], BF16),
            "MH": em.nc.dram_tensor("MH", [NH, Tt], BF16, kind="Internal").ap(),
        }
    NQT = Tt // 128
    NQG = Tt // 512

    (QHD, KHD, VSCD, SLOPES, IOTA128, IOTAT, PREMASK, PREMASKT, IDENT, OH, MH) = (io[k] for k in ("QHD", "KHD", "VSCD", "SLOPES", "IOTA128", "IOTAT", "PREMASK", "PREMASKT", "IDENT", "OH", "MH",))
    vsc = em.salloc("vsc", [128, (Tt // 128) * 1056], BF16)
    kb = [em.salloc(f"kb{i}", [66, Tt], BF16) for i in range(2)]
    qb = [em.salloc(f"qb{i}", [66, Tt], BF16) for i in range(2)]
    slp = em.sb("slp", [1, 16], F32)
    slpb = em.sb("slpb", [128, 16], F32)
    onesf = em.sb("onesf", [1, 128], F32)
    onesb64 = em.sb("onesb64", [128, 64], BF16)
    io128 = em.sb("io128", [128, 1], F32)
    iot = em.salloc("iot", [1, Tt], F32)
    alro = em.salloc("alro", [1, Tt], BF16)
    albc = em.salloc("albc", [128, NH * 32], F32)
    prem = em.salloc("prem", [128, 4 * 512], BF16)
    premt = em.salloc("premt", [128, 4 * 512], BF16)
    idn = em.sb("idn", [128, 128], BF16)
    rmax = em.salloc("rmax", [128, NQT], BF16)
    rtmp = em.sb("rtmp", [128, 4], BF16)
    mrow = em.salloc("mrow", [32, 128], BF16)
    abuf = em.salloc("abuf", [128, 4 * 512], BF16)
    zrs = em.salloc("zrs", [128, 512], F32)
    zrb = em.salloc("zrb", [128, 512], BF16)
    fbc = em.salloc("fbc", [64, 512], F32)
    obuf = em.salloc("obuf", [64, 512], BF16)

    psS = [em.ps(f"psS{i}", [128, 512]) for i in range(4)]
    psO = [em.ps(f"psO{i}", [128, 512]) for i in range(2)]
    psB = em.ps("psB", [128, 512])
    psTr = em.ps("psTr", [128, 1024], BF16)

    em.dma("y", vsc[:], VSCD, writes=["k:vsc"])
    em.dma("y", slp[:], SLOPES, writes=["k:slp"])
    em.dma("y", io128[:], IOTA128, writes=["k:io"])
    em.dma("y", iot[:], IOTAT, writes=["k:io"])
    em.dma("y", prem[:], PREMASK.rearrange("(m p) j -> p m j", p=128),
           writes=["k:prem"])
    em.dma("y", premt[:], PREMASKT.rearrange("(m p) j -> p m j", p=128),
           writes=["k:premt"])
    em.dma("y", idn[:], IDENT, writes=["k:idn"])
    em.op("v", lambda v: v.memset(onesf[:], 1.0), writes=["k:ones"])
    em.op("v", lambda v: v.memset(onesb64[:], 1.0), writes=["k:ones64"])
    for i in range(2):
        em.op("v", lambda v, i=i: v.memset(kb[i][64:65, :], 1.0),
              writes=[f"k:kb{i}"])
        em.op("v", lambda v, i=i: v.memset(qb[i][64:66, :], 1.0),
              writes=[f"k:qb{i}"])
    nt = (Tt // 128) * 16
    em.op("v", lambda v: v.memset(vsc[:, 64:1056 * (Tt // 128):66], 1.0),
          reads=[], writes=["k:vsc"])
    em.op("t", lambda t: t.matmul(psB[0:128, 0:16], onesf[0:1, :], slp[0:1, :],
                                  start=True, stop=True),
          reads=["k:ones", "k:slp"], writes=["ps:B"])
    em.op("s", lambda s: s.copy(slpb[:], psB[0:128, 0:16]),
          reads=["ps:B"], writes=["k:slpb"])
    for h in range(NH):
        for kt in range(Tt // 128):
            em.op("v", lambda v, h=h, kt=kt: v.tensor_scalar(
                albc[:, 32 * h + kt:32 * h + kt + 1], io128[:],
                float(128 * kt), slpb[:, h:h + 1], ALU.add, ALU.mult),
                reads=["k:io", "k:slpb"], writes=["k:albc"], inc=0)
    em.op("v", lambda v: v.tensor_copy(rtmp[:, 0:1], io128[:]),
          reads=["k:io"], writes=["k:rtmp0"])

    def b_prepass(h):
        par = h % 2
        kbu, qbu = kb[par], qb[par]
        kkey, qkey = f"k:kb{par}", f"k:qb{par}"
        em.dma("y", kbu[0:64, :], KHD[h], reads=[], writes=[kkey])
        em.dma("y", qbu[0:64, :], QHD[h], reads=[], writes=[qkey])
        em.op("v", lambda v, h=h: v.tensor_scalar(
            alro[:], iot[:], slp[0:1, h:h + 1], None, ALU.mult),
            reads=["k:io", "k:slp", "k:alro"], writes=["k:alro"])
        em.dma("s", kbu[65:66, :], alro[0:1, :], reads=["k:alro"],
               writes=[kkey])
        em.op("v", lambda v, qbu=qbu: v.memset(qbu[64:65, :], 0.0),
              writes=[qkey])
        for qt in range(NQT):
            for kg in range(qt // 4 + 1):
                ip = (qt * 9 + kg) % 4
                pp = psS[ip]
                diag = (kg == qt // 4)
                em.op("t", lambda t, pp=pp, qt=qt, kg=kg, qbu=qbu, kbu=kbu,
                      diag=diag:
                      t.matmul(pp[:, 0:512],
                               qbu[0:66, 128 * qt:128 * (qt + 1)],
                               kbu[0:66, 512 * kg:512 * (kg + 1)],
                               start=True, stop=not diag),
                      reads=[qkey, kkey], writes=[f"ps:S{ip}"],
                      inc=0 if diag else 1)
                if diag:
                    m = qt % 4
                    em.op("t", lambda t, pp=pp, m=m: t.matmul(
                        pp[:, 0:512], idn[:],
                        prem[:, 512 * m:512 * (m + 1)],
                        start=False, stop=True),
                        reads=["k:idn", "k:prem"],
                        writes=[f"ps:S{ip}"])
                if kg == 0:
                    em.op("v", lambda v, pp=pp, qt=qt: v.reduce_max(
                        rmax[:, qt:qt + 1], pp[:, 0:512], axis=AX.X),
                        reads=[f"ps:S{ip}"], writes=["k:rmax"])
                else:
                    em.op("v", lambda v, pp=pp, qt=qt: v.reduce_max(
                        rtmp[:, qt % 4:qt % 4 + 1], pp[:, 0:512], axis=AX.X),
                        reads=[f"ps:S{ip}"], writes=[f"k:rtmp{qt % 4}"])
                    em.op("v", lambda v, qt=qt: v.tensor_tensor(
                        rmax[:, qt:qt + 1], rmax[:, qt:qt + 1],
                        rtmp[:, qt % 4:qt % 4 + 1], ALU.max),
                        reads=[f"k:rtmp{qt % 4}", "k:rmax"], writes=["k:rmax"])
        em.op("t", lambda t: t.transpose(psTr[0:NQT, 0:128], rmax[:, 0:NQT],
                                         idn[:]),
              reads=["k:rmax", "k:idn"], writes=["ps:Tr"])
        em.op("s", lambda s: s.activation(mrow[0:NQT, :], psTr[0:NQT, 0:128],
                                          AF.Copy, bias=CSH, scale=-1.0),
              reads=["ps:Tr"], writes=["k:mrow"])
        em.dma("y", MH[h:h + 1, :].rearrange("o (a b) -> (o a) b", b=128),
               mrow[0:NQT, :], reads=["k:mrow"], writes=["k:MH"])
        em.dma("y", qbu[64:65, :], MH[h:h + 1, :], reads=["k:MH"],
               writes=[qkey])

    def b_realpass(h):
        par = h % 2
        kbu, qbu = kb[par], qb[par]
        kkey, qkey = f"k:kb{par}", f"k:qb{par}"
        for qg in range(NQG):
            nkt = 4 * qg + 4
            LAG = 2
            pso = psO[qg % 2]
            pokey = f"ps:O{qg % 2}"

            def emit_o(kt, nkt=None, h=h, pso=pso, pokey=pokey):
                asl = abuf[:, 512 * (kt % 4):512 * (kt % 4 + 1)]
                em.op("t", lambda t, asl=asl, kt=kt, nkt=nkt, h=h, pso=pso:
                      t.matmul(
                    pso[0:66, 0:512], vsc[:, 1056 * kt + 66 * h:
                                          1056 * kt + 66 * h + 66],
                    asl, start=(kt == 0), stop=(kt == nkt - 1)),
                    reads=[f"k:ab{kt % 4}", "k:vsc"],
                    writes=[pokey] if kt in (0, nkt - 1) else [],
                    inc=1 if kt == nkt - 1 else 0)

            for kt in range(nkt):
                ss = psS[kt % 4]
                diag = kt >= 4 * qg
                em.op("t", lambda t, ss=ss, kt=kt, qg=qg, kbu=kbu, qbu=qbu,
                      diag=diag:
                      t.matmul(ss[:, 0:512],
                               kbu[0:65, 128 * kt:128 * (kt + 1)],
                               qbu[0:65, 512 * qg:512 * (qg + 1)],
                               start=True, stop=not diag),
                      reads=[kkey, qkey], writes=[f"ps:S{kt % 4}"],
                      inc=0 if diag else 1)
                if diag:
                    m = kt - 4 * qg
                    em.op("t", lambda t, ss=ss, m=m: t.matmul(
                        ss[:, 0:512], idn[:],
                        premt[:, 512 * m:512 * (m + 1)],
                        start=False, stop=True),
                        reads=["k:idn", "k:premt"],
                        writes=[f"ps:S{kt % 4}"])
                asl = abuf[:, 512 * (kt % 4):512 * (kt % 4 + 1)]
                em.op("s", lambda s, ss=ss, asl=asl, h=h, kt=kt: s.activation(
                    asl, ss[:, 0:512], AF.Exp,
                    bias=albc[:, 32 * h + kt:32 * h + kt + 1], scale=1.0),
                    reads=[f"ps:S{kt % 4}", "k:albc"],
                    writes=[f"k:ab{kt % 4}"])
                if kt >= LAG:
                    emit_o(kt - LAG, nkt)
            for kt in range(max(0, nkt - LAG), nkt):
                emit_o(kt, nkt)
            em.op("s", lambda s, pso=pso: s.activation(
                zrs[64:65, 0:512], pso[64:65, 0:512], AF.Ln,
                bias=0.0, scale=2.0 ** -24),
                  reads=[pokey], writes=["k:zrs"])
            em.op("s", lambda s: s.activation(
                zrb[64:65, 0:512], zrs[64:65, 0:512], AF.Exp,
                bias=0.0, scale=-1.0),
                  reads=["k:zrs"], writes=["k:zrb"])
            em.op("t", lambda t: t.matmul(psB[0:64, 0:512],
                                          onesb64[64:65, 0:64],
                                          zrb[64:65, 0:512],
                                          start=True, stop=True),
                  reads=["k:zrb", "k:ones64"], writes=["ps:B"])
            em.op("s", lambda s: s.activation(
                fbc[0:64, :], psB[0:64, 0:512], AF.Copy,
                bias=0.0, scale=2.0 ** -24),
                  reads=["ps:B"], writes=["k:fbc"])
            em.op("v", lambda v, pso=pso: v.tensor_tensor(
                obuf[:], pso[0:64, 0:512], fbc[:], ALU.mult),
                  reads=[pokey, "k:fbc"], writes=["k:obuf"])
            em.dma("y", OH[h, :, 512 * qg:512 * (qg + 1)], obuf[:],
                   reads=["k:obuf"], is_out=True)

    b_prepass(0)
    for h in range(NH):
        if h + 1 < NH:
            b_prepass(h + 1)
        b_realpass(h)
    return em
def build_phase_c(TG=256, Tt=T, em=None, io=None, B=None):
    if em is None:
        em = Em("C")
        io = {
            "OHD": em.din("OH", [NH, DH, Tt], BF16),
            "PGD": em.din("PG", [4096, Tt], BF16),
            "WOT_O": em.din("WOT_O", [64, 16 * 1024], F32),
            "WOT_PG": em.din("WOT_PG", [4096, 1024], F32),
            "OUTG": em.din("OUTG", [128, 8], F32),
            "OUTB": em.din("OUTB", [128, 8], F32),
            "XFD": em.din("XF", [D, Tt], F32),
            "XMID": em.dout("XMID", [D, Tt], F32),
        }
    NG = Tt // TG

    (OHD, PGD, WOT_O, WOT_PG, OUTG, OUTB, XFD, XMID) = (io[k] for k in ("OHD", "PGD", "WOT_O", "WOT_PG", "OUTG", "OUTB", "XFD", "XMID",))
    woto = em.salloc("woto", [64, 16 * 1024], BF16)
    wotp = em.salloc("wotp", [128, 32 * 1024], BF16)
    outg = em.sb("outg", [128, 8], F32)
    outb = em.sb("outb", [128, 8], F32)
    ohg = em.salloc("ohg", [64, 16 * TG], BF16)
    pgg = em.salloc("pgg", [128, 32 * TG], BF16)
    wg = em.salloc("wg", [128, NCH * TG], F32)
    wgb = em.salloc("wgb", [128, NCH * TG], BF16)
    xg = [em.salloc(f"xg{i}", [128, TG], F32) for i in range(2)]
    t2 = em.salloc("t2", [128, TG], F32)
    if B is None:
        B = mk_ln_bufs(em, TG, 8)
    psA = [em.ps(f"psA{i}", [128, 512]) for i in range(2)]
    psP2 = [em.ps(f"psP2{i}", [128, 512]) for i in range(2)]

    for i in range(4):
        em.dma("g", woto[:, 4096 * i:4096 * (i + 1)],
               WOT_O[:, 4096 * i:4096 * (i + 1)], writes=[f"k:woto{i}"])
    for f in range(32):
        em.dma("g", wotp[:, 1024 * f:1024 * (f + 1)],
               WOT_PG[128 * f:128 * (f + 1), :], writes=[f"k:wotpf{f}"])
    em.dma("y", outg[:], OUTG, writes=["k:par"])
    em.dma("y", outb[:], OUTB, writes=["k:parb"])

    for g in range(NG):
        t0 = g * TG
        for h in range(NH):
            em.dma("y", ohg[:, TG * h:TG * (h + 1)], OHD[h, :, t0:t0 + TG],
                   writes=["k:ohg"])
        em.dma("y", pgg[:], fm_ap(PGD, 0, 32, t0, TG), writes=["k:pgg"])
        for dc in range(NCH):
            pa, pp2 = psA[dc % 2], psP2[dc % 2]
            for h in range(NH):
                em.op("t", lambda t, h=h, dc=dc, pa=pa: t.matmul(
                    pa[0:128, 0:TG],
                    woto[0:64, 1024 * h + 128 * dc:1024 * h + 128 * (dc + 1)],
                    ohg[0:64, TG * h:TG * (h + 1)],
                    start=(h == 0), stop=(h == NH - 1)),
                    reads=["k:ohg", f"k:woto{h // 4}"],
                    writes=[f"ps:A{dc % 2}"] if h in (0, NH - 1) else [],
                    inc=1 if h == NH - 1 else 0)
            for f in range(32):
                em.op("t", lambda t, f=f, dc=dc, pp2=pp2: t.matmul(
                    pp2[:, 0:TG],
                    wotp[:, 1024 * f + 128 * dc:1024 * f + 128 * (dc + 1)],
                    pgg[:, TG * f:TG * (f + 1)],
                    start=(f == 0), stop=(f == 31)),
                    reads=["k:pgg", f"k:wotpf{f}"],
                    writes=[f"ps:P2{dc % 2}"] if f in (0, 31) else [],
                    inc=1 if f == 31 else 0)
            em.op("s", lambda s, pp2=pp2, dc=dc: s.copy(t2[:, 0:TG],
                                                          pp2[:, 0:TG]),
                  reads=[f"ps:P2{dc % 2}"], writes=["k:t2"])
            wsl = wg[:, dc * TG:(dc + 1) * TG]
            em.op("v", lambda v, wsl=wsl, pa=pa, dc=dc: v.tensor_tensor(
                wsl, pa[0:128, 0:TG], t2[:, 0:TG], ALU.add),
                reads=[f"ps:A{dc % 2}", "k:t2"], writes=["k:wg"])
            em.op("s", lambda s, dc=dc, wsl=wsl: s.copy(
                wgb[:, dc * TG:(dc + 1) * TG], wsl),
                reads=["k:wg"], writes=["k:wgb"])
        emit_ln(em, B, wgb, "k:wgb", NCH, TG, "bc8", "o")
        rings = [(B["t1"], "k:t1"), (B["mu2"], "k:mu2")]
        for dc in range(NCH):
            wsl = wg[:, dc * TG:(dc + 1) * TG]
            tr, tk = rings[dc % 2]
            xgr, xk = xg[dc % 2], f"k:xg{dc % 2}"
            em.op("v", lambda v, wsl=wsl, tr=tr: v.tensor_tensor(
                tr[:, 0:TG], wsl, B["mub"][:, 0:TG], ALU.subtract),
                reads=["k:wg", "k:mub"], writes=[tk])
            em.op("v", lambda v, tr=tr: v.tensor_tensor(
                tr[:, 0:TG], tr[:, 0:TG], B["rstd"][:, 0:TG],
                ALU.mult),
                reads=[tk, "k:rstd"], writes=[tk])
            em.op("s", lambda s, dc=dc, tr=tr: s.activation(
                tr[:, 0:TG], tr[:, 0:TG], AF.Identity,
                bias=outb[:, dc:dc + 1], scale=outg[:, dc:dc + 1]),
                reads=[tk, "k:par", "k:parb"], writes=[tk])
            em.dma("y", xgr[:, 0:TG], fm_ap(XFD, dc, 1, t0, TG),
                   writes=[xk])
            em.op("v", lambda v, wsl=wsl, tr=tr, xgr=xgr: v.tensor_tensor(
                wsl, tr[:, 0:TG], xgr[:, 0:TG], ALU.add),
                reads=[tk, xk], writes=["k:wg"])
        em.dma("y", fm_ap(XMID, 0, NCH, t0, TG), wg[:], reads=["k:wg"],
               is_out=True)
    return em


def build_phase_d1(TG=256, Tt=T, em=None, io=None, B=None):
    if em is None:
        em = Em("D1")
        io = {"XNB": em.din("XNB", [D, Tt], BF16),
              "W1": em.din("W1", [D, 4096], F32),
              "B1": em.din("B1", [128, 32], F32),
              "HF": em.dout("HF", [4096, Tt], BF16)}
    NG = Tt // TG
    XNBD, W1D, B1D, HFD = io["XNB"], io["W1"], io["B1"], io["HF"]
    w1 = em.salloc("w1", [128, NCH * 4096], BF16)
    b1 = em.sb("b1", [128, 32], F32)
    xnbg = em.salloc("xnbg", [128, NCH * TG], BF16)
    hbuf = em.salloc("hbuf", [128, TG], BF16)
    psH = [em.ps(f"psH{i}", [128, 512]) for i in range(4)]
    for c in range(NCH):
        em.dma("g", w1[:, 4096 * c:4096 * (c + 1)],
               W1D[128 * c:128 * (c + 1), :], writes=[f"k:w1c{c}"])
    em.dma("y", b1[:], B1D, writes=["k:par"])
    for g in range(NG):
        t0 = g * TG
        em.dma("y", xnbg[:], fm_ap(XNBD, 0, NCH, t0, TG), writes=["k:xnbg"])
        for f in range(32):
            pw = psH[f % 4]
            for c in range(NCH):
                em.op("t", lambda t, f=f, c=c, pw=pw: t.matmul(
                    pw[:, 0:TG],
                    w1[:, 4096 * c + 128 * f:4096 * c + 128 * (f + 1)],
                    xnbg[:, c * TG:(c + 1) * TG],
                    start=(c == 0), stop=(c == NCH - 1)),
                    reads=["k:xnbg", f"k:w1c{c}"],
                    writes=[f"ps:H{f % 4}"] if c in (0, NCH - 1) else [],
                    inc=1 if c == NCH - 1 else 0)
            em.op("s", lambda s, f=f, pw=pw: s.activation(
                hbuf[:, 0:TG], pw[:, 0:TG], AF.Relu,
                bias=b1[:, f:f + 1], scale=1.0),
                reads=[f"ps:H{f % 4}", "k:par"], writes=["k:hbuf"])
            em.dma("y", HFD[128 * f:128 * (f + 1), t0:t0 + TG], hbuf[:, 0:TG],
                   reads=["k:hbuf"], is_out=True)
    return em


def build_phase_d2(TG=256, Tt=T, em=None, io=None, B=None):
    if em is None:
        em = Em("D2")
        io = {"HF": em.din("HF", [4096, Tt], BF16),
              "XNF": em.din("XNF", [D, Tt], F32),
              "W2": em.din("W2", [4096, D], F32),
              "B2": em.din("B2", [128, 8], F32),
              "GAM": em.din("GAM", [128, 8], F32),
              "BET": em.din("BET", [128, 8], F32),
              "XMID": em.din("XMID", [D, Tt], F32),
              "X1R": em.dout("X1R", [D, Tt], F32)}
    NG = Tt // TG
    HFD, XNFD, W2D, B2D, GAMD, BETD, XMIDD, X1R = (io[k] for k in (
        "HF", "XNF", "W2", "B2", "GAM", "BET", "XMID", "X1R"))
    w2 = em.salloc("w2", [128, 32 * 1024], BF16)
    b2 = em.sb("b2", [128, 8], F32)
    gam = em.sb("gam", [128, 8], F32)
    bet = em.sb("bet", [128, 8], F32)
    hfg = em.salloc("hfg", [128, 32 * TG], BF16)
    xnfg = em.salloc("xnfg", [128, TG], F32)
    xmg = [em.salloc(f"xmg{i}", [128, TG], F32) for i in range(2)]
    yg = em.salloc("yg", [128, NCH * TG], F32)
    ygb = em.salloc("ygb", [128, NCH * TG], BF16)
    if B is None:
        B = mk_ln_bufs(em, TG, 8)
    psO2 = [em.ps(f"psO2{i}", [128, 512]) for i in range(2)]
    for f in range(32):
        em.dma("g", w2[:, 1024 * f:1024 * (f + 1)],
               W2D[128 * f:128 * (f + 1), :], writes=[f"k:w2f{f}"])
    em.dma("y", b2[:], B2D, writes=["k:parb"])
    em.dma("y", gam[:], GAMD, writes=["k:parc"])
    em.dma("y", bet[:], BETD, writes=["k:pard"])
    for g in range(NG):
        t0 = g * TG
        em.dma("y", hfg[:], fm_ap(HFD, 0, 32, t0, TG), writes=["k:hfg"])
        for dc in range(NCH):
            for f in range(32):
                em.op("t", lambda t, f=f, dc=dc: t.matmul(
                    psO2[dc % 2][:, 0:TG],
                    w2[:, 1024 * f + 128 * dc:1024 * f + 128 * (dc + 1)],
                    hfg[:, f * TG:(f + 1) * TG],
                    start=(f == 0), stop=(f == 31)),
                    reads=["k:hfg", f"k:w2f{f}"],
                    writes=[f"ps:O2{dc % 2}"] if f in (0, 31) else [],
                    inc=1 if f == 31 else 0)
            ysl = yg[:, dc * TG:(dc + 1) * TG]
            em.dma("y", xnfg[:, 0:TG], fm_ap(XNFD, dc, 1, t0, TG),
                   writes=["k:xnfg"])
            em.op("v", lambda v, ysl=ysl, dc=dc: v.scalar_tensor_tensor(
                ysl, psO2[dc % 2][:, 0:TG], b2[:, dc:dc + 1],
                xnfg[:, 0:TG], ALU.add, ALU.add),
                reads=[f"ps:O2{dc % 2}", "k:parb", "k:xnfg"], writes=["k:yg"])
            em.op("s", lambda s, ysl=ysl, dc=dc: s.copy(
                ygb[:, dc * TG:(dc + 1) * TG], ysl),
                reads=["k:yg"], writes=["k:ygb"])
        emit_ln(em, B, ygb, "k:ygb", NCH, TG, "bc8", "f")
        rings = [(B["t1"], "k:t1"), (B["mu2"], "k:mu2")]
        for dc in range(NCH):
            ysl = yg[:, dc * TG:(dc + 1) * TG]
            tr, tk = rings[dc % 2]
            xmr, xk = xmg[dc % 2], f"k:xmg{dc % 2}"
            em.op("v", lambda v, ysl=ysl, tr=tr: v.tensor_tensor(
                tr[:, 0:TG], ysl, B["mub"][:, 0:TG], ALU.subtract),
                reads=["k:yg", "k:mub"], writes=[tk])
            em.op("v", lambda v, tr=tr: v.tensor_tensor(
                tr[:, 0:TG], tr[:, 0:TG], B["rstd"][:, 0:TG],
                ALU.mult),
                reads=[tk, "k:rstd"], writes=[tk])
            em.op("s", lambda s, dc=dc, tr=tr: s.activation(
                tr[:, 0:TG], tr[:, 0:TG], AF.Identity,
                bias=bet[:, dc:dc + 1], scale=gam[:, dc:dc + 1]),
                reads=[tk, "k:parc", "k:pard"], writes=[tk])
            em.dma("y", xmr[:, 0:TG], fm_ap(XMIDD, dc, 1, t0, TG),
                   writes=[xk])
            em.op("v", lambda v, ysl=ysl, tr=tr, xmr=xmr: v.tensor_tensor(
                ysl, tr[:, 0:TG], xmr[:, 0:TG], ALU.add),
                reads=[tk, xk], writes=["k:yg"])
        em.dma("y", fm_ap(X1R, 0, NCH, t0, TG), yg[:], reads=["k:yg"],
               is_out=True)
    return em

def build_phase_e(TG=256, Tt=T, em=None, io=None, B=None):
    if em is None:
        em = Em("E")
        io = {
            "X1RD": em.din("X1R", [D, Tt], F32),
            "WID": em.din("WI", [D, D], F32),
            "WOD": em.din("WO", [D, D], F32),
            "WVD": em.din("WV", [D, D], F32),
            "WAD": em.din("WA", [D, 16], F32),
            "WED": em.din("WE", [D, 16], F32),
            "BAD": em.din("BA", [16, 1], F32),
            "BED": em.din("BE", [16, 1], F32),
            "HEADIND": em.din("HEADIND", [128, 128], F32),
            "IDENT": em.din("IDENT", [128, 128], BF16),
            "XN1F": em.dout("XN1F", [D, Tt], F32),
            "XN1B": em.dout("XN1B", [D, Tt], BF16),
            "Q1H": em.dout("Q1H", [NH, DH, Tt], BF16),
            "K1H": em.dout("K1H", [NH, DH, Tt], BF16),
            "VSC1": em.dout("VSC1", [128, (Tt // 128) * 1056], BF16),
            "M1H": em.dout("M1H", [NH, Tt], BF16),
            "APH": em.dout("APH", [NH, Tt], F32),
        }
    NG = Tt // TG
    KTPG = TG // 128

    (X1RD, WID, WOD, WVD, WAD, WED, BAD, BED, HEADIND, IDENT, XN1F, XN1B, Q1H, K1H, VSC1, M1H, APH) = (io[k] for k in ("X1RD", "WID", "WOD", "WVD", "WAD", "WED", "BAD", "BED", "HEADIND", "IDENT", "XN1F", "XN1B", "Q1H", "K1H", "VSC1", "M1H", "APH",))
    wi = em.salloc("wi", [128, NCH * 1024], BF16)
    wo = em.salloc("wo", [128, NCH * 1024], BF16)
    wv = em.salloc("wv", [128, NCH * 1024], BF16)
    wa = em.salloc("wa", [128, NCH * 16], BF16)
    we = em.salloc("we", [128, NCH * 16], BF16)
    hind = em.sb("hind", [128, 128], BF16)
    idn = em.sb("idn", [128, 128], BF16)
    ba = em.sb("ba", [16, 1], F32)
    be = em.sb("be", [16, 1], F32)
    xg = em.salloc("xg", [128, NCH * TG], F32)
    xn1f = em.salloc("xn1f", [128, NCH * TG], F32)
    xn1b = em.salloc("xn1b", [128, NCH * TG], BF16)
    stage = em.salloc("stage", [128, TG], BF16)
    sqs = em.salloc("sqs", [128, TG], BF16)
    apg = em.salloc("apg", [16, TG], F32)
    exg = em.salloc("exg", [16, TG], BF16)
    ext = em.sb("ext", [128, KTPG * 16], F32)
    vsc1g = em.salloc("vsc1g", [128, KTPG * 1056], BF16)
    sq16 = em.salloc("sq16", [16, Tt], BF16)
    sk16 = em.salloc("sk16", [16, Tt], BF16)
    kmx = em.sb("kmx", [16, 1], F32)
    nkm = em.sb("nkm", [16, 1], F32)
    srt = em.salloc("srt", [16, Tt], BF16)
    m1r = em.salloc("m1r", [16, Tt], BF16)
    if B is None:
        B = mk_ln_bufs(em, TG, 8)
    psM = [em.ps(f"psM{i}", [128, 512]) for i in range(2)]
    psAE = em.ps("psAE", [16, 512])
    psSq = em.ps("psSq", [16, 512])
    psTe = em.ps("psTe", [128, 1024], BF16)

    for c in range(NCH):
        em.dma("g", wi[:, 1024 * c:1024 * (c + 1)],
               WID[128 * c:128 * (c + 1), :], writes=["k:wi"])
        em.dma("g", wo[:, 1024 * c:1024 * (c + 1)],
               WOD[128 * c:128 * (c + 1), :], writes=["k:wo"])
        em.dma("g", wv[:, 1024 * c:1024 * (c + 1)],
               WVD[128 * c:128 * (c + 1), :], writes=["k:wv"])
        em.dma("g", wa[:, 16 * c:16 * (c + 1)],
               WAD[128 * c:128 * (c + 1), :], writes=["k:wa"])
        em.dma("g", we[:, 16 * c:16 * (c + 1)],
               WED[128 * c:128 * (c + 1), :], writes=["k:we"])
    em.dma("g", hind[:], HEADIND, writes=["k:hind"])
    em.dma("g", idn[:], IDENT, writes=["k:idn"])
    em.dma("y", ba[:], BAD, writes=["k:par"])
    em.dma("y", be[:], BED, writes=["k:parb"])
    em.op("v", lambda v: v.memset(vsc1g[:], 0.0), writes=["k:vsc1g"])
    em.op("v", lambda v: v.memset(
        vsc1g[:, 65:1056 * KTPG:66], 1.0), writes=["k:vsc1g"])

    for g in range(NG):
        t0 = g * TG
        em.dma("y", xg[:], fm_ap(X1RD, 0, NCH, t0, TG), writes=["k:xg"])
        emit_ln(em, B, xg, "k:xg", NCH, TG, "bc8", "e",
                src_is_f32=True)
        for c in range(NCH):
            emit_ln_apply(em, B, xg, "k:xg", c, TG,
                          xn1f[:, c * TG:(c + 1) * TG], "k:xn1f",
                          out2=xn1b[:, c * TG:(c + 1) * TG], out2_key="k:xn1b")
        em.dma("y", fm_ap(XN1F, 0, NCH, t0, TG), xn1f[:], reads=["k:xn1f"],
               is_out=True)
        em.dma("y", fm_ap(XN1B, 0, NCH, t0, TG), xn1b[:], reads=["k:xn1b"],
               is_out=True)
        for w_, bb, dst, dkey, wkey in ((wa, ba, apg, "k:apg", "k:wa"),
                                        (we, be, exg, "k:exg", "k:we")):
            for c in range(NCH):
                em.op("t", lambda t, w_=w_, c=c: t.matmul(
                    psAE[0:16, 0:TG], w_[:, 16 * c:16 * (c + 1)],
                    xn1b[:, c * TG:(c + 1) * TG],
                    start=(c == 0), stop=(c == NCH - 1)),
                    reads=["k:xn1b", wkey],
                    writes=["ps:AE"] if c in (0, NCH - 1) else [],
                    inc=1 if c == NCH - 1 else 0)
            em.op("s", lambda s, bb=bb, dst=dst, dkey=dkey: s.activation(
                dst[0:16, 0:TG], psAE[0:16, 0:TG], AF.Sigmoid,
                bias=bb[0:16, 0:1], scale=1.0),
                reads=["ps:AE", "k:par", "k:parb"], writes=[dkey])
        em.dma("y", APH[0:16, t0:t0 + TG], apg[0:16, 0:TG], reads=["k:apg"],
               is_out=True)
        for tt in range(KTPG):
            em.op("t", lambda t, tt=tt: t.transpose(
                psTe[0:128, 0:16], exg[0:16, 128 * tt:128 * (tt + 1)],
                idn[0:16, 0:16]),
                reads=["k:exg", "k:idn"], writes=["ps:Te"])
            em.op("s", lambda s, tt=tt: s.copy(
                ext[:, 16 * tt:16 * (tt + 1)], psTe[0:128, 0:16]),
                reads=["ps:Te"], writes=["k:ext"])
        for which, wmat, wkey, DST, SQ16 in (
                ("q", wi, "k:wi", Q1H, sq16), ("k", wo, "k:wo", K1H, sk16)):
            for dc in range(NCH):
                pm = psM[dc % 2]
                for c in range(NCH):
                    em.op("t", lambda t, wmat=wmat, c=c, dc=dc, pm=pm:
                          t.matmul(pm[:, 0:TG],
                                   wmat[:, 1024 * c + 128 * dc:
                                        1024 * c + 128 * (dc + 1)],
                                   xn1b[:, c * TG:(c + 1) * TG],
                                   start=(c == 0), stop=(c == NCH - 1)),
                          reads=["k:xn1b", wkey],
                          writes=[f"ps:M{dc % 2}"] if c in (0, NCH - 1) else [],
                          inc=1 if c == NCH - 1 else 0)
                sc = 0.125 if which == "q" else 1.0
                em.op("s", lambda s, pm=pm, sc=sc: s.activation(
                    stage[:, 0:TG], pm[:, 0:TG], AF.Copy, bias=0.0, scale=sc),
                    reads=[f"ps:M{dc % 2}"], writes=["k:stage"])
                em.dma("y", DST[2 * dc, :, t0:t0 + TG], stage[0:64, 0:TG],
                       reads=["k:stage"], is_out=True)
                em.dma("y", DST[2 * dc + 1, :, t0:t0 + TG], stage[64:128, 0:TG],
                       reads=["k:stage"], is_out=True)
                em.op("s", lambda s: s.activation(
                    sqs[:, 0:TG], stage[:, 0:TG], AF.Square, bias=0.0,
                    scale=1.0),
                    reads=["k:stage"], writes=["k:sqs"])
                em.op("t", lambda t, dc=dc: t.matmul(
                    psSq[0:16, 0:TG], hind[:, 16 * dc:16 * (dc + 1)],
                    sqs[:, 0:TG], start=(dc == 0), stop=(dc == NCH - 1)),
                    reads=["k:sqs", "k:hind"],
                    writes=["ps:Sq"] if dc in (0, NCH - 1) else [], inc=1)
            em.op("s", lambda s, SQ16=SQ16, t0=t0: s.copy(
                SQ16[0:16, t0:t0 + TG], psSq[0:16, 0:TG]),
                reads=["ps:Sq"], writes=["k:sq16"])
        for tt in range(KTPG):
            for vcg in range(2):
                pm = psM[vcg]
                for c in range(NCH):
                    em.op("t", lambda t, c=c, tt=tt, vcg=vcg, pm=pm: t.matmul(
                        pm[:, 0:512],
                        xn1b[:, c * TG + 128 * tt:c * TG + 128 * (tt + 1)],
                        wv[:, 1024 * c + 512 * vcg:1024 * c + 512 * (vcg + 1)],
                        start=(c == 0), stop=(c == NCH - 1)),
                        reads=["k:xn1b", "k:wv"],
                        writes=[f"ps:M{vcg}"] if c in (0, NCH - 1) else [],
                        inc=1 if c == NCH - 1 else 0)
                for j in range(8):
                    h = 8 * vcg + j
                    em.op("v", lambda v, tt=tt, h=h, j=j, pm=pm: v.tensor_scalar(
                        vsc1g[:, 1056 * tt + 66 * h:1056 * tt + 66 * h + 64],
                        pm[:, 64 * j:64 * (j + 1)],
                        ext[:, 16 * tt + h:16 * tt + h + 1], None, ALU.mult),
                        reads=[f"ps:M{vcg}", "k:ext"], writes=["k:vsc1g"])
                    em.op("s", lambda s, tt=tt, h=h: s.copy(
                        vsc1g[:, 1056 * tt + 66 * h + 64:
                              1056 * tt + 66 * h + 65],
                        ext[:, 16 * tt + h:16 * tt + h + 1]),
                        reads=["k:ext"], writes=["k:vsc1g"])
        em.dma("y", VSC1[:, 1056 * KTPG * g:1056 * KTPG * (g + 1)], vsc1g[:],
               reads=["k:vsc1g"], is_out=True)
    em.op("v", lambda v: v.reduce_max(kmx[:], sk16[0:16, :], axis=AX.X),
          reads=["k:sq16"], writes=["k:kmx"])
    em.op("s", lambda s: s.activation(kmx[:], kmx[:], AF.Sqrt, bias=0.0,
                                      scale=1.0),
          reads=["k:kmx"], writes=["k:kmx"])
    em.op("v", lambda v: v.tensor_scalar(nkm[:], kmx[:], -1.0, None, ALU.mult),
          reads=["k:kmx"], writes=["k:nkm"])
    em.op("s", lambda s: s.activation(srt[0:16, :], sq16[0:16, :], AF.Sqrt,
                                      bias=0.0, scale=1.0),
          reads=["k:sq16"], writes=["k:srt"])
    em.op("v", lambda v: v.tensor_scalar(m1r[0:16, :], srt[0:16, :], nkm[:],
                                         CSH, ALU.mult, ALU.add),
          reads=["k:srt", "k:nkm"], writes=["k:m1r"])
    em.dma("y", M1H[0:16, :], m1r[0:16, :], reads=["k:m1r"], is_out=True)
    return em


def build_phase_f(Tt=T, em=None, io=None, B=None):
    if em is None:
        em = Em("F")
        io = {
            "Q1HD": em.din("Q1H", [NH, DH, Tt], BF16),
            "K1HD": em.din("K1H", [NH, DH, Tt], BF16),
            "VSC1D": em.din("VSC1", [128, NKT * 1056], BF16),
            "M1HD": em.din("M1H", [NH, Tt], BF16),
            "APHD": em.din("APH", [NH, Tt], F32),
            "O1H": em.dout("O1H", [NH, DH, Tt], BF16),
        }
    NQG = Tt // 512
    NKT = Tt // 128

    (Q1HD, K1HD, VSC1D, M1HD, APHD, O1H) = (io[k] for k in ("Q1HD", "K1HD", "VSC1D", "M1HD", "APHD", "O1H",))
    vsc = em.salloc("vsc", [128, NKT * 1056], BF16)
    kb = [em.salloc(f"kb{i}", [65, Tt], BF16) for i in range(2)]
    qb = [em.salloc(f"qb{i}", [65, Tt], BF16) for i in range(2)]
    wb = em.salloc("wb", [128, 512], F32)
    ap0 = em.salloc("ap0", [1, 512], F32)
    onesc = em.sb("onesc", [128, 1], BF16)
    onesr = em.sb("onesr", [1, 64], BF16)
    abuf = em.salloc("abuf", [128, 4 * 512], BF16)
    szb = em.salloc("szb", [128, 512], BF16)
    rr = em.salloc("rr", [1, 512], F32)
    frow = em.salloc("frow", [1, 512], BF16)
    fbc = em.salloc("fbc", [64, 512], BF16)
    obuf = em.salloc("obuf", [64, 512], BF16)
    psS = [em.ps(f"psS{i}", [128, 512]) for i in range(3)]
    psO = [em.ps(f"psO{i}", [128, 512]) for i in range(2)]
    psB = em.ps("psB", [128, 512])
    psD = psB

    em.dma("y", vsc[:], VSC1D, writes=["k:vsc"])
    em.op("v", lambda v: v.memset(onesc[:], 1.0), writes=["k:ones"])
    em.op("v", lambda v: v.memset(onesr[:], 1.0), writes=["k:onesr"])
    em.op("v", lambda v: v.memset(wb[64:66, :], 1e-8), writes=["k:wb"])
    for i in range(2):
        em.op("v", lambda v, i=i: v.memset(kb[i][64:65, :], 1.0),
              writes=[f"k:kb{i}"])

    for h in range(NH):
        par = h % 2
        kbu, qbu = kb[par], qb[par]
        kkey, qkey = f"k:kb{par}", f"k:qb{par}"
        em.dma("y", kbu[0:64, :], K1HD[h], writes=[kkey])
        em.dma("y", qbu[0:64, :], Q1HD[h], writes=[qkey])
        em.dma("y", qbu[64:65, :], M1HD[h:h + 1, :], writes=[qkey])

        for qg in range(NQG):
            LAG = 2
            pso = psO[qg % 2]
            pokey = f"ps:O{qg % 2}"

            def emit_o(kt, h=h, pso=pso, pokey=pokey):
                asl = abuf[:, 512 * (kt % 3):512 * (kt % 3 + 1)]
                em.op("t", lambda t, asl=asl, kt=kt, h=h, pso=pso: t.matmul(
                    pso[0:66, 0:512],
                    vsc[:, 1056 * kt + 66 * h:1056 * kt + 66 * h + 66],
                    asl, start=(kt == 0), stop=(kt == NKT - 1)),
                    reads=[f"k:ab{kt % 3}", "k:vsc"],
                    writes=[pokey] if kt in (0, NKT - 1) else [],
                    inc=1 if kt == NKT - 1 else 0)

            for kt in range(NKT):
                ss = psS[kt % 3]
                em.op("t", lambda t, ss=ss, kt=kt, qg=qg, kbu=kbu, qbu=qbu:
                      t.matmul(ss[:, 0:512],
                               kbu[0:65, 128 * kt:128 * (kt + 1)],
                               qbu[0:65, 512 * qg:512 * (qg + 1)],
                               start=True, stop=True),
                      reads=[kkey, qkey], writes=[f"ps:S{kt % 3}"])
                asl = abuf[:, 512 * (kt % 3):512 * (kt % 3 + 1)]
                em.op("s", lambda s, ss=ss, asl=asl: s.activation(
                    asl, ss[:, 0:512], AF.Exp, bias=0.0, scale=1.0),
                    reads=[f"ps:S{kt % 3}"], writes=[f"k:ab{kt % 3}"])
                if kt >= LAG:
                    emit_o(kt - LAG)
            for kt in range(NKT - LAG, NKT):
                emit_o(kt)
            em.dma("s", wb[64:65, 0:512],
                   APHD[h:h + 1, 512 * qg:512 * (qg + 1)], writes=["k:wb"])
            em.dma("s", ap0[0:1, 0:512],
                   APHD[h:h + 1, 512 * qg:512 * (qg + 1)], writes=["k:ap0"])
            em.op("s", lambda s, pso=pso: s.copy(
                szb[64:66, 0:512], pso[64:66, 0:512]),
                  reads=[pokey], writes=["k:szb"])
            em.op("v", lambda v, qg=qg: v.tensor_tensor(
                szb[64:66, 0:512], szb[64:66, 0:512],
                wb[64:66, 0:512], ALU.mult),
                reads=["k:szb", "k:wb"], writes=["k:szb"])
            em.op("t", lambda t: t.matmul(psB[0:1, 0:512], onesc[64:66, 0:1],
                                          szb[64:66, 0:512],
                                          start=True, stop=True),
                  reads=["k:szb", "k:ones"], writes=["ps:B"])
            em.op("v", lambda v: v.reciprocal(rr[0:1, :], psB[0:1, 0:512]),
                  reads=["ps:B"], writes=["k:rr"])
            em.op("v", lambda v, qg=qg: v.tensor_tensor(
                rr[0:1, :], rr[0:1, :], ap0[0:1, 0:512], ALU.mult),
                reads=["k:rr", "k:ap0"], writes=["k:rr"])
            em.op("s", lambda s: s.copy(frow[0:1, :], rr[0:1, :]),
                  reads=["k:rr"], writes=["k:frow"])
            em.op("t", lambda t: t.matmul(psB[0:64, 0:512], onesr[0:1, 0:64],
                                          frow[0:1, 0:512],
                                          start=True, stop=True),
                  reads=["k:frow", "k:onesr"], writes=["ps:B"])
            em.op("s", lambda s: s.copy(fbc[0:64, :], psB[0:64, 0:512]),
                  reads=["ps:B"], writes=["k:fbc"])
            em.op("v", lambda v, pso=pso: v.tensor_tensor(
                obuf[:], pso[0:64, 0:512], fbc[:], ALU.mult),
                  reads=[pokey, "k:fbc"], writes=["k:obuf"])
            em.dma("y", O1H[h, :, 512 * qg:512 * (qg + 1)], obuf[:],
                   reads=["k:obuf"], is_out=True)
    return em


def build_phase_g(TG=256, Tt=T, em=None, io=None, B=None):
    if em is None:
        em = Em("G")
        io = {
            "O1HD": em.din("O1H", [NH, DH, Tt], BF16),
            "WOUT1D": em.din("WOUT1", [64, 16 * 1024], F32),
            "BETA_P": em.din("BETA_P", [1, 16], F32),
            "IV_P": em.din("IV_P", [1, 16], F32),
            "OV_P": em.din("OV_P", [1, 16], F32),
            "CHI": em.din("CHI", [1, 16], F32),
            "SIGN64": em.din("SIGN64", [64, 1], F32),
            "X1RD": em.din("X1R", [D, Tt], F32),
            "XMID1": em.dout("XMID1", [D, Tt], F32),
        }
    NG = Tt // TG

    (O1HD, WOUT1D, BETA_P, IV_P, OV_P, CHI, SIGN64, X1RD, XMID1) = (io[k] for k in ("O1HD", "WOUT1D", "BETA_P", "IV_P", "OV_P", "CHI", "SIGN64", "X1RD", "XMID1",))
    wo1 = em.salloc("wo1", [64, 1024], F32)
    w1o = em.salloc("w1o", [64, 16 * 1024], BF16)
    sgn = em.sb("sgn", [64, 1], F32)
    rows = em.sb("rows", [1, 16 * 6], F32)
    onesf = em.sb("onesf", [1, 128], F32)
    acol = em.sb("acol", [64, 16], F32)
    bcol = em.sb("bcol", [64, 16], F32)
    rt = em.salloc("rt", [64, 1024], F32)
    t1g = em.salloc("t1g", [64, 1024], F32)
    ohg = em.salloc("ohg", [64, 16 * TG], BF16)
    xg = em.salloc("xg", [128, TG], F32)
    og = em.salloc("og", [128, NCH * TG], F32)
    psA = [em.ps(f"psA{i}", [128, 512]) for i in range(2)]
    psBc = em.ps("psBc", [128, 512])

    em.dma("y", sgn[:], SIGN64, writes=["k:sgn"])
    em.dma("y", rows[0:1, 0:16], BETA_P, writes=["k:rows"])
    em.dma("y", rows[0:1, 16:32], IV_P, writes=["k:rows2"])
    em.dma("y", rows[0:1, 32:48], OV_P, writes=["k:rows3"])
    em.dma("y", rows[0:1, 48:64], CHI, writes=["k:rows4"])
    em.op("v", lambda v: v.memset(onesf[:], 1.0), writes=["k:onesf"])
    RK = ["k:rows", "k:rows2", "k:rows3", "k:rows4"]
    em.op("s", lambda s: s.activation(rows[0:1, 0:48], rows[0:1, 0:48],
                                      AF.Sigmoid, bias=0.0, scale=1.0),
          reads=RK, writes=RK)
    em.op("s", lambda s: s.activation(rows[0:1, 48:64], rows[0:1, 48:64],
                                      AF.Tanh, bias=0.0, scale=1.0),
          reads=RK, writes=RK)
    import math
    em.op("v", lambda v: v.tensor_scalar(rows[0:1, 64:80], rows[0:1, 0:16],
                                         -math.pi, math.pi / 2.0,
                                         ALU.mult, ALU.add),
          reads=RK, writes=["k:rows5"])
    em.op("s", lambda s: s.activation(rows[0:1, 64:80], rows[0:1, 64:80],
                                      AF.Sin, bias=0.0, scale=1.0),
          reads=["k:rows5"], writes=["k:rows5"])
    em.op("v", lambda v: v.tensor_scalar(rows[0:1, 80:96], rows[0:1, 0:16],
                                         math.pi, None, ALU.mult),
          reads=RK, writes=["k:rows6"])
    em.op("s", lambda s: s.activation(rows[0:1, 80:96], rows[0:1, 80:96],
                                      AF.Sin, bias=0.0, scale=1.0),
          reads=["k:rows6"], writes=["k:rows6"])
    for dst, key in ((64, "k:rows5"), (80, "k:rows6")):
        for off in (16, 32, 48):
            em.op("v", lambda v, dst=dst, off=off: v.tensor_tensor(
                rows[0:1, dst:dst + 16], rows[0:1, dst:dst + 16],
                rows[0:1, off:off + 16], ALU.mult),
                reads=RK + [key], writes=[key])
    em.op("t", lambda t: t.matmul(psBc[0:64, 0:16], onesf[0:1, 0:64],
                                  rows[0:1, 64:80], start=True, stop=True),
          reads=["k:rows5", "k:onesf"], writes=["ps:Bc"])
    em.op("s", lambda s: s.copy(acol[:], psBc[0:64, 0:16]),
          reads=["ps:Bc"], writes=["k:acol"])
    em.op("t", lambda t: t.matmul(psBc[0:64, 0:16], onesf[0:1, 0:64],
                                  rows[0:1, 80:96], start=True, stop=True),
          reads=["k:rows6", "k:onesf"], writes=["ps:Bc"])
    em.op("s", lambda s: s.copy(bcol[:], psBc[0:64, 0:16]),
          reads=["ps:Bc"], writes=["k:bcol"])
    em.op("v", lambda v: v.tensor_scalar(bcol[:], bcol[:], sgn[:], None,
                                         ALU.mult),
          reads=["k:bcol", "k:sgn"], writes=["k:bcol"])
    for h in range(NH):
        em.dma("y", wo1[:], WOUT1D[:, 1024 * h:1024 * (h + 1)],
               writes=["k:wo1"])
        em.op("s", lambda s: s.copy(rt[0:32, 0:1024], wo1[32:64, 0:1024]),
              reads=["k:wo1"], writes=["k:rt"])
        em.op("s", lambda s: s.copy(rt[32:64, 0:1024], wo1[0:32, 0:1024]),
              reads=["k:wo1"], writes=["k:rt"])
        em.op("v", lambda v, h=h: v.tensor_scalar(
            t1g[:, 0:1024], wo1[0:64, 0:1024], acol[:, h:h + 1], None,
            ALU.mult),
            reads=["k:wo1", "k:acol"], writes=["k:t1g"])
        em.op("v", lambda v, h=h: v.scalar_tensor_tensor(
            w1o[0:64, 1024 * h:1024 * (h + 1)], rt[0:64, 0:1024],
            bcol[:, h:h + 1], t1g[:, 0:1024], ALU.mult, ALU.add),
            reads=["k:rt", "k:bcol", "k:t1g"], writes=["k:w1o2"])
    for g in range(NG):
        t0 = g * TG
        for h in range(NH):
            em.dma("y", ohg[:, TG * h:TG * (h + 1)], O1HD[h, :, t0:t0 + TG],
                   writes=["k:ohg"])
        for dc in range(NCH):
            pa = psA[dc % 2]
            for h in range(NH):
                em.op("t", lambda t, h=h, dc=dc, pa=pa: t.matmul(
                    pa[0:128, 0:TG],
                    w1o[0:64, 1024 * h + 128 * dc:1024 * h + 128 * (dc + 1)],
                    ohg[0:64, TG * h:TG * (h + 1)],
                    start=(h == 0), stop=(h == NH - 1)),
                    reads=["k:ohg", "k:w1o2"],
                    writes=[f"ps:A{dc % 2}"] if h in (0, NH - 1) else [],
                    inc=1 if h == NH - 1 else 0)
            em.dma("y", xg[:, 0:TG], fm_ap(X1RD, dc, 1, t0, TG),
                   writes=["k:xg"])
            em.op("v", lambda v, dc=dc, pa=pa: v.tensor_tensor(
                og[:, dc * TG:(dc + 1) * TG], pa[0:128, 0:TG],
                xg[:, 0:TG], ALU.add),
                reads=[f"ps:A{dc % 2}", "k:xg"], writes=["k:og"])
        em.dma("y", fm_ap(XMID1, 0, NCH, t0, TG), og[:], reads=["k:og"],
               is_out=True)
    return em


def build_phase_h1a(TG=256, Tt=T, em=None, io=None, B=None):
    if em is None:
        em = Em("H1a")
        io = {"XN1B": em.din("XN1B", [D, Tt], BF16),
              "FW1T": em.din("FW1T", [D, 4096], F32),
              "FW1S": em.din("FW1S", [128, 32], F32),
              "G1S": em.dout("G1S", [4096, Tt], BF16)}
    NG = Tt // TG
    XN1BD, FW1T, FW1S, G1S = io["XN1B"], io["FW1T"], io["FW1S"], io["G1S"]
    w1 = em.salloc("w1", [128, NCH * 4096], BF16)
    s1 = em.sb("s1", [128, 32], F32)
    xnbg = em.salloc("xnbg", [128, NCH * TG], BF16)
    g1 = em.salloc("g1", [128, TG], BF16)
    psH = [em.ps(f"psH{i}", [128, 512]) for i in range(4)]
    for c in range(NCH):
        em.dma("g", w1[:, 4096 * c:4096 * (c + 1)],
               FW1T[128 * c:128 * (c + 1), :], writes=[f"k:w1c{c}"])
    em.dma("y", s1[:], FW1S, writes=["k:par"])
    for g in range(NG):
        t0 = g * TG
        em.dma("y", xnbg[:], fm_ap(XN1BD, 0, NCH, t0, TG), writes=["k:xnbg"])
        for f in range(32):
            pw = psH[f % 4]
            for c in range(NCH):
                em.op("t", lambda t, c=c, f=f, pw=pw: t.matmul(
                    pw[:, 0:TG],
                    w1[:, 4096 * c + 128 * f:4096 * c + 128 * (f + 1)],
                    xnbg[:, c * TG:(c + 1) * TG],
                    start=(c == 0), stop=(c == NCH - 1)),
                    reads=["k:xnbg", f"k:w1c{c}"],
                    writes=[f"ps:H{f % 4}"] if c in (0, NCH - 1) else [],
                    inc=1 if c == NCH - 1 else 0)
            em.op("s", lambda s, f=f, pw=pw: s.activation(
                g1[:, 0:TG], pw[:, 0:TG], AF.Silu, bias=0.0,
                scale=s1[:, f:f + 1]),
                reads=[f"ps:H{f % 4}", "k:par"], writes=["k:g1"])
            em.dma("y", G1S[128 * f:128 * (f + 1), t0:t0 + TG], g1[:, 0:TG],
                   reads=["k:g1"], is_out=True)
    return em


def build_phase_h1b(TG=256, Tt=T, em=None, io=None, B=None):
    if em is None:
        em = Em("H1b")
        io = {"XN1B": em.din("XN1B", [D, Tt], BF16),
              "G1S": em.din("G1S", [4096, Tt], BF16),
              "FW2T": em.din("FW2T", [D, 4096], F32),
              "FW2S": em.din("FW2S", [128, 32], F32),
              "H1S": em.dout("H1S", [4096, Tt], BF16)}
    NG = Tt // TG
    XN1BD, G1SD, FW2T, FW2S, H1S = (io[k] for k in (
        "XN1B", "G1S", "FW2T", "FW2S", "H1S"))
    w2 = em.salloc("w2", [128, NCH * 4096], BF16)
    s2 = em.sb("s2", [128, 32], F32)
    xnbg = em.salloc("xnbg", [128, NCH * TG], BF16)
    g1g = em.salloc("g1g", [128, 32 * TG], BF16)
    g2 = em.salloc("g2", [128, TG], F32)
    hbuf = em.salloc("hbuf", [128, TG], BF16)
    psH = [em.ps(f"psH{i}", [128, 512]) for i in range(2)]
    for c in range(NCH):
        em.dma("g", w2[:, 4096 * c:4096 * (c + 1)],
               FW2T[128 * c:128 * (c + 1), :], writes=[f"k:w2c{c}"])
    em.dma("y", s2[:], FW2S, writes=["k:par"])
    for g in range(NG):
        t0 = g * TG
        em.dma("y", xnbg[:], fm_ap(XN1BD, 0, NCH, t0, TG), writes=["k:xnbg"])
        em.dma("y", g1g[:], fm_ap(G1SD, 0, 32, t0, TG), writes=["k:g1g"])
        for f in range(32):
            pw = psH[f % 2]
            for c in range(NCH):
                em.op("t", lambda t, c=c, f=f, pw=pw: t.matmul(
                    pw[:, 0:TG],
                    w2[:, 4096 * c + 128 * f:4096 * c + 128 * (f + 1)],
                    xnbg[:, c * TG:(c + 1) * TG],
                    start=(c == 0), stop=(c == NCH - 1)),
                    reads=["k:xnbg", f"k:w2c{c}"],
                    writes=[f"ps:H{f % 2}"] if c in (0, NCH - 1) else [],
                    inc=1 if c == NCH - 1 else 0)
            em.op("v", lambda v, f=f, pw=pw: v.tensor_scalar(
                g2[:, 0:TG], pw[:, 0:TG], s2[:, f:f + 1], None, ALU.mult),
                reads=[f"ps:H{f % 2}", "k:par"], writes=["k:g2"])
            em.op("v", lambda v, f=f: v.tensor_tensor(
                hbuf[:, 0:TG], g1g[:, f * TG:(f + 1) * TG], g2[:, 0:TG],
                ALU.mult),
                reads=["k:g1g", "k:g2"], writes=["k:hbuf"])
            em.dma("y", H1S[128 * f:128 * (f + 1), t0:t0 + TG], hbuf[:, 0:TG],
                   reads=["k:hbuf"], is_out=True)
    return em

def build_phase_h2(name, TG=256, Tt=T, em=None, io=None, B=None):
    _mk = em is None
    if _mk:
        em = Em(name)
    NG = Tt // TG
    if _mk:
        io = {"HIN": em.din("HIN", [4096, Tt], BF16),
              "WT": em.din("WT", [4096, 4096], F32),
              "SC": em.din("SC", [128, 32], F32),
              "HOUT": em.dout("HOUT", [4096, Tt], BF16)}
    HIN, WT, SC, HOUT = io["HIN"], io["WT"], io["SC"], io["HOUT"]
    wblk = [em.salloc(f"wblk{i}", [128, 32 * 512], BF16) for i in range(2)]
    sc = em.sb("sc", [128, 32], F32)
    hing = [em.salloc(f"hing{i}", [128, 32 * TG], BF16) for i in range(2)]
    hbuf = em.salloc("hbuf", [128, TG], BF16)
    psH = [em.ps(f"psH{i}", [128, 512]) for i in range(4)]
    em.dma("y", sc[:], SC, writes=["k:par"])
    for ob in range(8):
        wb_ = wblk[ob % 2]
        wkey = f"k:wb{ob % 2}"
        for c in range(32):
            em.dma("g", wb_[:, 512 * c:512 * (c + 1)],
                   WT[128 * c:128 * (c + 1), 512 * ob:512 * (ob + 1)],
                   writes=[wkey])
        for g in range(NG):
            t0 = g * TG
            hg = hing[g % 2]
            hkey = f"k:hg{g % 2}"
            em.dma("y", hg[:], fm_ap(HIN, 0, 32, t0, TG), writes=[hkey])
            for sub in range(4):
                pp = psH[sub % 4]
                for c in range(32):
                    em.op("t", lambda t, c=c, sub=sub, wb_=wb_, hg=hg, pp=pp:
                          t.matmul(pp[:, 0:TG],
                                   wb_[:, 512 * c + 128 * sub:
                                       512 * c + 128 * (sub + 1)],
                                   hg[:, c * TG:(c + 1) * TG],
                                   start=(c == 0), stop=(c == 31)),
                          reads=[hkey, wkey],
                          writes=[f"ps:H{sub % 4}"] if c in (0, 31) else [],
                          inc=1 if c == 31 else 0)
                fo = 4 * ob + sub
                em.op("s", lambda s, pp=pp, fo=fo, sub=sub: s.activation(
                    hbuf[:, 0:TG], pp[:, 0:TG], AF.Gelu, bias=0.0,
                    scale=sc[:, fo:fo + 1]),
                    reads=[f"ps:H{sub % 4}", "k:par"], writes=["k:hbuf"])
                em.dma("y", HOUT[512 * ob + 128 * sub:512 * ob + 128 * (sub + 1),
                                 t0:t0 + TG], hbuf[:, 0:TG],
                       reads=["k:hbuf"], is_out=True)
    return em


def build_phase_h4(TG=256, Tt=T, em=None, io=None, B=None):
    if em is None:
        em = Em("H4")
        io = {
            "H3S": em.din("H3S", [4096, Tt], BF16),
            "FW3T": em.din("FW3T", [4096, D], F32),
            "FW3S": em.din("FW3S", [128, 8], F32),
            "XMID1D": em.din("XMID1", [D, Tt], F32),
            "YF": em.dout("YF", [D, Tt], F32),
        }
    NG = Tt // TG
    (H3S, FW3T, FW3S, XMID1D, YF) = (io[k] for k in ("H3S", "FW3T", "FW3S", "XMID1D", "YF",))
    w3 = em.salloc("w3", [128, 32 * 1024], BF16)
    s3 = em.sb("s3", [128, 8], F32)
    hg = em.salloc("hg", [128, 32 * TG], BF16)
    xmg = em.salloc("xmg", [128, TG], F32)
    og = em.salloc("og", [128, NCH * TG], F32)
    psH = [em.ps(f"psH{i}", [128, 512]) for i in range(4)]
    for f in range(32):
        em.dma("g", w3[:, 1024 * f:1024 * (f + 1)],
               FW3T[128 * f:128 * (f + 1), :], writes=[f"k:w3f{f}"])
    em.dma("y", s3[:], FW3S, writes=["k:par"])
    for g in range(NG):
        t0 = g * TG
        em.dma("y", hg[:], fm_ap(H3S, 0, 32, t0, TG), writes=["k:hg"])
        for dc in range(NCH):
            pp = psH[dc % 4]
            for f in range(32):
                em.op("t", lambda t, f=f, dc=dc, pp=pp: t.matmul(
                    pp[:, 0:TG],
                    w3[:, 1024 * f + 128 * dc:1024 * f + 128 * (dc + 1)],
                    hg[:, f * TG:(f + 1) * TG],
                    start=(f == 0), stop=(f == 31)),
                    reads=["k:hg", f"k:w3f{f}"],
                    writes=[f"ps:H{dc % 4}"] if f in (0, 31) else [],
                    inc=1 if f == 31 else 0)
            em.dma("y", xmg[:, 0:TG], fm_ap(XMID1D, dc, 1, t0, TG),
                   writes=["k:xmg"])
            em.op("v", lambda v, dc=dc, pp=pp: v.scalar_tensor_tensor(
                og[:, dc * TG:(dc + 1) * TG], pp[:, 0:TG], s3[:, dc:dc + 1],
                xmg[:, 0:TG], ALU.mult, ALU.add),
                reads=[f"ps:H{dc % 4}", "k:par", "k:xmg"], writes=["k:og"])
        em.dma("y", fm_ap(YF, 0, NCH, t0, TG), og[:], reads=["k:og"],
               is_out=True)
    return em

SHARED_INPUTS = [
    ("WINT", [D, QKVP], F32), ("LNG_IN", [128, 8], F32),
    ("LNB_IN", [128, 8], F32), ("MIDG", [128, 56], F32),
    ("MIDB", [128, 56], F32), ("SHIFT1", [128, 128], BF16),
    ("OH127", [128, 128], BF16), ("IDENT", [128, 128], BF16),
    ("SLOPES", [1, 16], F32), ("IOTA128", [128, 1], F32),
    ("IOTAT", [1, T], F32), ("PREMASK", [512, 512], BF16),
    ("PREMASKT", [512, 512], BF16), ("WOT_O", [64, 16 * 1024], F32),
    ("WOT_PG", [4096, 1024], F32), ("OUTG", [128, 8], F32),
    ("OUTB", [128, 8], F32), ("W1", [D, 4096], F32), ("B1", [128, 32], F32),
    ("W2", [4096, D], F32), ("B2", [128, 8], F32), ("GAM", [128, 8], F32),
    ("BET", [128, 8], F32), ("WI", [D, D], F32), ("WO", [D, D], F32),
    ("WV", [D, D], F32), ("WA", [D, 16], F32), ("WE", [D, 16], F32),
    ("BA", [16, 1], F32), ("BE", [16, 1], F32), ("HEADIND", [128, 128], F32),
    ("WOUT1", [64, 16 * 1024], F32), ("BETA_P", [1, 16], F32),
    ("IV_P", [1, 16], F32), ("OV_P", [1, 16], F32), ("CHI", [1, 16], F32),
    ("SIGN64", [64, 1], F32), ("FW1T", [D, 4096], F32),
    ("FW1S", [128, 32], F32), ("FW2T", [D, 4096], F32),
    ("FW2S", [128, 32], F32), ("FH1T", [4096, 4096], F32),
    ("FH1S", [128, 32], F32), ("FH2T", [4096, 4096], F32),
    ("FH2S", [128, 32], F32), ("FW3T", [4096, D], F32),
    ("FW3S", [128, 8], F32),
]

SCRATCH = [
    ("XNF", [D, T], F32), ("XNB", [D, T], BF16),
    ("QH", [NH, DH, T], BF16), ("KH", [NH, DH, T], BF16),
    ("VSC", [128, (T // 128) * 1056], BF16), ("PG", [4096, T], BF16),
    ("MH", [NH, T], BF16), ("OH", [NH, DH, T], BF16),
    ("XMID", [D, T], F32), ("HF", [4096, T], BF16), ("X1R", [D, T], F32),
    ("XN1F", [D, T], F32), ("XN1B", [D, T], BF16),
    ("Q1H", [NH, DH, T], BF16), ("K1H", [NH, DH, T], BF16),
    ("VSC1", [128, (T // 128) * 1056], BF16), ("M1H", [NH, T], BF16),
    ("APH", [NH, T], F32), ("O1H", [NH, DH, T], BF16),
    ("XMID1", [D, T], F32), ("G1S", [4096, T], BF16),
    ("H1S", [4096, T], BF16), ("H2S", [4096, T], BF16),
    ("H3S", [4096, T], BF16),
]


def build_fused():
    em = Em("FUSED")
    em.mk_arenas()
    io = {"XF": em.din("XF", [D, T], F32)}
    for nm, shp, dt in SHARED_INPUTS:
        io[nm] = em.din(nm, shp, dt)
    for nm, shp, dt in SCRATCH:
        io[nm] = em.nc.dram_tensor(nm, shp, dt, kind="Internal").ap()
    io["YF"] = em.dout("YF", [D, T], F32)
    Bsh = mk_ln_bufs(em, 512, 56)

    def ph(pfx, base, fn, **kw):
        em.arena_reset(pfx, psum_base=base)
        fn(em=em, io=dict(io, **kw.pop("io_map", {})), **kw)
        em.barrier()

    ph("A:", 3, build_phase_a, B=Bsh)
    ph("B:", 0, build_phase_b,
       io_map=dict(QHD=io["QH"], KHD=io["KH"], VSCD=io["VSC"]))
    ph("C:", 3, build_phase_c, TG=512, B=Bsh,
       io_map=dict(OHD=io["OH"], PGD=io["PG"], XFD=io["XF"]))
    ph("D1:", 0, build_phase_d1, TG=512)
    ph("D2:", 3, build_phase_d2, TG=512, B=Bsh)
    ph("E:", 3, build_phase_e, B=Bsh,
       io_map=dict(X1RD=io["X1R"], WID=io["WI"], WOD=io["WO"], WVD=io["WV"],
                   WAD=io["WA"], WED=io["WE"], BAD=io["BA"], BED=io["BE"]))
    ph("F:", 0, build_phase_f,
       io_map=dict(Q1HD=io["Q1H"], K1HD=io["K1H"], VSC1D=io["VSC1"],
                   M1HD=io["M1H"], APHD=io["APH"]))
    ph("G:", 0, build_phase_g,
       io_map=dict(O1HD=io["O1H"], WOUT1D=io["WOUT1"], X1RD=io["X1R"]))
    ph("H1a:", 0, build_phase_h1a, TG=512)
    ph("H1b:", 0, build_phase_h1b, TG=512)
    em.arena_reset("H2:", 0)
    build_phase_h2("H2x", TG=512, em=em, io=dict(HIN=io["H1S"], WT=io["FH1T"],
                                                 SC=io["FH1S"], HOUT=io["H2S"]))
    em.barrier()
    em.arena_reset("H3:", 0)
    build_phase_h2("H3x", TG=512, em=em, io=dict(HIN=io["H2S"], WT=io["FH2T"],
                                                 SC=io["FH2S"], HOUT=io["H3S"]))
    em.barrier()
    em.arena_reset("H4:", 0)
    build_phase_h4(TG=512, em=em, io=dict(io, XMID1D=io["XMID1"]))
    return em


def kernel(**inputs):
    H = host_prep(inputs)
    NC = 8
    em = build_fused()
    nc = em.finalize()
    shared = {nm: H[nm] for nm, _, _ in SHARED_INPUTS}
    maps = [dict(shared, XF=H["XF"][b]) for b in range(NC)]
    res = run_bass_kernel_spmd(nc, maps, core_ids=list(range(NC)))
    return np.stack([np.ascontiguousarray(np.asarray(res.results[b]["YF"]).T)
                     for b in range(NC)])


# version 21 (final)
# speedup vs baseline: 1.1043x; 1.0324x over previous
import numpy as np
import ml_dtypes
from contextlib import ExitStack
import concourse.bass as bass
import concourse.mybir as mybir
from concourse.alu_op_type import AluOpType as ALU
from concourse.bass_utils import run_bass_kernel_spmd

F32 = mybir.dt.float32
BF16 = mybir.dt.bfloat16
AF = mybir.ActivationFunctionType
AX = mybir.AxisListType

T = 4096
D = 1024
NH, DH = 16, 64
QKVP = 7168
NCH = 8
CSH = 40.0
bf16 = ml_dtypes.bfloat16

ENGS = ("t", "s", "v", "g", "y")
DMA_QUEUES = ("y", "g", "s")
NSLOT = 8


class _ArView:

    def __init__(self, base, off, p, n):
        self.base, self.off, self.p, self.n = base, off, p, n

    def __getitem__(self, idx):
        if idx == slice(None):
            return self.base[0:self.p, self.off:self.off + self.n]
        if isinstance(idx, tuple):
            ps, cs = idx
        else:
            ps, cs = idx, slice(None)
        p0 = ps.start or 0
        p1 = self.p if ps.stop is None else ps.stop
        c0 = cs.start or 0
        c1 = self.n if cs.stop is None else cs.stop
        if cs.step in (None, 1):
            return self.base[p0:p1, self.off + c0:self.off + c1]
        return self.base[p0:p1, self.off + c0:self.off + c1:cs.step]


class Em:

    def __init__(self, name):
        self.name = name
        self.nc = bass.Bass("TRN2", target_bir_lowering=False)
        self.st = ExitStack()
        self.sems = {}
        self.cnt = {}
        self.ops = {e: [] for e in ENGS}
        self.writer = {}
        self.readers = {}
        self.pending = {}
        self.pending_keys = set()
        self.dmaslot = {q: 0 for q in DMA_QUEUES}
        self.waited = {}
        self.out_tokens = []
        self.pfx = ""
        self.arenas = {}
        self.shared_psum = False
        self.barrier_junk = None

    def sem(self, name):
        if name not in self.sems:
            self.sems[name] = self.st.enter_context(self.nc.semaphore(name))
            self.cnt[name] = 0
        return self.sems[name]

    def sb(self, name, shape, dtype):
        return self.st.enter_context(self.nc.sbuf_tensor(
            self.pfx + name, list(shape), dtype))

    def ps(self, name, shape, dtype=F32):
        if not self.shared_psum:
            return self.st.enter_context(self.nc.psum_tensor(
                self.pfx + name, list(shape), dtype))
        if dtype == BF16:
            return self.psum_banks["bf"]
        i = self.psum_next
        self.psum_next += 1
        assert i < 7, "out of shared f32 psum banks"
        return self.psum_banks[i]

    def arena_reset(self, pfx, psum_base=0):
        self.pfx = pfx
        self.arena_off = {k: 0 for k in self.arenas}
        self.psum_next = psum_base

    def mk_arenas(self):
        self.arenas = {
            "A1": self.st.enter_context(
                self.nc.sbuf_tensor("AR1", [128, 50176], BF16)),
            "A2": self.st.enter_context(
                self.nc.sbuf_tensor("AR2", [128, 29568], BF16)),
            "AF": self.st.enter_context(
                self.nc.sbuf_tensor("ARF", [128, 5632], F32)),
        }
        self.arena_cols = {"A1": 50176, "A2": 29568, "AF": 5632}
        self.psum_banks = {}
        for i in range(7):
            self.psum_banks[i] = self.st.enter_context(
                self.nc.psum_tensor(f"PSB{i}", [128, 512], F32))
        self.psum_banks["bf"] = self.st.enter_context(
            self.nc.psum_tensor("PSBbf", [128, 1024], BF16))
        self.shared_psum = True
        self.arena_off = {k: 0 for k in self.arenas}
        self.psum_next = 0

    def salloc(self, name, shape, dtype):
        if not self.arenas:
            return self.sb(name, shape, dtype)
        p, n = shape
        want = "AF" if dtype == F32 else None
        if want is None:
            a1left = self.arena_cols["A1"] - self.arena_off["A1"]
            want = "A1" if a1left >= n else "A2"
        off = self.arena_off[want]
        assert off + n <= self.arena_cols[want], \
            f"arena {want} overflow at {self.pfx}{name} (+{n} at {off})"
        self.arena_off[want] = off + n
        return _ArView(self.arenas[want], off, p, n)

    def barrier(self):
        snap = dict(self.cnt)
        for e in ("t", "s", "v", "g", "y"):
            tag = ("E" + e) if e != "y" else "Qy"
            deps = {s: c for s, c in snap.items()
                    if c > 0 and s != ("E" + e if e != "y" else None)}

            def run(h, deps=deps, tag=tag):
                for s, c in sorted(deps.items()):
                    if self.waited.get((tag, s), 0) < c:
                        h.wait_ge(self.sems[s], c)
                        self.waited[(tag, s)] = c
            self.ops[e].append(run)
        self.writer = {}
        self.readers = {}

    def din(self, name, shape, dtype):
        return self.nc.dram_tensor(name, list(shape), dtype, kind="ExternalInput").ap()

    def dout(self, name, shape, dtype):
        return self.nc.dram_tensor(name, list(shape), dtype, kind="ExternalOutput").ap()

    def _dep_sems(self, reads, writes):
        deps = {}

        def add(sc):
            if sc and deps.get(sc[0], 0) < sc[1]:
                deps[sc[0]] = sc[1]
        for k in reads:
            assert k not in self.pending_keys, f"read of pending key {k}"
            add(self.writer.get(k))
            if k.startswith("ps:"):
                for s, c in self.readers.get(k, {}).items():
                    add((s, c))
        for k in writes:
            add(self.writer.get(k))
            for s, c in self.readers.get(k, {}).items():
                add((s, c))
        return deps

    def _commit(self, eng_sem, reads, writes, inc):
        if inc == 0:
            self.pending.setdefault(eng_sem, []).append((reads, writes))
            self.pending_keys.update(writes)
            return
        c = self.cnt[eng_sem]
        pend = self.pending.pop(eng_sem, [])
        for r, w in pend + [(reads, writes)]:
            for k in r:
                self.readers.setdefault(k, {})[eng_sem] = c
            for k in w:
                self.writer[k] = (eng_sem, c)
                self.readers[k] = {}
                self.pending_keys.discard(k)

    def op(self, eng, fn, reads=(), writes=(), inc=1):
        assert eng in ("t", "s", "v", "g")
        reads, writes = [str(k) for k in reads], [str(k) for k in writes]
        deps = self._dep_sems(reads, writes)
        sem_name = "E" + eng
        self.sem(sem_name)
        self.cnt[sem_name] += inc
        my = self.cnt[sem_name]
        sems = self.sems
        waited = self.waited

        def run(h, deps=deps, fn=fn, inc=inc, sem_name=sem_name):
            for s, c in sorted(deps.items()):
                if waited.get((sem_name, s), 0) < c:
                    h.wait_ge(sems[s], c)
                    waited[(sem_name, s)] = c
            instr = fn(h)
            if inc:
                instr.then_inc(sems[sem_name], inc)
        self.ops[eng].append(run)
        self._commit(sem_name, reads, writes, inc)
        return (sem_name, my)

    def dma(self, queue, out, in_, reads=(), writes=(), is_out=False):
        assert queue in DMA_QUEUES
        reads, writes = [str(k) for k in reads], [str(k) for k in writes]
        slot = self.dmaslot[queue]
        self.dmaslot[queue] = (slot + 1) % NSLOT
        sem_name = f"D{queue}{slot}"
        self.sem(sem_name)
        deps = self._dep_sems(reads, writes)
        if self.cnt[sem_name] > 0:
            deps[sem_name] = max(deps.get(sem_name, 0), self.cnt[sem_name])
        self.cnt[sem_name] += 16
        my = self.cnt[sem_name]
        eng_tag = "Q" + queue
        sems = self.sems
        waited = self.waited

        def run(h, deps=deps, out=out, in_=in_, sem_name=sem_name, my=my):
            for s, c in sorted(deps.items()):
                if waited.get((eng_tag, s), 0) < c:
                    h.wait_ge(sems[s], c)
                    waited[(eng_tag, s)] = c
            h.dma_start(out=out, in_=in_).then_inc(sems[sem_name], 16)
        self.ops[queue].append(run)
        self._commit(sem_name, reads, writes, 16)
        tok = (sem_name, my)
        if is_out:
            self.out_tokens.append(tok)
        return tok

    def finalize(self):
        assert not self.pending_keys, f"pending at finalize: {self.pending_keys}"
        block = self.st.enter_context(self.nc.Block())
        handles = {"t": block.tensor, "s": block.scalar, "v": block.vector,
                   "g": block.gpsimd, "y": block.sync}
        toks = list(self.out_tokens)
        for e in ENGS:
            oplist = self.ops[e]
            if not oplist and e != "y":
                continue

            def section(h, oplist=oplist, e=e):
                for run in oplist:
                    run(h)
                if e == "y":
                    for s, c in toks:
                        h.wait_ge(self.sems[s], c)
            handles[e](section)
        self.st.close()
        return self.nc


def fm_ap(X, c0, nch, t0, nt):
    return X[128 * c0:128 * (c0 + nch), t0:t0 + nt].rearrange(
        "(c p) t -> p c t", p=128)


def host_prep(inputs):
    f32 = np.float32
    d = {}
    _ = inputs["L0_ffn_router_w"], inputs["L0_ffn_router_b"]
    x = np.asarray(inputs["L0_attn_x"], f32)
    d["XF"] = [np.ascontiguousarray(x[b].T) for b in range(8)]

    def pack(v, n):
        return np.ascontiguousarray(np.asarray(v, f32).reshape(n, 128).T)

    d["WINT"] = np.ascontiguousarray(np.asarray(inputs["L0_attn_W_in"], f32).T)
    d["LNG_IN"] = pack(inputs["L0_attn_in_ln_g"], 8)
    d["LNB_IN"] = pack(inputs["L0_attn_in_ln_b"], 8)
    d["MIDG"] = pack(inputs["L0_attn_mid_ln_g"], 56)
    d["MIDB"] = pack(inputs["L0_attn_mid_ln_b"], 56)
    d["SLOPES"] = np.asarray(inputs["L0_attn_slopes"], f32).reshape(1, 16)
    WoT = np.ascontiguousarray(np.asarray(inputs["L0_attn_W_out"], f32).T)
    d["WOT_O"] = np.ascontiguousarray(
        WoT[:1024].reshape(16, 64, 1024).transpose(1, 0, 2).reshape(64, 16 * 1024))
    d["WOT_PG"] = np.ascontiguousarray(WoT[1024:])
    d["OUTG"] = pack(inputs["L0_attn_out_ln_g"], 8)
    d["OUTB"] = pack(inputs["L0_attn_out_ln_b"], 8)
    d["W1"] = np.ascontiguousarray(np.asarray(inputs["L0_ffn_w1"], f32)[0])
    d["B1"] = pack(np.asarray(inputs["L0_ffn_b1"], f32)[0], 32)
    d["W2"] = np.ascontiguousarray(np.asarray(inputs["L0_ffn_w2"], f32)[0])
    d["B2"] = pack(np.asarray(inputs["L0_ffn_b2"], f32)[0], 8)
    d["GAM"] = pack(inputs["L0_ffn_gamma"], 8)
    d["BET"] = pack(inputs["L0_ffn_beta"], 8)
    d["WA"] = np.ascontiguousarray(np.asarray(inputs["L1_attn_Wa"], f32))
    d["BA"] = np.asarray(inputs["L1_attn_ba"], f32).reshape(16, 1)
    d["WE"] = np.ascontiguousarray(np.asarray(inputs["L1_attn_We"], f32))
    d["BE"] = np.asarray(inputs["L1_attn_be"], f32).reshape(16, 1)
    d["WI"] = np.ascontiguousarray(np.asarray(inputs["L1_attn_Wi"], f32))
    d["WO"] = np.ascontiguousarray(np.asarray(inputs["L1_attn_Wo"], f32))
    d["WV"] = np.ascontiguousarray(np.asarray(inputs["L1_attn_Wv"], f32))
    d["WOUT1"] = np.ascontiguousarray(
        np.asarray(inputs["L1_attn_Wout"], f32).reshape(16, 64, 1024)
        .transpose(1, 0, 2).reshape(64, 16 * 1024))
    d["CHI"] = np.asarray(inputs["L1_attn_chi"], f32).reshape(1, 16)
    d["BETA_P"] = np.asarray(inputs["L1_attn_beta_p"], f32).reshape(1, 16)
    d["IV_P"] = np.asarray(inputs["L1_attn_iv_p"], f32).reshape(1, 16)
    d["OV_P"] = np.asarray(inputs["L1_attn_ov_p"], f32).reshape(1, 16)
    d["FW1T"] = np.ascontiguousarray(np.asarray(inputs["L1_ffn_w1_w"], f32).T)
    d["FW1S"] = pack(inputs["L1_ffn_w1_s"], 32)
    d["FW2T"] = np.ascontiguousarray(np.asarray(inputs["L1_ffn_w2_w"], f32).T)
    d["FW2S"] = pack(inputs["L1_ffn_w2_s"], 32)
    d["FH1T"] = np.ascontiguousarray(np.asarray(inputs["L1_ffn_h1_w"], f32).T)
    d["FH1S"] = pack(inputs["L1_ffn_h1_s"], 32)
    d["FH2T"] = np.ascontiguousarray(np.asarray(inputs["L1_ffn_h2_w"], f32).T)
    d["FH2S"] = pack(inputs["L1_ffn_h2_s"], 32)
    d["FW3T"] = np.ascontiguousarray(np.asarray(inputs["L1_ffn_w3_w"], f32).T)
    d["FW3S"] = pack(inputs["L1_ffn_w3_s"], 8)
    d["IOTA128"] = np.arange(128, dtype=f32).reshape(128, 1)
    d["IOTAT"] = np.arange(T, dtype=f32).reshape(1, T)
    hi = np.zeros((8, 128, 16), f32)
    for dc in range(8):
        hi[dc, 0:64, 2 * dc] = 1.0
        hi[dc, 64:128, 2 * dc + 1] = 1.0
    d["HEADIND"] = np.ascontiguousarray(hi.transpose(1, 0, 2).reshape(128, 128))
    d["SIGN64"] = np.concatenate([np.ones(32, f32), -np.ones(32, f32)]).reshape(64, 1)
    masks = np.zeros((4, 128, 512), f32)
    keep = np.zeros((4, 128, 512), f32)
    pidx = np.arange(128)[:, None]
    jidx = np.arange(512)[None, :]
    for m in range(4):
        masks[m] = np.where(jidx > 128 * m + pidx, -1e10, 0.0)
        keep[m] = (128 * m + pidx <= jidx).astype(f32)
    d["PREMASK"] = np.ascontiguousarray(masks.reshape(4 * 128, 512)).astype(bf16)
    d["KEEPMASK"] = np.ascontiguousarray(keep.reshape(4 * 128, 512)).astype(bf16)
    maskt = np.zeros((4, 128, 512), f32)
    for m in range(4):
        maskt[m] = np.where(128 * m + pidx > jidx, -1e10, 0.0)
    d["PREMASKT"] = np.ascontiguousarray(maskt.reshape(4 * 128, 512)).astype(bf16)
    d["IDENT"] = np.eye(128, dtype=f32).astype(bf16)
    sh = np.zeros((128, 128), f32)
    sh[np.arange(127), np.arange(1, 128)] = 1.0
    d["SHIFT1"] = sh.astype(bf16)
    oh = np.zeros((128, 128), f32)
    oh[127, 0] = 1.0
    d["OH127"] = oh.astype(bf16)
    return d


def emit_ln(em, B, src, src_key, nch, tg, bc_key, tag, src_is_f32=False):
    oc_ = B["onescol"]
    if src_is_f32:
        for c in range(nch):
            sl = src[:, c * tg:(c + 1) * tg]
            cbb = B["cb"][c % 2]
            em.op("s", lambda s, sl=sl, cbb=cbb: s.copy(cbb[:, 0:tg], sl),
                  reads=[src_key], writes=[f"k:cb{c % 2}"])
            em.op("t", lambda t, cbb=cbb, c=c: t.matmul(
                B["psS1"][0:1, 0:tg], oc_[:], cbb[:, 0:tg],
                start=(c == 0), stop=(c == nch - 1)),
                reads=[f"k:cb{c % 2}", "k:ones"],
                writes=["ps:S1"] if c in (0, nch - 1) else [], inc=1)
            sqb = B["sq"][c % 2]
            em.op("s", lambda s, cbb=cbb, sqb=sqb: s.activation(
                sqb[:, 0:tg], cbb[:, 0:tg], AF.Square, bias=0.0, scale=1.0),
                reads=[f"k:cb{c % 2}"], writes=[f"k:sq{c % 2}"])
            em.op("t", lambda t, sqb=sqb, c=c: t.matmul(
                B["psS2"][0:1, 0:tg], oc_[:], sqb[:, 0:tg],
                start=(c == 0), stop=(c == nch - 1)),
                reads=[f"k:sq{c % 2}", "k:ones"],
                writes=["ps:S2"] if c in (0, nch - 1) else [], inc=1)
    else:
        for c in range(nch):
            sl = src[:, c * tg:(c + 1) * tg]
            em.op("t", lambda t, sl=sl, c=c: t.matmul(
                B["psS1"][0:1, 0:tg], oc_[:], sl,
                start=(c == 0), stop=(c == nch - 1)),
                reads=[src_key, "k:ones"],
                writes=["ps:S1"] if c in (0, nch - 1) else [],
                inc=1 if c == nch - 1 else 0)
        for c in range(nch):
            sl = src[:, c * tg:(c + 1) * tg]
            sqb = B["sq"][c % 2]
            em.op("s", lambda s, sl=sl, sqb=sqb: s.activation(
                sqb[:, 0:tg], sl, AF.Square, bias=0.0, scale=1.0),
                reads=[src_key], writes=[f"k:sq{c % 2}"])
            em.op("t", lambda t, sqb=sqb, c=c: t.matmul(
                B["psS2"][0:1, 0:tg], oc_[:], sqb[:, 0:tg],
                start=(c == 0), stop=(c == nch - 1)),
                reads=[f"k:sq{c % 2}", "k:ones"],
                writes=["ps:S2"] if c in (0, nch - 1) else [], inc=1)
    em.op("s", lambda s: s.copy(B["s1r"][0:1, 0:tg], B["psS1"][0:1, 0:tg]),
          reads=["ps:S1"], writes=["k:s1r"])
    em.op("s", lambda s: s.copy(B["s2r"][0:1, 0:tg], B["psS2"][0:1, 0:tg]),
          reads=["ps:S2"], writes=["k:s2r"])
    em.op("t", lambda t: t.matmul(B["psB"][0:128, 0:tg], B[bc_key][0:1, :],
                                  B["s1r"][0:1, 0:tg], start=True, stop=True),
          reads=["k:s1r", "k:" + bc_key], writes=["ps:B"])
    em.op("s", lambda s: s.copy(B["mub"][:, 0:tg], B["psB"][0:128, 0:tg]),
          reads=["ps:B"], writes=["k:mub"])
    em.op("t", lambda t: t.matmul(B["psB"][0:128, 0:tg], B[bc_key][0:1, :],
                                  B["s2r"][0:1, 0:tg], start=True, stop=True),
          reads=["k:s2r", "k:" + bc_key], writes=["ps:B"])
    em.op("s", lambda s: s.copy(B["m2b"][:, 0:tg], B["psB"][0:128, 0:tg]),
          reads=["ps:B"], writes=["k:m2b"])
    em.op("s", lambda s: s.activation(B["mu2"][:, 0:tg], B["mub"][:, 0:tg],
                                      AF.Square, bias=0.0, scale=1.0),
          reads=["k:mub"], writes=["k:mu2"])
    em.op("v", lambda v: v.tensor_tensor(B["varb"][:, 0:tg], B["m2b"][:, 0:tg],
                                         B["mu2"][:, 0:tg], ALU.subtract),
          reads=["k:m2b", "k:mu2"], writes=["k:varb"])
    em.op("v", lambda v: v.tensor_scalar(B["varb"][:, 0:tg], B["varb"][:, 0:tg],
                                         1e-5, None, ALU.add),
          reads=["k:varb"], writes=["k:varb"])
    em.op("s", lambda s: s.activation(B["varb"][:, 0:tg], B["varb"][:, 0:tg],
                                      AF.Ln, bias=0.0, scale=1.0),
          reads=["k:varb"], writes=["k:varb"])
    em.op("s", lambda s: s.activation(B["rstd"][:, 0:tg], B["varb"][:, 0:tg],
                                      AF.Exp, bias=0.0, scale=-0.5),
          reads=["k:varb"], writes=["k:rstd"])


def emit_ln_apply(em, B, src, src_key, c, tg, out, out_key, scale=None,
                  bias=None, out2=None, out2_key=None):
    sl = src[:, c * tg:(c + 1) * tg]
    t1 = B["t1"]
    em.op("v", lambda v: v.tensor_tensor(t1[:, 0:tg], sl, B["mub"][:, 0:tg],
                                         ALU.subtract),
          reads=[src_key, "k:mub"], writes=["k:t1"])
    if scale is None:
        em.op("v", lambda v: v.tensor_tensor(out, t1[:, 0:tg],
                                             B["rstd"][:, 0:tg], ALU.mult),
              reads=["k:t1", "k:rstd"], writes=[out_key])
        if out2 is not None:
            em.op("s", lambda s: s.copy(out2, out), reads=[out_key],
                  writes=[out2_key])
    else:
        em.op("v", lambda v: v.tensor_tensor(t1[:, 0:tg], t1[:, 0:tg],
                                             B["rstd"][:, 0:tg], ALU.mult),
              reads=["k:t1", "k:rstd"], writes=["k:t1"])
        em.op("v", lambda v: v.tensor_scalar(out, t1[:, 0:tg], scale, bias,
                                             ALU.mult, ALU.add),
              reads=["k:t1", "k:par", "k:parb", "k:parc", "k:pard"],
              writes=[out_key])


def mk_ln_bufs(em, tg, maxch):
    B = {}
    B["onescol"] = em.sb("onescol", [128, 1], BF16)
    B["bc8"] = em.sb("bc8", [1, 128], F32)
    B["bc56"] = em.sb("bc56", [1, 128], F32)
    sqT = em.sb("sqT", [128, 2 * tg], BF16)
    B["sqT"] = sqT
    B["sq"] = [_ArView(sqT, i * tg, 128, tg) for i in range(2)]
    B["cb"] = [em.sb(f"cbb{i}", [128, tg], BF16) for i in range(2)]
    B["s1r"] = em.sb("s1r", [1, tg], F32)
    B["s2r"] = em.sb("s2r", [1, tg], F32)
    B["mub"] = em.sb("mub", [128, tg], F32)
    B["m2b"] = em.sb("m2b", [128, tg], F32)
    B["varb"] = em.sb("varb", [128, tg], F32)
    B["rstd"] = em.sb("rstd", [128, tg], F32)
    B["mu2"] = em.sb("mu2", [128, tg], F32)
    B["t1"] = em.sb("t1", [128, tg], F32)
    B["psS1"] = em.ps("psS1", [1, 512])
    B["psS2"] = em.ps("psS2", [1, 512])
    B["psB"] = em.ps("psB", [128, 512])
    em.op("v", lambda v: v.memset(B["onescol"][:], 1.0), writes=["k:ones"])
    em.op("v", lambda v: v.memset(B["bc8"][:], 1.0 / 1024.0), writes=["k:bc8"])
    em.op("v", lambda v: v.memset(B["bc56"][:], 1.0 / 7168.0), writes=["k:bc56"])
    return B


def build_phase_a(TG=256, Tt=T, em=None, io=None, B=None):
    if em is None:
        em = Em("A")
        io = {
            "XF": em.din("XF", [D, Tt], F32),
            "WINT": em.din("WINT", [D, QKVP], F32),
            "LNG_IN": em.din("LNG_IN", [128, 8], F32),
            "LNB_IN": em.din("LNB_IN", [128, 8], F32),
            "MIDG": em.din("MIDG", [128, 56], F32),
            "MIDB": em.din("MIDB", [128, 56], F32),
            "SHIFT1": em.din("SHIFT1", [128, 128], BF16),
            "OH127": em.din("OH127", [128, 128], BF16),
            "IDENT": em.din("IDENT", [128, 128], BF16),
            "XNF": em.dout("XNF", [D, Tt], F32),
            "XNB": em.dout("XNB", [D, Tt], BF16),
            "QH": em.dout("QH", [NH, DH, Tt], BF16),
            "KH": em.dout("KH", [NH, DH, Tt], BF16),
            "VSC": em.dout("VSC", [128, (Tt // 128) * 1056], BF16),
            "PG": em.dout("PG", [4096, Tt], BF16),
        }
    NG = Tt // TG
    KTPG = TG // 128

    (XF, WINT, LNG_IN, LNB_IN, MIDG, MIDB, SHIFT1, OH127, IDENT, XNF, XNB, QH, KH, VSC, PG) = (io[k] for k in ("XF", "WINT", "LNG_IN", "LNB_IN", "MIDG", "MIDB", "SHIFT1", "OH127", "IDENT", "XNF", "XNB", "QH", "KH", "VSC", "PG",))
    winTa = em.salloc("winTa", [128, 7 * QKVP], BF16)
    winTb = em.salloc("winTb", [128, QKVP], BF16)

    def winT_sl(c, lo, hi):
        if c < 7:
            return winTa[:, QKVP * c + lo:QKVP * c + hi]
        return winTb[:, lo:hi]
    lng = em.sb("lng", [128, 8], F32)
    lnb = em.sb("lnb", [128, 8], F32)
    midg = em.sb("midg", [128, 56], F32)
    midb = em.sb("midb", [128, 56], F32)
    qg_s = em.sb("qg_s", [128, 8], F32)
    qb_s = em.sb("qb_s", [128, 8], F32)
    shm = em.sb("shm", [128, 128], BF16)
    ohm = em.sb("ohm", [128, 128], BF16)
    idn = em.sb("idn", [128, 128], BF16)
    xg = em.salloc("xg", [128, NCH * TG], F32)
    xnf = em.salloc("xnf", [128, NCH * TG], F32)
    xnb = em.salloc("xnb", [128, NCH * TG], BF16)
    x1 = em.salloc("x1", [128, NCH * (TG + 4)], BF16)
    hgrp = em.salloc("hgrp", [128, 56 * TG], BF16)
    stage = [em.salloc(f"stage{i}", [128, TG], BF16) for i in range(2)]
    vstg = em.salloc("vstg", [128, TG], BF16)
    vaugg = em.salloc("vaugg", [128, KTPG * 1056], BF16)
    gbuf = [em.salloc(f"gbuf{i}", [128, TG], BF16) for i in range(2)]
    pgo = [em.salloc(f"pgo{i}", [128, TG], BF16) for i in range(2)]

    if B is None:
        B = mk_ln_bufs(em, TG, 56)
    psW = [em.ps(f"psW{i}", [128, 512]) for i in range(3)]
    psT = em.ps("psT", [128, 1024], BF16)
    psSh = em.ps("psSh", [128, 512])

    em.dma("y", lng[:], LNG_IN, writes=["k:par"])
    em.dma("y", lnb[:], LNB_IN, writes=["k:parb"])
    em.dma("y", midg[:], MIDG, writes=["k:parc"])
    em.dma("y", midb[:], MIDB, writes=["k:pard"])
    em.dma("g", shm[:], SHIFT1, writes=["k:shm"])
    em.dma("g", ohm[:], OH127, writes=["k:ohm"])
    em.dma("g", idn[:], IDENT, writes=["k:idn"])
    for c in range(NCH):
        em.dma("g", winT_sl(c, 0, QKVP),
               WINT[128 * c:128 * (c + 1), :], writes=[f"k:winT{c}"])
    em.op("v", lambda v: v.tensor_scalar(qg_s[:], midg[:, 0:8], 0.125, None,
                                         ALU.mult),
          reads=["k:parc"], writes=["k:par"])
    em.op("v", lambda v: v.tensor_scalar(qb_s[:], midb[:, 0:8], 0.125, None,
                                         ALU.mult),
          reads=["k:pard"], writes=["k:par"])
    em.op("v", lambda v: v.memset(x1[:, :], 0.0), writes=["k:x1"])
    em.op("v", lambda v: v.memset(vaugg[:, :], 0.0), writes=["k:vaugg"])

    for g in range(NG):
        t0 = g * TG
        em.dma("y", xg[:], fm_ap(XF, 0, NCH, t0, TG), writes=["k:xg"])

        emit_ln(em, B, xg, "k:xg", NCH, TG, "bc8", "e",
                src_is_f32=True)
        for c in range(NCH):
            emit_ln_apply(em, B, xg, "k:xg", c, TG,
                          xnf[:, c * TG:(c + 1) * TG], "k:xnf",
                          out2=xnb[:, c * TG:(c + 1) * TG], out2_key="k:xnb")
        em.dma("y", fm_ap(XNF, 0, NCH, t0, TG), xnf[:], reads=["k:xnf"],
               is_out=True)
        em.dma("y", fm_ap(XNB, 0, NCH, t0, TG), xnb[:], reads=["k:xnb"],
               is_out=True)

        for c in (0, 1):
            base = c * (TG + 4)
            em.op("v", lambda v, base=base: v.tensor_copy(
                x1[:, base:base + 4], x1[:, base + TG:base + TG + 4]),
                reads=["k:x1"], writes=["k:x1"])
        emit_ln(em, B, xnb, "k:xnb", NCH, TG, "bc8", "i")
        for c in range(NCH):
            base = c * (TG + 4)
            emit_ln_apply(em, B, xnb, "k:xnb", c, TG,
                          x1[:, base + 4:base + 4 + TG], "k:x1",
                          scale=lng[:, c:c + 1], bias=lnb[:, c:c + 1])

        def rhs_for(c):
            base = c * (TG + 4)
            if c == 0:
                return x1[:, base + 3:base + 3 + TG]
            if c == 1:
                return x1[:, base + 1:base + 1 + TG]
            return x1[:, base + 4:base + 4 + TG]

        def a_stats(oc):
            if oc % 2 == 0:
                em.op("t", lambda t, oc=oc: t.matmul(
                    B["psS1"][0:1, 0:2 * TG], B["onescol"][:],
                    hgrp[:, oc * TG:(oc + 2) * TG],
                    start=(oc == 0), stop=(oc == 54)),
                    reads=["k:hgrp", "k:ones"],
                    writes=["ps:S1"] if oc in (0, 54) else [], inc=1)
            sc0 = (oc % 2) * TG
            em.op("v", lambda v, oc=oc, sc0=sc0: v.tensor_tensor(
                B["sqT"][0:128, sc0:sc0 + TG],
                hgrp[:, oc * TG:(oc + 1) * TG],
                hgrp[:, oc * TG:(oc + 1) * TG], ALU.mult),
                reads=["k:hgrp"], writes=[f"k:sq{oc % 2}"])
            if oc % 2 == 1:
                em.op("t", lambda t, oc=oc: t.matmul(
                    B["psS2"][0:1, 0:2 * TG], B["onescol"][:],
                    B["sqT"][0:128, 0:2 * TG],
                    start=(oc == 1), stop=(oc == 55)),
                    reads=["k:sq0", "k:sq1", "k:ones"],
                    writes=["ps:S2"] if oc in (1, 55) else [], inc=1)

        for oc in range(56):
            pw = psW[oc % 3]
            for c in range(NCH):
                lhs = winT_sl(c, 128 * oc, 128 * (oc + 1))
                em.op("t", lambda t, lhs=lhs, c=c, pw=pw: t.matmul(
                    pw[:, 0:TG], lhs, rhs_for(c), start=(c == 0),
                    stop=(c == NCH - 1)),
                    reads=["k:x1", f"k:winT{c}"],
                    writes=[f"ps:W{oc % 3}"] if c in (0, NCH - 1) else [],
                    inc=1 if c == NCH - 1 else 0)
            hsl = hgrp[:, oc * TG:(oc + 1) * TG]
            em.op("s", lambda s, hsl=hsl, pw=pw, oc=oc: s.activation(
                hsl, pw[:, 0:TG], AF.Relu, bias=0.0, scale=1.0),
                reads=[f"ps:W{oc % 3}"], writes=["k:hgrp"])
            if oc >= 1:
                a_stats(oc - 1)
        a_stats(55)
        em.op("s", lambda s: s.copy(B["s1r"][0:1, 0:2 * TG],
                                    B["psS1"][0:1, 0:2 * TG]),
              reads=["ps:S1"], writes=["k:s1r"])
        em.op("v", lambda v: v.tensor_tensor(
            B["s1r"][0:1, 0:TG], B["s1r"][0:1, 0:TG],
            B["s1r"][0:1, TG:2 * TG], ALU.add),
            reads=["k:s1r"], writes=["k:s1r"])
        em.op("s", lambda s: s.copy(B["s2r"][0:1, 0:2 * TG],
                                    B["psS2"][0:1, 0:2 * TG]),
              reads=["ps:S2"], writes=["k:s2r"])
        em.op("v", lambda v: v.tensor_tensor(
            B["s2r"][0:1, 0:TG], B["s2r"][0:1, 0:TG],
            B["s2r"][0:1, TG:2 * TG], ALU.add),
            reads=["k:s2r"], writes=["k:s2r"])
        em.op("t", lambda t: t.matmul(B["psB"][0:128, 0:TG], B["bc56"][0:1, :],
                                      B["s1r"][0:1, 0:TG], start=True, stop=True),
              reads=["k:s1r", "k:bc56"], writes=["ps:B"])
        em.op("s", lambda s: s.copy(B["mub"][:, 0:TG], B["psB"][0:128, 0:TG]),
              reads=["ps:B"], writes=["k:mub"])
        em.op("t", lambda t: t.matmul(B["psB"][0:128, 0:TG], B["bc56"][0:1, :],
                                      B["s2r"][0:1, 0:TG], start=True, stop=True),
              reads=["k:s2r", "k:bc56"], writes=["ps:B"])
        em.op("s", lambda s: s.copy(B["m2b"][:, 0:TG], B["psB"][0:128, 0:TG]),
              reads=["ps:B"], writes=["k:m2b"])
        em.op("s", lambda s: s.activation(B["mu2"][:, 0:TG], B["mub"][:, 0:TG],
                                          AF.Square, bias=0.0, scale=1.0),
              reads=["k:mub"], writes=["k:mu2"])
        em.op("v", lambda v: v.tensor_tensor(B["varb"][:, 0:TG],
                                             B["m2b"][:, 0:TG],
                                             B["mu2"][:, 0:TG], ALU.subtract),
              reads=["k:m2b", "k:mu2"], writes=["k:varb"])
        em.op("v", lambda v: v.tensor_scalar(B["varb"][:, 0:TG],
                                             B["varb"][:, 0:TG],
                                             1e-5, None, ALU.add),
              reads=["k:varb"], writes=["k:varb"])
        em.op("s", lambda s: s.activation(B["varb"][:, 0:TG], B["varb"][:, 0:TG],
                                          AF.Ln, bias=0.0, scale=1.0),
              reads=["k:varb"], writes=["k:varb"])
        em.op("s", lambda s: s.activation(B["rstd"][:, 0:TG], B["varb"][:, 0:TG],
                                          AF.Exp, bias=0.0, scale=-0.5),
              reads=["k:varb"], writes=["k:rstd"])

        def mid_apply(oc, dst, dkey, sc, bi):
            hsl = hgrp[:, oc * TG:(oc + 1) * TG]
            em.op("v", lambda v, hsl=hsl: v.tensor_tensor(
                B["t1"][:, 0:TG], hsl, B["mub"][:, 0:TG], ALU.subtract),
                reads=["k:hgrp", "k:mub"], writes=["k:t1"])
            em.op("v", lambda v: v.tensor_tensor(
                B["t1"][:, 0:TG], B["t1"][:, 0:TG], B["rstd"][:, 0:TG],
                ALU.mult),
                reads=["k:t1", "k:rstd"], writes=["k:t1"])
            em.op("v", lambda v, dst=dst, sc=sc, bi=bi: v.tensor_scalar(
                dst, B["t1"][:, 0:TG], sc, bi, ALU.mult, ALU.add),
                reads=["k:t1", "k:par", "k:parb", "k:parc", "k:pard"],
                writes=[dkey])

        for oc in range(8):
            sg = stage[oc % 2]
            mid_apply(oc, sg[:, 0:TG], f"k:stage{oc % 2}",
                      qg_s[:, oc:oc + 1], qb_s[:, oc:oc + 1])
            em.dma("y", QH[2 * oc, :, t0:t0 + TG], sg[0:64, 0:TG],
                   reads=[f"k:stage{oc % 2}"], is_out=True)
            em.dma("y", QH[2 * oc + 1, :, t0:t0 + TG], sg[64:128, 0:TG],
                   reads=[f"k:stage{oc % 2}"], is_out=True)
        for oc in range(8, 16):
            sg = stage[oc % 2]
            mid_apply(oc, sg[:, 0:TG], f"k:stage{oc % 2}",
                      midg[:, oc:oc + 1], midb[:, oc:oc + 1])
            h0 = 2 * (oc - 8)
            em.dma("y", KH[h0, :, t0:t0 + TG], sg[0:64, 0:TG],
                   reads=[f"k:stage{oc % 2}"], is_out=True)
            em.dma("y", KH[h0 + 1, :, t0:t0 + TG], sg[64:128, 0:TG],
                   reads=[f"k:stage{oc % 2}"], is_out=True)
        for oc in range(16, 24):
            mid_apply(oc, vstg[:, 0:TG], "k:vstg",
                      midg[:, oc:oc + 1], midb[:, oc:oc + 1])
            for ktl in range(KTPG):
                em.op("t", lambda t, ktl=ktl: t.transpose(
                    psT[:, 0:128], vstg[:, 128 * ktl:128 * (ktl + 1)], idn[:]),
                    reads=["k:vstg", "k:idn"], writes=["ps:T"])
                h0 = 2 * (oc - 16)
                for hh in range(2):
                    cbase = 1056 * ktl + 66 * (h0 + hh)
                    em.op("s", lambda s, cbase=cbase, hh=hh: s.copy(
                        vaugg[:, cbase:cbase + 64],
                        psT[:, 64 * hh:64 * (hh + 1)]),
                        reads=["ps:T"], writes=["k:vaugg"])
        em.dma("y", VSC[:, 1056 * KTPG * g:1056 * KTPG * (g + 1)], vaugg[:],
               reads=["k:vaugg"], is_out=True)

        for oc in range(24, 56):
            mid_apply(oc, hgrp[:, oc * TG:(oc + 1) * TG], "k:hgrp",
                      midg[:, oc:oc + 1], midb[:, oc:oc + 1])

        def emit_pg_sh(j, i):
            cur = hgrp[:, j * TG:(j + 1) * TG]
            prev = hgrp[:, (55 if j == 24 else j - 1) * TG:
                        (56 if j == 24 else j) * TG]
            pb = psSh[:, 0:TG] if i % 2 == 0 else psW[0][:, 0:TG]
            pkey = "ps:Sh" if i % 2 == 0 else "ps:W0"
            em.op("t", lambda t, cur=cur, pb=pb: t.matmul(
                pb, shm[:], cur, start=True, stop=False),
                reads=["k:hgrp", "k:shm"], writes=[pkey], inc=0)
            em.op("t", lambda t, prev=prev, pb=pb: t.matmul(
                pb, ohm[:], prev, start=False, stop=True),
                reads=["k:hgrp", "k:ohm"], writes=[pkey])
            return pb, pkey

        order = list(range(25, 56)) + [24]
        shq = []
        for i, j in enumerate(order):
            pb, pkey = emit_pg_sh(j, i)
            em.op("s", lambda s, j=j, i=i: s.activation(
                gbuf[i % 2][:, 0:TG], hgrp[:, j * TG:(j + 1) * TG],
                AF.Gelu, bias=0.0, scale=1.0),
                reads=["k:hgrp"], writes=[f"k:gbuf{i % 2}"])
            em.op("v", lambda v, i=i, pb=pb: v.tensor_tensor(
                pgo[i % 2][:, 0:TG], gbuf[i % 2][:, 0:TG], pb, ALU.mult),
                reads=[f"k:gbuf{i % 2}", pkey], writes=[f"k:pgo{i % 2}"])
            em.dma("y", PG[128 * (j - 24):128 * (j - 23), t0:t0 + TG],
                   pgo[i % 2][:, 0:TG], reads=[f"k:pgo{i % 2}"], is_out=True)

    return em



def build_phase_b(Tt=T, em=None, io=None, B=None):
    if em is None:
        em = Em("B")
        io = {
            "QHD": em.din("QH", [NH, DH, Tt], BF16),
            "KHD": em.din("KH", [NH, DH, Tt], BF16),
            "VSCD": em.din("VSC", [128, (Tt // 128) * 1056], BF16),
            "SLOPES": em.din("SLOPES", [1, 16], F32),
            "IOTA128": em.din("IOTA128", [128, 1], F32),
            "IOTAT": em.din("IOTAT", [1, Tt], F32),
            "PREMASK": em.din("PREMASK", [512, 512], BF16),
            "PREMASKT": em.din("PREMASKT", [512, 512], BF16),
            "IDENT": em.din("IDENT", [128, 128], BF16),
            "OH": em.dout("OH", [NH, DH, Tt], BF16),
            "MH": em.nc.dram_tensor("MH", [NH, Tt], BF16, kind="Internal").ap(),
        }
    NQT = Tt // 128
    NQG = Tt // 512

    (QHD, KHD, VSCD, SLOPES, IOTA128, IOTAT, PREMASK, PREMASKT, IDENT, OH, MH) = (io[k] for k in ("QHD", "KHD", "VSCD", "SLOPES", "IOTA128", "IOTAT", "PREMASK", "PREMASKT", "IDENT", "OH", "MH",))
    vsc = em.salloc("vsc", [128, (Tt // 128) * 1056], BF16)
    kb = [em.salloc(f"kb{i}", [66, Tt], BF16) for i in range(2)]
    qb = [em.salloc(f"qb{i}", [66, Tt], BF16) for i in range(2)]
    slp = em.sb("slp", [1, 16], F32)
    slpb = em.sb("slpb", [128, 16], F32)
    onesf = em.sb("onesf", [1, 128], F32)
    onesb64 = em.sb("onesb64", [128, 64], BF16)
    io128 = em.sb("io128", [128, 1], F32)
    iot = em.salloc("iot", [1, Tt], F32)
    alro = em.salloc("alro", [1, Tt], BF16)
    albc = em.salloc("albc", [128, NH * 32], F32)
    prem = em.salloc("prem", [128, 4 * 512], BF16)
    premt = em.salloc("premt", [128, 4 * 512], BF16)
    idn = em.sb("idn", [128, 128], BF16)
    rmax = em.salloc("rmax", [128, NQT], BF16)
    rtmp = em.sb("rtmp", [128, 4], BF16)
    mrow = em.salloc("mrow", [32, 128], BF16)
    abuf = em.salloc("abuf", [128, 4 * 512], BF16)
    zrs = em.salloc("zrs", [128, 512], F32)
    zrb = em.salloc("zrb", [128, 512], BF16)
    fbc = em.salloc("fbc", [64, 512], F32)
    obuf = em.salloc("obuf", [64, 512], BF16)

    psS = [em.ps(f"psS{i}", [128, 512]) for i in range(4)]
    psO = [em.ps(f"psO{i}", [128, 512]) for i in range(2)]
    psB = em.ps("psB", [128, 512])
    psTr = em.ps("psTr", [128, 1024], BF16)

    em.dma("y", vsc[:], VSCD, writes=["k:vsc"])
    em.dma("y", slp[:], SLOPES, writes=["k:slp"])
    em.dma("y", io128[:], IOTA128, writes=["k:io"])
    em.dma("y", iot[:], IOTAT, writes=["k:io"])
    em.dma("y", prem[:], PREMASK.rearrange("(m p) j -> p m j", p=128),
           writes=["k:prem"])
    em.dma("y", premt[:], PREMASKT.rearrange("(m p) j -> p m j", p=128),
           writes=["k:premt"])
    em.dma("y", idn[:], IDENT, writes=["k:idn"])
    em.op("v", lambda v: v.memset(onesf[:], 1.0), writes=["k:ones"])
    em.op("v", lambda v: v.memset(onesb64[:], 1.0), writes=["k:ones64"])
    for i in range(2):
        em.op("v", lambda v, i=i: v.memset(kb[i][64:65, :], 1.0),
              writes=[f"k:kb{i}"])
        em.op("v", lambda v, i=i: v.memset(qb[i][64:66, :], 1.0),
              writes=[f"k:qb{i}"])
    nt = (Tt // 128) * 16
    em.op("v", lambda v: v.memset(vsc[:, 64:1056 * (Tt // 128):66], 1.0),
          reads=[], writes=["k:vsc"])
    em.op("t", lambda t: t.matmul(psB[0:128, 0:16], onesf[0:1, :], slp[0:1, :],
                                  start=True, stop=True),
          reads=["k:ones", "k:slp"], writes=["ps:B"])
    em.op("s", lambda s: s.copy(slpb[:], psB[0:128, 0:16]),
          reads=["ps:B"], writes=["k:slpb"])
    for h in range(NH):
        for kt in range(Tt // 128):
            em.op("v", lambda v, h=h, kt=kt: v.tensor_scalar(
                albc[:, 32 * h + kt:32 * h + kt + 1], io128[:],
                float(128 * kt), slpb[:, h:h + 1], ALU.add, ALU.mult),
                reads=["k:io", "k:slpb"], writes=["k:albc"], inc=0)
    em.op("v", lambda v: v.tensor_copy(rtmp[:, 0:1], io128[:]),
          reads=["k:io"], writes=["k:rtmp0"])

    def b_prepass(h, interleaved=False):
        par = h % 2
        kbu, qbu = kb[par], qb[par]
        kkey, qkey = f"k:kb{par}", f"k:qb{par}"
        em.dma("y", kbu[0:64, :], KHD[h], reads=[], writes=[kkey])
        em.dma("y", qbu[0:64, :], QHD[h], reads=[], writes=[qkey])
        em.op("v", lambda v, h=h: v.tensor_scalar(
            alro[:], iot[:], slp[0:1, h:h + 1], None, ALU.mult),
            reads=["k:io", "k:slp", "k:alro"], writes=["k:alro"])
        em.dma("s", kbu[65:66, :], alro[0:1, :], reads=["k:alro"],
               writes=[kkey])
        em.op("v", lambda v, qbu=qbu: v.memset(qbu[64:65, :], 0.0),
              writes=[qkey])
        if interleaved:
            yield
        j = 0
        for qt in range(NQT):
            for kg in range(qt // 4 + 1):
                ip = ((j + 2) % 4) if interleaved else ((qt * 9 + kg) % 4)
                pp = psS[ip]
                diag = (kg == qt // 4)
                em.op("t", lambda t, pp=pp, qt=qt, kg=kg, qbu=qbu, kbu=kbu,
                      diag=diag:
                      t.matmul(pp[:, 0:512],
                               qbu[0:66, 128 * qt:128 * (qt + 1)],
                               kbu[0:66, 512 * kg:512 * (kg + 1)],
                               start=True, stop=not diag),
                      reads=[qkey, kkey], writes=[f"ps:S{ip}"],
                      inc=0 if diag else 1)
                if diag:
                    m = qt % 4
                    em.op("t", lambda t, pp=pp, m=m: t.matmul(
                        pp[:, 0:512], idn[:],
                        prem[:, 512 * m:512 * (m + 1)],
                        start=False, stop=True),
                        reads=["k:idn", "k:prem"],
                        writes=[f"ps:S{ip}"])
                if kg == 0:
                    em.op("v", lambda v, pp=pp, qt=qt: v.reduce_max(
                        rmax[:, qt:qt + 1], pp[:, 0:512], axis=AX.X),
                        reads=[f"ps:S{ip}"], writes=["k:rmax"])
                else:
                    em.op("v", lambda v, pp=pp, qt=qt: v.reduce_max(
                        rtmp[:, qt % 4:qt % 4 + 1], pp[:, 0:512], axis=AX.X),
                        reads=[f"ps:S{ip}"], writes=[f"k:rtmp{qt % 4}"])
                    em.op("v", lambda v, qt=qt: v.tensor_tensor(
                        rmax[:, qt:qt + 1], rmax[:, qt:qt + 1],
                        rtmp[:, qt % 4:qt % 4 + 1], ALU.max),
                        reads=[f"k:rtmp{qt % 4}", "k:rmax"], writes=["k:rmax"])
                j += 1
                if interleaved:
                    yield
        em.op("t", lambda t: t.transpose(psTr[0:NQT, 0:128], rmax[:, 0:NQT],
                                         idn[:]),
              reads=["k:rmax", "k:idn"], writes=["ps:Tr"])
        em.op("s", lambda s: s.activation(mrow[0:NQT, :], psTr[0:NQT, 0:128],
                                          AF.Copy, bias=CSH, scale=-1.0),
              reads=["ps:Tr"], writes=["k:mrow"])
        em.dma("y", MH[h:h + 1, :].rearrange("o (a b) -> (o a) b", b=128),
               mrow[0:NQT, :], reads=["k:mrow"], writes=["k:MH"])
        em.dma("y", qbu[64:65, :], MH[h:h + 1, :], reads=["k:MH"],
               writes=[qkey])

    def b_realpass(h, feeder=iter(())):
        par = h % 2
        kbu, qbu = kb[par], qb[par]
        kkey, qkey = f"k:kb{par}", f"k:qb{par}"
        for qg in range(NQG):
            nkt = 4 * qg + 4
            LAG = 2
            pso = psO[qg % 2]
            pokey = f"ps:O{qg % 2}"

            def emit_o(kt, nkt=None, h=h, pso=pso, pokey=pokey):
                asl = abuf[:, 512 * (kt % 4):512 * (kt % 4 + 1)]
                em.op("t", lambda t, asl=asl, kt=kt, nkt=nkt, h=h, pso=pso:
                      t.matmul(
                    pso[0:66, 0:512], vsc[:, 1056 * kt + 66 * h:
                                          1056 * kt + 66 * h + 66],
                    asl, start=(kt == 0), stop=(kt == nkt - 1)),
                    reads=[f"k:ab{kt % 4}", "k:vsc"],
                    writes=[pokey] if kt in (0, nkt - 1) else [],
                    inc=1 if kt == nkt - 1 else 0)

            for kt in range(nkt):
                ss = psS[kt % 4]
                diag = kt >= 4 * qg
                em.op("t", lambda t, ss=ss, kt=kt, qg=qg, kbu=kbu, qbu=qbu,
                      diag=diag:
                      t.matmul(ss[:, 0:512],
                               kbu[0:65, 128 * kt:128 * (kt + 1)],
                               qbu[0:65, 512 * qg:512 * (qg + 1)],
                               start=True, stop=not diag),
                      reads=[kkey, qkey], writes=[f"ps:S{kt % 4}"],
                      inc=0 if diag else 1)
                if diag:
                    m = kt - 4 * qg
                    em.op("t", lambda t, ss=ss, m=m: t.matmul(
                        ss[:, 0:512], idn[:],
                        premt[:, 512 * m:512 * (m + 1)],
                        start=False, stop=True),
                        reads=["k:idn", "k:premt"],
                        writes=[f"ps:S{kt % 4}"])
                asl = abuf[:, 512 * (kt % 4):512 * (kt % 4 + 1)]
                em.op("s", lambda s, ss=ss, asl=asl, h=h, kt=kt: s.activation(
                    asl, ss[:, 0:512], AF.Exp,
                    bias=albc[:, 32 * h + kt:32 * h + kt + 1], scale=1.0),
                    reads=[f"ps:S{kt % 4}", "k:albc"],
                    writes=[f"k:ab{kt % 4}"])
                next(feeder, None)
                if kt >= LAG:
                    emit_o(kt - LAG, nkt)
            for kt in range(max(0, nkt - LAG), nkt):
                emit_o(kt, nkt)
            em.op("s", lambda s, pso=pso: s.activation(
                zrs[64:65, 0:512], pso[64:65, 0:512], AF.Ln,
                bias=0.0, scale=2.0 ** -24),
                  reads=[pokey], writes=["k:zrs"])
            em.op("s", lambda s: s.activation(
                zrb[64:65, 0:512], zrs[64:65, 0:512], AF.Exp,
                bias=0.0, scale=-1.0),
                  reads=["k:zrs"], writes=["k:zrb"])
            em.op("t", lambda t: t.matmul(psB[0:64, 0:512],
                                          onesb64[64:65, 0:64],
                                          zrb[64:65, 0:512],
                                          start=True, stop=True),
                  reads=["k:zrb", "k:ones64"], writes=["ps:B"])
            em.op("s", lambda s: s.activation(
                fbc[0:64, :], psB[0:64, 0:512], AF.Copy,
                bias=0.0, scale=2.0 ** -24),
                  reads=["ps:B"], writes=["k:fbc"])
            em.op("v", lambda v, pso=pso: v.tensor_tensor(
                obuf[:], pso[0:64, 0:512], fbc[:], ALU.mult),
                  reads=[pokey, "k:fbc"], writes=["k:obuf"])
            em.dma("y", OH[h, :, 512 * qg:512 * (qg + 1)], obuf[:],
                   reads=["k:obuf"], is_out=True)

    for _ in b_prepass(0):
        pass
    for h in range(NH):
        feeder = b_prepass(h + 1, True) if h + 1 < NH else iter(())
        next(feeder, None)
        b_realpass(h, feeder)
        for _ in feeder:
            pass
    return em
def build_phase_c(TG=256, Tt=T, em=None, io=None, B=None):
    if em is None:
        em = Em("C")
        io = {
            "OHD": em.din("OH", [NH, DH, Tt], BF16),
            "PGD": em.din("PG", [4096, Tt], BF16),
            "WOT_O": em.din("WOT_O", [64, 16 * 1024], F32),
            "WOT_PG": em.din("WOT_PG", [4096, 1024], F32),
            "OUTG": em.din("OUTG", [128, 8], F32),
            "OUTB": em.din("OUTB", [128, 8], F32),
            "XFD": em.din("XF", [D, Tt], F32),
            "XMID": em.dout("XMID", [D, Tt], F32),
        }
    NG = Tt // TG

    (OHD, PGD, WOT_O, WOT_PG, OUTG, OUTB, XFD, XMID) = (io[k] for k in ("OHD", "PGD", "WOT_O", "WOT_PG", "OUTG", "OUTB", "XFD", "XMID",))
    woto = em.salloc("woto", [64, 16 * 1024], BF16)
    wotp = em.salloc("wotp", [128, 32 * 1024], BF16)
    outg = em.sb("outg", [128, 8], F32)
    outb = em.sb("outb", [128, 8], F32)
    ohg = em.salloc("ohg", [64, 16 * TG], BF16)
    pgg = em.salloc("pgg", [128, 32 * TG], BF16)
    wg = em.salloc("wg", [128, NCH * TG], F32)
    wgb = em.salloc("wgb", [128, NCH * TG], BF16)
    xg = [em.salloc(f"xg{i}", [128, TG], F32) for i in range(2)]
    t2 = em.salloc("t2", [128, TG], F32)
    if B is None:
        B = mk_ln_bufs(em, TG, 8)
    psA = [em.ps(f"psA{i}", [128, 512]) for i in range(2)]
    psP2 = [em.ps(f"psP2{i}", [128, 512]) for i in range(2)]

    for i in range(4):
        em.dma("g", woto[:, 4096 * i:4096 * (i + 1)],
               WOT_O[:, 4096 * i:4096 * (i + 1)], writes=[f"k:woto{i}"])
    for f in range(32):
        em.dma("g", wotp[:, 1024 * f:1024 * (f + 1)],
               WOT_PG[128 * f:128 * (f + 1), :], writes=[f"k:wotpf{f}"])
    em.dma("y", outg[:], OUTG, writes=["k:par"])
    em.dma("y", outb[:], OUTB, writes=["k:parb"])

    for g in range(NG):
        t0 = g * TG
        for h in range(NH):
            em.dma("y", ohg[:, TG * h:TG * (h + 1)], OHD[h, :, t0:t0 + TG],
                   writes=["k:ohg"])
        em.dma("y", pgg[:], fm_ap(PGD, 0, 32, t0, TG), writes=["k:pgg"])
        for dc in range(NCH):
            pa, pp2 = psA[dc % 2], psP2[dc % 2]
            for h in range(NH):
                em.op("t", lambda t, h=h, dc=dc, pa=pa: t.matmul(
                    pa[0:128, 0:TG],
                    woto[0:64, 1024 * h + 128 * dc:1024 * h + 128 * (dc + 1)],
                    ohg[0:64, TG * h:TG * (h + 1)],
                    start=(h == 0), stop=(h == NH - 1)),
                    reads=["k:ohg", f"k:woto{h // 4}"],
                    writes=[f"ps:A{dc % 2}"] if h in (0, NH - 1) else [],
                    inc=1 if h == NH - 1 else 0)
            for f in range(32):
                em.op("t", lambda t, f=f, dc=dc, pp2=pp2: t.matmul(
                    pp2[:, 0:TG],
                    wotp[:, 1024 * f + 128 * dc:1024 * f + 128 * (dc + 1)],
                    pgg[:, TG * f:TG * (f + 1)],
                    start=(f == 0), stop=(f == 31)),
                    reads=["k:pgg", f"k:wotpf{f}"],
                    writes=[f"ps:P2{dc % 2}"] if f in (0, 31) else [],
                    inc=1 if f == 31 else 0)
            em.op("s", lambda s, pp2=pp2, dc=dc: s.copy(t2[:, 0:TG],
                                                          pp2[:, 0:TG]),
                  reads=[f"ps:P2{dc % 2}"], writes=["k:t2"])
            wsl = wg[:, dc * TG:(dc + 1) * TG]
            em.op("v", lambda v, wsl=wsl, pa=pa, dc=dc: v.tensor_tensor(
                wsl, pa[0:128, 0:TG], t2[:, 0:TG], ALU.add),
                reads=[f"ps:A{dc % 2}", "k:t2"], writes=["k:wg"])
            em.op("s", lambda s, dc=dc, wsl=wsl: s.copy(
                wgb[:, dc * TG:(dc + 1) * TG], wsl),
                reads=["k:wg"], writes=["k:wgb"])
        emit_ln(em, B, wgb, "k:wgb", NCH, TG, "bc8", "o")
        rings = [(B["t1"], "k:t1"), (B["mu2"], "k:mu2")]
        for dc in range(NCH):
            wsl = wg[:, dc * TG:(dc + 1) * TG]
            tr, tk = rings[dc % 2]
            xgr, xk = xg[dc % 2], f"k:xg{dc % 2}"
            em.op("v", lambda v, wsl=wsl, tr=tr: v.tensor_tensor(
                tr[:, 0:TG], wsl, B["mub"][:, 0:TG], ALU.subtract),
                reads=["k:wg", "k:mub"], writes=[tk])
            em.op("v", lambda v, tr=tr: v.tensor_tensor(
                tr[:, 0:TG], tr[:, 0:TG], B["rstd"][:, 0:TG],
                ALU.mult),
                reads=[tk, "k:rstd"], writes=[tk])
            em.op("s", lambda s, dc=dc, tr=tr: s.activation(
                tr[:, 0:TG], tr[:, 0:TG], AF.Identity,
                bias=outb[:, dc:dc + 1], scale=outg[:, dc:dc + 1]),
                reads=[tk, "k:par", "k:parb"], writes=[tk])
            em.dma("y", xgr[:, 0:TG], fm_ap(XFD, dc, 1, t0, TG),
                   writes=[xk])
            em.op("v", lambda v, wsl=wsl, tr=tr, xgr=xgr: v.tensor_tensor(
                wsl, tr[:, 0:TG], xgr[:, 0:TG], ALU.add),
                reads=[tk, xk], writes=["k:wg"])
        em.dma("y", fm_ap(XMID, 0, NCH, t0, TG), wg[:], reads=["k:wg"],
               is_out=True)
    return em


def build_phase_d1(TG=256, Tt=T, em=None, io=None, B=None):
    if em is None:
        em = Em("D1")
        io = {"XNB": em.din("XNB", [D, Tt], BF16),
              "W1": em.din("W1", [D, 4096], F32),
              "B1": em.din("B1", [128, 32], F32),
              "HF": em.dout("HF", [4096, Tt], BF16)}
    NG = Tt // TG
    XNBD, W1D, B1D, HFD = io["XNB"], io["W1"], io["B1"], io["HF"]
    w1 = em.salloc("w1", [128, NCH * 4096], BF16)
    b1 = em.sb("b1", [128, 32], F32)
    xnbg = em.salloc("xnbg", [128, NCH * TG], BF16)
    hbuf = em.salloc("hbuf", [128, TG], BF16)
    psH = [em.ps(f"psH{i}", [128, 512]) for i in range(4)]
    for c in range(NCH):
        em.dma("g", w1[:, 4096 * c:4096 * (c + 1)],
               W1D[128 * c:128 * (c + 1), :], writes=[f"k:w1c{c}"])
    em.dma("y", b1[:], B1D, writes=["k:par"])
    for g in range(NG):
        t0 = g * TG
        em.dma("y", xnbg[:], fm_ap(XNBD, 0, NCH, t0, TG), writes=["k:xnbg"])
        for f in range(32):
            pw = psH[f % 4]
            for c in range(NCH):
                em.op("t", lambda t, f=f, c=c, pw=pw: t.matmul(
                    pw[:, 0:TG],
                    w1[:, 4096 * c + 128 * f:4096 * c + 128 * (f + 1)],
                    xnbg[:, c * TG:(c + 1) * TG],
                    start=(c == 0), stop=(c == NCH - 1)),
                    reads=["k:xnbg", f"k:w1c{c}"],
                    writes=[f"ps:H{f % 4}"] if c in (0, NCH - 1) else [],
                    inc=1 if c == NCH - 1 else 0)
            em.op("s", lambda s, f=f, pw=pw: s.activation(
                hbuf[:, 0:TG], pw[:, 0:TG], AF.Relu,
                bias=b1[:, f:f + 1], scale=1.0),
                reads=[f"ps:H{f % 4}", "k:par"], writes=["k:hbuf"])
            em.dma("y", HFD[128 * f:128 * (f + 1), t0:t0 + TG], hbuf[:, 0:TG],
                   reads=["k:hbuf"], is_out=True)
    return em


def build_phase_d2(TG=256, Tt=T, em=None, io=None, B=None):
    if em is None:
        em = Em("D2")
        io = {"HF": em.din("HF", [4096, Tt], BF16),
              "XNF": em.din("XNF", [D, Tt], F32),
              "W2": em.din("W2", [4096, D], F32),
              "B2": em.din("B2", [128, 8], F32),
              "GAM": em.din("GAM", [128, 8], F32),
              "BET": em.din("BET", [128, 8], F32),
              "XMID": em.din("XMID", [D, Tt], F32),
              "X1R": em.dout("X1R", [D, Tt], F32)}
    NG = Tt // TG
    HFD, XNFD, W2D, B2D, GAMD, BETD, XMIDD, X1R = (io[k] for k in (
        "HF", "XNF", "W2", "B2", "GAM", "BET", "XMID", "X1R"))
    w2 = em.salloc("w2", [128, 32 * 1024], BF16)
    b2 = em.sb("b2", [128, 8], F32)
    gam = em.sb("gam", [128, 8], F32)
    bet = em.sb("bet", [128, 8], F32)
    hfg = em.salloc("hfg", [128, 32 * TG], BF16)
    xnfg = em.salloc("xnfg", [128, TG], F32)
    xmg = [em.salloc(f"xmg{i}", [128, TG], F32) for i in range(2)]
    yg = em.salloc("yg", [128, NCH * TG], F32)
    ygb = em.salloc("ygb", [128, NCH * TG], BF16)
    if B is None:
        B = mk_ln_bufs(em, TG, 8)
    psO2 = [em.ps(f"psO2{i}", [128, 512]) for i in range(2)]
    for f in range(32):
        em.dma("g", w2[:, 1024 * f:1024 * (f + 1)],
               W2D[128 * f:128 * (f + 1), :], writes=[f"k:w2f{f}"])
    em.dma("y", b2[:], B2D, writes=["k:parb"])
    em.dma("y", gam[:], GAMD, writes=["k:parc"])
    em.dma("y", bet[:], BETD, writes=["k:pard"])
    for g in range(NG):
        t0 = g * TG
        em.dma("y", hfg[:], fm_ap(HFD, 0, 32, t0, TG), writes=["k:hfg"])
        for dc in range(NCH):
            for f in range(32):
                em.op("t", lambda t, f=f, dc=dc: t.matmul(
                    psO2[dc % 2][:, 0:TG],
                    w2[:, 1024 * f + 128 * dc:1024 * f + 128 * (dc + 1)],
                    hfg[:, f * TG:(f + 1) * TG],
                    start=(f == 0), stop=(f == 31)),
                    reads=["k:hfg", f"k:w2f{f}"],
                    writes=[f"ps:O2{dc % 2}"] if f in (0, 31) else [],
                    inc=1 if f == 31 else 0)
            ysl = yg[:, dc * TG:(dc + 1) * TG]
            em.dma("y", xnfg[:, 0:TG], fm_ap(XNFD, dc, 1, t0, TG),
                   writes=["k:xnfg"])
            em.op("v", lambda v, ysl=ysl, dc=dc: v.scalar_tensor_tensor(
                ysl, psO2[dc % 2][:, 0:TG], b2[:, dc:dc + 1],
                xnfg[:, 0:TG], ALU.add, ALU.add),
                reads=[f"ps:O2{dc % 2}", "k:parb", "k:xnfg"], writes=["k:yg"])
            em.op("s", lambda s, ysl=ysl, dc=dc: s.copy(
                ygb[:, dc * TG:(dc + 1) * TG], ysl),
                reads=["k:yg"], writes=["k:ygb"])
        emit_ln(em, B, ygb, "k:ygb", NCH, TG, "bc8", "f")
        rings = [(B["t1"], "k:t1"), (B["mu2"], "k:mu2")]
        for dc in range(NCH):
            ysl = yg[:, dc * TG:(dc + 1) * TG]
            tr, tk = rings[dc % 2]
            xmr, xk = xmg[dc % 2], f"k:xmg{dc % 2}"
            em.op("v", lambda v, ysl=ysl, tr=tr: v.tensor_tensor(
                tr[:, 0:TG], ysl, B["mub"][:, 0:TG], ALU.subtract),
                reads=["k:yg", "k:mub"], writes=[tk])
            em.op("v", lambda v, tr=tr: v.tensor_tensor(
                tr[:, 0:TG], tr[:, 0:TG], B["rstd"][:, 0:TG],
                ALU.mult),
                reads=[tk, "k:rstd"], writes=[tk])
            em.op("s", lambda s, dc=dc, tr=tr: s.activation(
                tr[:, 0:TG], tr[:, 0:TG], AF.Identity,
                bias=bet[:, dc:dc + 1], scale=gam[:, dc:dc + 1]),
                reads=[tk, "k:parc", "k:pard"], writes=[tk])
            em.dma("y", xmr[:, 0:TG], fm_ap(XMIDD, dc, 1, t0, TG),
                   writes=[xk])
            em.op("v", lambda v, ysl=ysl, tr=tr, xmr=xmr: v.tensor_tensor(
                ysl, tr[:, 0:TG], xmr[:, 0:TG], ALU.add),
                reads=[tk, xk], writes=["k:yg"])
        em.dma("y", fm_ap(X1R, 0, NCH, t0, TG), yg[:], reads=["k:yg"],
               is_out=True)
    return em

def build_phase_e(TG=256, Tt=T, em=None, io=None, B=None):
    if em is None:
        em = Em("E")
        io = {
            "X1RD": em.din("X1R", [D, Tt], F32),
            "WID": em.din("WI", [D, D], F32),
            "WOD": em.din("WO", [D, D], F32),
            "WVD": em.din("WV", [D, D], F32),
            "WAD": em.din("WA", [D, 16], F32),
            "WED": em.din("WE", [D, 16], F32),
            "BAD": em.din("BA", [16, 1], F32),
            "BED": em.din("BE", [16, 1], F32),
            "HEADIND": em.din("HEADIND", [128, 128], F32),
            "IDENT": em.din("IDENT", [128, 128], BF16),
            "XN1F": em.dout("XN1F", [D, Tt], F32),
            "XN1B": em.dout("XN1B", [D, Tt], BF16),
            "Q1H": em.dout("Q1H", [NH, DH, Tt], BF16),
            "K1H": em.dout("K1H", [NH, DH, Tt], BF16),
            "VSC1": em.dout("VSC1", [128, (Tt // 128) * 1056], BF16),
            "M1H": em.dout("M1H", [NH, Tt], BF16),
            "APH": em.dout("APH", [NH, Tt], F32),
        }
    NG = Tt // TG
    KTPG = TG // 128

    (X1RD, WID, WOD, WVD, WAD, WED, BAD, BED, HEADIND, IDENT, XN1F, XN1B, Q1H, K1H, VSC1, M1H, APH) = (io[k] for k in ("X1RD", "WID", "WOD", "WVD", "WAD", "WED", "BAD", "BED", "HEADIND", "IDENT", "XN1F", "XN1B", "Q1H", "K1H", "VSC1", "M1H", "APH",))
    wi = em.salloc("wi", [128, NCH * 1024], BF16)
    wo = em.salloc("wo", [128, NCH * 1024], BF16)
    wv = em.salloc("wv", [128, NCH * 1024], BF16)
    wa = em.salloc("wa", [128, NCH * 16], BF16)
    we = em.salloc("we", [128, NCH * 16], BF16)
    hind = em.sb("hind", [128, 128], BF16)
    idn = em.sb("idn", [128, 128], BF16)
    ba = em.sb("ba", [16, 1], F32)
    be = em.sb("be", [16, 1], F32)
    xg = em.salloc("xg", [128, NCH * TG], F32)
    xn1f = em.salloc("xn1f", [128, NCH * TG], F32)
    xn1b = em.salloc("xn1b", [128, NCH * TG], BF16)
    stage = em.salloc("stage", [128, TG], BF16)
    sqs = em.salloc("sqs", [128, TG], BF16)
    apg = em.salloc("apg", [16, TG], F32)
    exg = em.salloc("exg", [16, TG], BF16)
    ext = em.sb("ext", [128, KTPG * 16], F32)
    vsc1g = em.salloc("vsc1g", [128, KTPG * 1056], BF16)
    sq16 = em.salloc("sq16", [16, Tt], BF16)
    sk16 = em.salloc("sk16", [16, Tt], BF16)
    kmx = em.sb("kmx", [16, 1], F32)
    nkm = em.sb("nkm", [16, 1], F32)
    srt = em.salloc("srt", [16, Tt], BF16)
    m1r = em.salloc("m1r", [16, Tt], BF16)
    if B is None:
        B = mk_ln_bufs(em, TG, 8)
    psM = [em.ps(f"psM{i}", [128, 512]) for i in range(2)]
    psAE = em.ps("psAE", [16, 512])
    psSq = em.ps("psSq", [16, 512])
    psTe = em.ps("psTe", [128, 1024], BF16)

    for c in range(NCH):
        em.dma("g", wi[:, 1024 * c:1024 * (c + 1)],
               WID[128 * c:128 * (c + 1), :], writes=["k:wi"])
        em.dma("g", wo[:, 1024 * c:1024 * (c + 1)],
               WOD[128 * c:128 * (c + 1), :], writes=["k:wo"])
        em.dma("g", wv[:, 1024 * c:1024 * (c + 1)],
               WVD[128 * c:128 * (c + 1), :], writes=["k:wv"])
        em.dma("g", wa[:, 16 * c:16 * (c + 1)],
               WAD[128 * c:128 * (c + 1), :], writes=["k:wa"])
        em.dma("g", we[:, 16 * c:16 * (c + 1)],
               WED[128 * c:128 * (c + 1), :], writes=["k:we"])
    em.dma("g", hind[:], HEADIND, writes=["k:hind"])
    em.dma("g", idn[:], IDENT, writes=["k:idn"])
    em.dma("y", ba[:], BAD, writes=["k:par"])
    em.dma("y", be[:], BED, writes=["k:parb"])
    em.op("v", lambda v: v.memset(vsc1g[:], 0.0), writes=["k:vsc1g"])
    em.op("v", lambda v: v.memset(
        vsc1g[:, 65:1056 * KTPG:66], 1.0), writes=["k:vsc1g"])

    for g in range(NG):
        t0 = g * TG
        em.dma("y", xg[:], fm_ap(X1RD, 0, NCH, t0, TG), writes=["k:xg"])
        emit_ln(em, B, xg, "k:xg", NCH, TG, "bc8", "e",
                src_is_f32=True)
        for c in range(NCH):
            emit_ln_apply(em, B, xg, "k:xg", c, TG,
                          xn1f[:, c * TG:(c + 1) * TG], "k:xn1f",
                          out2=xn1b[:, c * TG:(c + 1) * TG], out2_key="k:xn1b")
        em.dma("y", fm_ap(XN1F, 0, NCH, t0, TG), xn1f[:], reads=["k:xn1f"],
               is_out=True)
        em.dma("y", fm_ap(XN1B, 0, NCH, t0, TG), xn1b[:], reads=["k:xn1b"],
               is_out=True)
        for w_, bb, dst, dkey, wkey in ((wa, ba, apg, "k:apg", "k:wa"),
                                        (we, be, exg, "k:exg", "k:we")):
            for c in range(NCH):
                em.op("t", lambda t, w_=w_, c=c: t.matmul(
                    psAE[0:16, 0:TG], w_[:, 16 * c:16 * (c + 1)],
                    xn1b[:, c * TG:(c + 1) * TG],
                    start=(c == 0), stop=(c == NCH - 1)),
                    reads=["k:xn1b", wkey],
                    writes=["ps:AE"] if c in (0, NCH - 1) else [],
                    inc=1 if c == NCH - 1 else 0)
            em.op("s", lambda s, bb=bb, dst=dst, dkey=dkey: s.activation(
                dst[0:16, 0:TG], psAE[0:16, 0:TG], AF.Sigmoid,
                bias=bb[0:16, 0:1], scale=1.0),
                reads=["ps:AE", "k:par", "k:parb"], writes=[dkey])
        em.dma("y", APH[0:16, t0:t0 + TG], apg[0:16, 0:TG], reads=["k:apg"],
               is_out=True)
        for tt in range(KTPG):
            em.op("t", lambda t, tt=tt: t.transpose(
                psTe[0:128, 0:16], exg[0:16, 128 * tt:128 * (tt + 1)],
                idn[0:16, 0:16]),
                reads=["k:exg", "k:idn"], writes=["ps:Te"])
            em.op("s", lambda s, tt=tt: s.copy(
                ext[:, 16 * tt:16 * (tt + 1)], psTe[0:128, 0:16]),
                reads=["ps:Te"], writes=["k:ext"])
        for which, wmat, wkey, DST, SQ16 in (
                ("q", wi, "k:wi", Q1H, sq16), ("k", wo, "k:wo", K1H, sk16)):
            for dc in range(NCH):
                pm = psM[dc % 2]
                for c in range(NCH):
                    em.op("t", lambda t, wmat=wmat, c=c, dc=dc, pm=pm:
                          t.matmul(pm[:, 0:TG],
                                   wmat[:, 1024 * c + 128 * dc:
                                        1024 * c + 128 * (dc + 1)],
                                   xn1b[:, c * TG:(c + 1) * TG],
                                   start=(c == 0), stop=(c == NCH - 1)),
                          reads=["k:xn1b", wkey],
                          writes=[f"ps:M{dc % 2}"] if c in (0, NCH - 1) else [],
                          inc=1 if c == NCH - 1 else 0)
                sc = 0.125 if which == "q" else 1.0
                em.op("s", lambda s, pm=pm, sc=sc: s.activation(
                    stage[:, 0:TG], pm[:, 0:TG], AF.Copy, bias=0.0, scale=sc),
                    reads=[f"ps:M{dc % 2}"], writes=["k:stage"])
                em.dma("y", DST[2 * dc, :, t0:t0 + TG], stage[0:64, 0:TG],
                       reads=["k:stage"], is_out=True)
                em.dma("y", DST[2 * dc + 1, :, t0:t0 + TG], stage[64:128, 0:TG],
                       reads=["k:stage"], is_out=True)
                em.op("s", lambda s: s.activation(
                    sqs[:, 0:TG], stage[:, 0:TG], AF.Square, bias=0.0,
                    scale=1.0),
                    reads=["k:stage"], writes=["k:sqs"])
                em.op("t", lambda t, dc=dc: t.matmul(
                    psSq[0:16, 0:TG], hind[:, 16 * dc:16 * (dc + 1)],
                    sqs[:, 0:TG], start=(dc == 0), stop=(dc == NCH - 1)),
                    reads=["k:sqs", "k:hind"],
                    writes=["ps:Sq"] if dc in (0, NCH - 1) else [], inc=1)
            em.op("s", lambda s, SQ16=SQ16, t0=t0: s.copy(
                SQ16[0:16, t0:t0 + TG], psSq[0:16, 0:TG]),
                reads=["ps:Sq"], writes=["k:sq16"])
        for tt in range(KTPG):
            for vcg in range(2):
                pm = psM[vcg]
                for c in range(NCH):
                    em.op("t", lambda t, c=c, tt=tt, vcg=vcg, pm=pm: t.matmul(
                        pm[:, 0:512],
                        xn1b[:, c * TG + 128 * tt:c * TG + 128 * (tt + 1)],
                        wv[:, 1024 * c + 512 * vcg:1024 * c + 512 * (vcg + 1)],
                        start=(c == 0), stop=(c == NCH - 1)),
                        reads=["k:xn1b", "k:wv"],
                        writes=[f"ps:M{vcg}"] if c in (0, NCH - 1) else [],
                        inc=1 if c == NCH - 1 else 0)
                for j in range(8):
                    h = 8 * vcg + j
                    em.op("v", lambda v, tt=tt, h=h, j=j, pm=pm: v.tensor_scalar(
                        vsc1g[:, 1056 * tt + 66 * h:1056 * tt + 66 * h + 64],
                        pm[:, 64 * j:64 * (j + 1)],
                        ext[:, 16 * tt + h:16 * tt + h + 1], None, ALU.mult),
                        reads=[f"ps:M{vcg}", "k:ext"], writes=["k:vsc1g"])
                    em.op("s", lambda s, tt=tt, h=h: s.copy(
                        vsc1g[:, 1056 * tt + 66 * h + 64:
                              1056 * tt + 66 * h + 65],
                        ext[:, 16 * tt + h:16 * tt + h + 1]),
                        reads=["k:ext"], writes=["k:vsc1g"])
        em.dma("y", VSC1[:, 1056 * KTPG * g:1056 * KTPG * (g + 1)], vsc1g[:],
               reads=["k:vsc1g"], is_out=True)
    em.op("v", lambda v: v.reduce_max(kmx[:], sk16[0:16, :], axis=AX.X),
          reads=["k:sq16"], writes=["k:kmx"])
    em.op("s", lambda s: s.activation(kmx[:], kmx[:], AF.Sqrt, bias=0.0,
                                      scale=1.0),
          reads=["k:kmx"], writes=["k:kmx"])
    em.op("v", lambda v: v.tensor_scalar(nkm[:], kmx[:], -1.0, None, ALU.mult),
          reads=["k:kmx"], writes=["k:nkm"])
    em.op("s", lambda s: s.activation(srt[0:16, :], sq16[0:16, :], AF.Sqrt,
                                      bias=0.0, scale=1.0),
          reads=["k:sq16"], writes=["k:srt"])
    em.op("v", lambda v: v.tensor_scalar(m1r[0:16, :], srt[0:16, :], nkm[:],
                                         CSH, ALU.mult, ALU.add),
          reads=["k:srt", "k:nkm"], writes=["k:m1r"])
    em.dma("y", M1H[0:16, :], m1r[0:16, :], reads=["k:m1r"], is_out=True)
    return em


def build_phase_f(Tt=T, em=None, io=None, B=None):
    if em is None:
        em = Em("F")
        io = {
            "Q1HD": em.din("Q1H", [NH, DH, Tt], BF16),
            "K1HD": em.din("K1H", [NH, DH, Tt], BF16),
            "VSC1D": em.din("VSC1", [128, NKT * 1056], BF16),
            "M1HD": em.din("M1H", [NH, Tt], BF16),
            "APHD": em.din("APH", [NH, Tt], F32),
            "O1H": em.dout("O1H", [NH, DH, Tt], BF16),
        }
    NQG = Tt // 512
    NKT = Tt // 128

    (Q1HD, K1HD, VSC1D, M1HD, APHD, O1H) = (io[k] for k in ("Q1HD", "K1HD", "VSC1D", "M1HD", "APHD", "O1H",))
    vsc = em.salloc("vsc", [128, NKT * 1056], BF16)
    kb = [em.salloc(f"kb{i}", [65, Tt], BF16) for i in range(2)]
    qb = [em.salloc(f"qb{i}", [65, Tt], BF16) for i in range(2)]
    wb = em.salloc("wb", [128, 512], F32)
    ap0 = em.salloc("ap0", [1, 512], F32)
    onesc = em.sb("onesc", [128, 1], BF16)
    onesr = em.sb("onesr", [1, 64], BF16)
    abuf = em.salloc("abuf", [128, 4 * 512], BF16)
    szb = em.salloc("szb", [128, 512], BF16)
    rr = em.salloc("rr", [1, 512], F32)
    frow = em.salloc("frow", [1, 512], BF16)
    fbc = em.salloc("fbc", [64, 512], BF16)
    obuf = em.salloc("obuf", [64, 512], BF16)
    psS = [em.ps(f"psS{i}", [128, 512]) for i in range(3)]
    psO = [em.ps(f"psO{i}", [128, 512]) for i in range(2)]
    psB = em.ps("psB", [128, 512])
    psD = psB

    em.dma("y", vsc[:], VSC1D, writes=["k:vsc"])
    em.op("v", lambda v: v.memset(onesc[:], 1.0), writes=["k:ones"])
    em.op("v", lambda v: v.memset(onesr[:], 1.0), writes=["k:onesr"])
    em.op("v", lambda v: v.memset(wb[64:66, :], 1e-8), writes=["k:wb"])
    for i in range(2):
        em.op("v", lambda v, i=i: v.memset(kb[i][64:65, :], 1.0),
              writes=[f"k:kb{i}"])

    for h in range(NH):
        par = h % 2
        kbu, qbu = kb[par], qb[par]
        kkey, qkey = f"k:kb{par}", f"k:qb{par}"
        em.dma("y", kbu[0:64, :], K1HD[h], writes=[kkey])
        em.dma("y", qbu[0:64, :], Q1HD[h], writes=[qkey])
        em.dma("y", qbu[64:65, :], M1HD[h:h + 1, :], writes=[qkey])

        for qg in range(NQG):
            LAG = 2
            pso = psO[qg % 2]
            pokey = f"ps:O{qg % 2}"

            def emit_o(kt, h=h, pso=pso, pokey=pokey):
                asl = abuf[:, 512 * (kt % 3):512 * (kt % 3 + 1)]
                em.op("t", lambda t, asl=asl, kt=kt, h=h, pso=pso: t.matmul(
                    pso[0:66, 0:512],
                    vsc[:, 1056 * kt + 66 * h:1056 * kt + 66 * h + 66],
                    asl, start=(kt == 0), stop=(kt == NKT - 1)),
                    reads=[f"k:ab{kt % 3}", "k:vsc"],
                    writes=[pokey] if kt in (0, NKT - 1) else [],
                    inc=1 if kt == NKT - 1 else 0)

            for kt in range(NKT):
                ss = psS[kt % 3]
                em.op("t", lambda t, ss=ss, kt=kt, qg=qg, kbu=kbu, qbu=qbu:
                      t.matmul(ss[:, 0:512],
                               kbu[0:65, 128 * kt:128 * (kt + 1)],
                               qbu[0:65, 512 * qg:512 * (qg + 1)],
                               start=True, stop=True),
                      reads=[kkey, qkey], writes=[f"ps:S{kt % 3}"])
                asl = abuf[:, 512 * (kt % 3):512 * (kt % 3 + 1)]
                em.op("s", lambda s, ss=ss, asl=asl: s.activation(
                    asl, ss[:, 0:512], AF.Exp, bias=0.0, scale=1.0),
                    reads=[f"ps:S{kt % 3}"], writes=[f"k:ab{kt % 3}"])
                if kt >= LAG:
                    emit_o(kt - LAG)
            for kt in range(NKT - LAG, NKT):
                emit_o(kt)
            em.dma("s", wb[64:65, 0:512],
                   APHD[h:h + 1, 512 * qg:512 * (qg + 1)], writes=["k:wb"])
            em.dma("s", ap0[0:1, 0:512],
                   APHD[h:h + 1, 512 * qg:512 * (qg + 1)], writes=["k:ap0"])
            em.op("s", lambda s, pso=pso: s.copy(
                szb[64:66, 0:512], pso[64:66, 0:512]),
                  reads=[pokey], writes=["k:szb"])
            em.op("v", lambda v, qg=qg: v.tensor_tensor(
                szb[64:66, 0:512], szb[64:66, 0:512],
                wb[64:66, 0:512], ALU.mult),
                reads=["k:szb", "k:wb"], writes=["k:szb"])
            em.op("t", lambda t: t.matmul(psB[0:1, 0:512], onesc[64:66, 0:1],
                                          szb[64:66, 0:512],
                                          start=True, stop=True),
                  reads=["k:szb", "k:ones"], writes=["ps:B"])
            em.op("v", lambda v: v.reciprocal(rr[0:1, :], psB[0:1, 0:512]),
                  reads=["ps:B"], writes=["k:rr"])
            em.op("v", lambda v, qg=qg: v.tensor_tensor(
                rr[0:1, :], rr[0:1, :], ap0[0:1, 0:512], ALU.mult),
                reads=["k:rr", "k:ap0"], writes=["k:rr"])
            em.op("s", lambda s: s.copy(frow[0:1, :], rr[0:1, :]),
                  reads=["k:rr"], writes=["k:frow"])
            em.op("t", lambda t: t.matmul(psB[0:64, 0:512], onesr[0:1, 0:64],
                                          frow[0:1, 0:512],
                                          start=True, stop=True),
                  reads=["k:frow", "k:onesr"], writes=["ps:B"])
            em.op("s", lambda s: s.copy(fbc[0:64, :], psB[0:64, 0:512]),
                  reads=["ps:B"], writes=["k:fbc"])
            em.op("v", lambda v, pso=pso: v.tensor_tensor(
                obuf[:], pso[0:64, 0:512], fbc[:], ALU.mult),
                  reads=[pokey, "k:fbc"], writes=["k:obuf"])
            em.dma("y", O1H[h, :, 512 * qg:512 * (qg + 1)], obuf[:],
                   reads=["k:obuf"], is_out=True)
    return em


def build_phase_g(TG=256, Tt=T, em=None, io=None, B=None):
    if em is None:
        em = Em("G")
        io = {
            "O1HD": em.din("O1H", [NH, DH, Tt], BF16),
            "WOUT1D": em.din("WOUT1", [64, 16 * 1024], F32),
            "BETA_P": em.din("BETA_P", [1, 16], F32),
            "IV_P": em.din("IV_P", [1, 16], F32),
            "OV_P": em.din("OV_P", [1, 16], F32),
            "CHI": em.din("CHI", [1, 16], F32),
            "SIGN64": em.din("SIGN64", [64, 1], F32),
            "X1RD": em.din("X1R", [D, Tt], F32),
            "XMID1": em.dout("XMID1", [D, Tt], F32),
        }
    NG = Tt // TG

    (O1HD, WOUT1D, BETA_P, IV_P, OV_P, CHI, SIGN64, X1RD, XMID1) = (io[k] for k in ("O1HD", "WOUT1D", "BETA_P", "IV_P", "OV_P", "CHI", "SIGN64", "X1RD", "XMID1",))
    wo1 = em.salloc("wo1", [64, 1024], F32)
    w1o = em.salloc("w1o", [64, 16 * 1024], BF16)
    sgn = em.sb("sgn", [64, 1], F32)
    rows = em.sb("rows", [1, 16 * 6], F32)
    onesf = em.sb("onesf", [1, 128], F32)
    acol = em.sb("acol", [64, 16], F32)
    bcol = em.sb("bcol", [64, 16], F32)
    rt = em.salloc("rt", [64, 1024], F32)
    t1g = em.salloc("t1g", [64, 1024], F32)
    ohg = em.salloc("ohg", [64, 16 * TG], BF16)
    xg = em.salloc("xg", [128, TG], F32)
    og = em.salloc("og", [128, NCH * TG], F32)
    psA = [em.ps(f"psA{i}", [128, 512]) for i in range(2)]
    psBc = em.ps("psBc", [128, 512])

    em.dma("y", sgn[:], SIGN64, writes=["k:sgn"])
    em.dma("y", rows[0:1, 0:16], BETA_P, writes=["k:rows"])
    em.dma("y", rows[0:1, 16:32], IV_P, writes=["k:rows2"])
    em.dma("y", rows[0:1, 32:48], OV_P, writes=["k:rows3"])
    em.dma("y", rows[0:1, 48:64], CHI, writes=["k:rows4"])
    em.op("v", lambda v: v.memset(onesf[:], 1.0), writes=["k:onesf"])
    RK = ["k:rows", "k:rows2", "k:rows3", "k:rows4"]
    em.op("s", lambda s: s.activation(rows[0:1, 0:48], rows[0:1, 0:48],
                                      AF.Sigmoid, bias=0.0, scale=1.0),
          reads=RK, writes=RK)
    em.op("s", lambda s: s.activation(rows[0:1, 48:64], rows[0:1, 48:64],
                                      AF.Tanh, bias=0.0, scale=1.0),
          reads=RK, writes=RK)
    import math
    em.op("v", lambda v: v.tensor_scalar(rows[0:1, 64:80], rows[0:1, 0:16],
                                         -math.pi, math.pi / 2.0,
                                         ALU.mult, ALU.add),
          reads=RK, writes=["k:rows5"])
    em.op("s", lambda s: s.activation(rows[0:1, 64:80], rows[0:1, 64:80],
                                      AF.Sin, bias=0.0, scale=1.0),
          reads=["k:rows5"], writes=["k:rows5"])
    em.op("v", lambda v: v.tensor_scalar(rows[0:1, 80:96], rows[0:1, 0:16],
                                         math.pi, None, ALU.mult),
          reads=RK, writes=["k:rows6"])
    em.op("s", lambda s: s.activation(rows[0:1, 80:96], rows[0:1, 80:96],
                                      AF.Sin, bias=0.0, scale=1.0),
          reads=["k:rows6"], writes=["k:rows6"])
    for dst, key in ((64, "k:rows5"), (80, "k:rows6")):
        for off in (16, 32, 48):
            em.op("v", lambda v, dst=dst, off=off: v.tensor_tensor(
                rows[0:1, dst:dst + 16], rows[0:1, dst:dst + 16],
                rows[0:1, off:off + 16], ALU.mult),
                reads=RK + [key], writes=[key])
    em.op("t", lambda t: t.matmul(psBc[0:64, 0:16], onesf[0:1, 0:64],
                                  rows[0:1, 64:80], start=True, stop=True),
          reads=["k:rows5", "k:onesf"], writes=["ps:Bc"])
    em.op("s", lambda s: s.copy(acol[:], psBc[0:64, 0:16]),
          reads=["ps:Bc"], writes=["k:acol"])
    em.op("t", lambda t: t.matmul(psBc[0:64, 0:16], onesf[0:1, 0:64],
                                  rows[0:1, 80:96], start=True, stop=True),
          reads=["k:rows6", "k:onesf"], writes=["ps:Bc"])
    em.op("s", lambda s: s.copy(bcol[:], psBc[0:64, 0:16]),
          reads=["ps:Bc"], writes=["k:bcol"])
    em.op("v", lambda v: v.tensor_scalar(bcol[:], bcol[:], sgn[:], None,
                                         ALU.mult),
          reads=["k:bcol", "k:sgn"], writes=["k:bcol"])
    for h in range(NH):
        em.dma("y", wo1[:], WOUT1D[:, 1024 * h:1024 * (h + 1)],
               writes=["k:wo1"])
        em.op("s", lambda s: s.copy(rt[0:32, 0:1024], wo1[32:64, 0:1024]),
              reads=["k:wo1"], writes=["k:rt"])
        em.op("s", lambda s: s.copy(rt[32:64, 0:1024], wo1[0:32, 0:1024]),
              reads=["k:wo1"], writes=["k:rt"])
        em.op("v", lambda v, h=h: v.tensor_scalar(
            t1g[:, 0:1024], wo1[0:64, 0:1024], acol[:, h:h + 1], None,
            ALU.mult),
            reads=["k:wo1", "k:acol"], writes=["k:t1g"])
        em.op("v", lambda v, h=h: v.scalar_tensor_tensor(
            w1o[0:64, 1024 * h:1024 * (h + 1)], rt[0:64, 0:1024],
            bcol[:, h:h + 1], t1g[:, 0:1024], ALU.mult, ALU.add),
            reads=["k:rt", "k:bcol", "k:t1g"], writes=["k:w1o2"])
    for g in range(NG):
        t0 = g * TG
        for h in range(NH):
            em.dma("y", ohg[:, TG * h:TG * (h + 1)], O1HD[h, :, t0:t0 + TG],
                   writes=["k:ohg"])
        for dc in range(NCH):
            pa = psA[dc % 2]
            for h in range(NH):
                em.op("t", lambda t, h=h, dc=dc, pa=pa: t.matmul(
                    pa[0:128, 0:TG],
                    w1o[0:64, 1024 * h + 128 * dc:1024 * h + 128 * (dc + 1)],
                    ohg[0:64, TG * h:TG * (h + 1)],
                    start=(h == 0), stop=(h == NH - 1)),
                    reads=["k:ohg", "k:w1o2"],
                    writes=[f"ps:A{dc % 2}"] if h in (0, NH - 1) else [],
                    inc=1 if h == NH - 1 else 0)
            em.dma("y", xg[:, 0:TG], fm_ap(X1RD, dc, 1, t0, TG),
                   writes=["k:xg"])
            em.op("v", lambda v, dc=dc, pa=pa: v.tensor_tensor(
                og[:, dc * TG:(dc + 1) * TG], pa[0:128, 0:TG],
                xg[:, 0:TG], ALU.add),
                reads=[f"ps:A{dc % 2}", "k:xg"], writes=["k:og"])
        em.dma("y", fm_ap(XMID1, 0, NCH, t0, TG), og[:], reads=["k:og"],
               is_out=True)
    return em


def build_phase_h1a(TG=256, Tt=T, em=None, io=None, B=None):
    if em is None:
        em = Em("H1a")
        io = {"XN1B": em.din("XN1B", [D, Tt], BF16),
              "FW1T": em.din("FW1T", [D, 4096], F32),
              "FW1S": em.din("FW1S", [128, 32], F32),
              "G1S": em.dout("G1S", [4096, Tt], BF16)}
    NG = Tt // TG
    XN1BD, FW1T, FW1S, G1S = io["XN1B"], io["FW1T"], io["FW1S"], io["G1S"]
    w1 = em.salloc("w1", [128, NCH * 4096], BF16)
    s1 = em.sb("s1", [128, 32], F32)
    xnbg = em.salloc("xnbg", [128, NCH * TG], BF16)
    g1 = em.salloc("g1", [128, TG], BF16)
    psH = [em.ps(f"psH{i}", [128, 512]) for i in range(4)]
    for c in range(NCH):
        em.dma("g", w1[:, 4096 * c:4096 * (c + 1)],
               FW1T[128 * c:128 * (c + 1), :], writes=[f"k:w1c{c}"])
    em.dma("y", s1[:], FW1S, writes=["k:par"])
    for g in range(NG):
        t0 = g * TG
        em.dma("y", xnbg[:], fm_ap(XN1BD, 0, NCH, t0, TG), writes=["k:xnbg"])
        for f in range(32):
            pw = psH[f % 4]
            for c in range(NCH):
                em.op("t", lambda t, c=c, f=f, pw=pw: t.matmul(
                    pw[:, 0:TG],
                    w1[:, 4096 * c + 128 * f:4096 * c + 128 * (f + 1)],
                    xnbg[:, c * TG:(c + 1) * TG],
                    start=(c == 0), stop=(c == NCH - 1)),
                    reads=["k:xnbg", f"k:w1c{c}"],
                    writes=[f"ps:H{f % 4}"] if c in (0, NCH - 1) else [],
                    inc=1 if c == NCH - 1 else 0)
            em.op("s", lambda s, f=f, pw=pw: s.activation(
                g1[:, 0:TG], pw[:, 0:TG], AF.Silu, bias=0.0,
                scale=s1[:, f:f + 1]),
                reads=[f"ps:H{f % 4}", "k:par"], writes=["k:g1"])
            em.dma("y", G1S[128 * f:128 * (f + 1), t0:t0 + TG], g1[:, 0:TG],
                   reads=["k:g1"], is_out=True)
    return em


def build_phase_h1b(TG=256, Tt=T, em=None, io=None, B=None):
    if em is None:
        em = Em("H1b")
        io = {"XN1B": em.din("XN1B", [D, Tt], BF16),
              "G1S": em.din("G1S", [4096, Tt], BF16),
              "FW2T": em.din("FW2T", [D, 4096], F32),
              "FW2S": em.din("FW2S", [128, 32], F32),
              "H1S": em.dout("H1S", [4096, Tt], BF16)}
    NG = Tt // TG
    XN1BD, G1SD, FW2T, FW2S, H1S = (io[k] for k in (
        "XN1B", "G1S", "FW2T", "FW2S", "H1S"))
    w2 = em.salloc("w2", [128, NCH * 4096], BF16)
    s2 = em.sb("s2", [128, 32], F32)
    xnbg = em.salloc("xnbg", [128, NCH * TG], BF16)
    g1g = em.salloc("g1g", [128, 32 * TG], BF16)
    g2 = em.salloc("g2", [128, TG], F32)
    hbuf = em.salloc("hbuf", [128, TG], BF16)
    psH = [em.ps(f"psH{i}", [128, 512]) for i in range(2)]
    for c in range(NCH):
        em.dma("g", w2[:, 4096 * c:4096 * (c + 1)],
               FW2T[128 * c:128 * (c + 1), :], writes=[f"k:w2c{c}"])
    em.dma("y", s2[:], FW2S, writes=["k:par"])
    for g in range(NG):
        t0 = g * TG
        em.dma("y", xnbg[:], fm_ap(XN1BD, 0, NCH, t0, TG), writes=["k:xnbg"])
        em.dma("y", g1g[:], fm_ap(G1SD, 0, 32, t0, TG), writes=["k:g1g"])
        for f in range(32):
            pw = psH[f % 2]
            for c in range(NCH):
                em.op("t", lambda t, c=c, f=f, pw=pw: t.matmul(
                    pw[:, 0:TG],
                    w2[:, 4096 * c + 128 * f:4096 * c + 128 * (f + 1)],
                    xnbg[:, c * TG:(c + 1) * TG],
                    start=(c == 0), stop=(c == NCH - 1)),
                    reads=["k:xnbg", f"k:w2c{c}"],
                    writes=[f"ps:H{f % 2}"] if c in (0, NCH - 1) else [],
                    inc=1 if c == NCH - 1 else 0)
            em.op("v", lambda v, f=f, pw=pw: v.tensor_scalar(
                g2[:, 0:TG], pw[:, 0:TG], s2[:, f:f + 1], None, ALU.mult),
                reads=[f"ps:H{f % 2}", "k:par"], writes=["k:g2"])
            em.op("v", lambda v, f=f: v.tensor_tensor(
                hbuf[:, 0:TG], g1g[:, f * TG:(f + 1) * TG], g2[:, 0:TG],
                ALU.mult),
                reads=["k:g1g", "k:g2"], writes=["k:hbuf"])
            em.dma("y", H1S[128 * f:128 * (f + 1), t0:t0 + TG], hbuf[:, 0:TG],
                   reads=["k:hbuf"], is_out=True)
    return em

def build_phase_h2(name, TG=256, Tt=T, em=None, io=None, B=None):
    _mk = em is None
    if _mk:
        em = Em(name)
    NG = Tt // TG
    if _mk:
        io = {"HIN": em.din("HIN", [4096, Tt], BF16),
              "WT": em.din("WT", [4096, 4096], F32),
              "SC": em.din("SC", [128, 32], F32),
              "HOUT": em.dout("HOUT", [4096, Tt], BF16)}
    HIN, WT, SC, HOUT = io["HIN"], io["WT"], io["SC"], io["HOUT"]
    wblk = [em.salloc(f"wblk{i}", [128, 32 * 512], BF16) for i in range(2)]
    sc = em.sb("sc", [128, 32], F32)
    hing = [em.salloc(f"hing{i}", [128, 32 * TG], BF16) for i in range(2)]
    hbuf = em.salloc("hbuf", [128, TG], BF16)
    psH = [em.ps(f"psH{i}", [128, 512]) for i in range(4)]
    em.dma("y", sc[:], SC, writes=["k:par"])
    for ob in range(8):
        wb_ = wblk[ob % 2]
        wkey = f"k:wb{ob % 2}"
        for c in range(32):
            em.dma("g", wb_[:, 512 * c:512 * (c + 1)],
                   WT[128 * c:128 * (c + 1), 512 * ob:512 * (ob + 1)],
                   writes=[wkey])
        for g in range(NG):
            t0 = g * TG
            hg = hing[g % 2]
            hkey = f"k:hg{g % 2}"
            em.dma("y", hg[:], fm_ap(HIN, 0, 32, t0, TG), writes=[hkey])
            for sub in range(4):
                pp = psH[sub % 4]
                for c in range(32):
                    em.op("t", lambda t, c=c, sub=sub, wb_=wb_, hg=hg, pp=pp:
                          t.matmul(pp[:, 0:TG],
                                   wb_[:, 512 * c + 128 * sub:
                                       512 * c + 128 * (sub + 1)],
                                   hg[:, c * TG:(c + 1) * TG],
                                   start=(c == 0), stop=(c == 31)),
                          reads=[hkey, wkey],
                          writes=[f"ps:H{sub % 4}"] if c in (0, 31) else [],
                          inc=1 if c == 31 else 0)
                fo = 4 * ob + sub
                em.op("s", lambda s, pp=pp, fo=fo, sub=sub: s.activation(
                    hbuf[:, 0:TG], pp[:, 0:TG], AF.Gelu, bias=0.0,
                    scale=sc[:, fo:fo + 1]),
                    reads=[f"ps:H{sub % 4}", "k:par"], writes=["k:hbuf"])
                em.dma("y", HOUT[512 * ob + 128 * sub:512 * ob + 128 * (sub + 1),
                                 t0:t0 + TG], hbuf[:, 0:TG],
                       reads=["k:hbuf"], is_out=True)
    return em


def build_phase_h4(TG=256, Tt=T, em=None, io=None, B=None):
    if em is None:
        em = Em("H4")
        io = {
            "H3S": em.din("H3S", [4096, Tt], BF16),
            "FW3T": em.din("FW3T", [4096, D], F32),
            "FW3S": em.din("FW3S", [128, 8], F32),
            "XMID1D": em.din("XMID1", [D, Tt], F32),
            "YF": em.dout("YF", [D, Tt], F32),
        }
    NG = Tt // TG
    (H3S, FW3T, FW3S, XMID1D, YF) = (io[k] for k in ("H3S", "FW3T", "FW3S", "XMID1D", "YF",))
    w3 = em.salloc("w3", [128, 32 * 1024], BF16)
    s3 = em.sb("s3", [128, 8], F32)
    hg = em.salloc("hg", [128, 32 * TG], BF16)
    xmg = em.salloc("xmg", [128, TG], F32)
    og = em.salloc("og", [128, NCH * TG], F32)
    psH = [em.ps(f"psH{i}", [128, 512]) for i in range(4)]
    for f in range(32):
        em.dma("g", w3[:, 1024 * f:1024 * (f + 1)],
               FW3T[128 * f:128 * (f + 1), :], writes=[f"k:w3f{f}"])
    em.dma("y", s3[:], FW3S, writes=["k:par"])
    for g in range(NG):
        t0 = g * TG
        em.dma("y", hg[:], fm_ap(H3S, 0, 32, t0, TG), writes=["k:hg"])
        for dc in range(NCH):
            pp = psH[dc % 4]
            for f in range(32):
                em.op("t", lambda t, f=f, dc=dc, pp=pp: t.matmul(
                    pp[:, 0:TG],
                    w3[:, 1024 * f + 128 * dc:1024 * f + 128 * (dc + 1)],
                    hg[:, f * TG:(f + 1) * TG],
                    start=(f == 0), stop=(f == 31)),
                    reads=["k:hg", f"k:w3f{f}"],
                    writes=[f"ps:H{dc % 4}"] if f in (0, 31) else [],
                    inc=1 if f == 31 else 0)
            em.dma("y", xmg[:, 0:TG], fm_ap(XMID1D, dc, 1, t0, TG),
                   writes=["k:xmg"])
            em.op("v", lambda v, dc=dc, pp=pp: v.scalar_tensor_tensor(
                og[:, dc * TG:(dc + 1) * TG], pp[:, 0:TG], s3[:, dc:dc + 1],
                xmg[:, 0:TG], ALU.mult, ALU.add),
                reads=[f"ps:H{dc % 4}", "k:par", "k:xmg"], writes=["k:og"])
        em.dma("y", fm_ap(YF, 0, NCH, t0, TG), og[:], reads=["k:og"],
               is_out=True)
    return em

SHARED_INPUTS = [
    ("WINT", [D, QKVP], F32), ("LNG_IN", [128, 8], F32),
    ("LNB_IN", [128, 8], F32), ("MIDG", [128, 56], F32),
    ("MIDB", [128, 56], F32), ("SHIFT1", [128, 128], BF16),
    ("OH127", [128, 128], BF16), ("IDENT", [128, 128], BF16),
    ("SLOPES", [1, 16], F32), ("IOTA128", [128, 1], F32),
    ("IOTAT", [1, T], F32), ("PREMASK", [512, 512], BF16),
    ("PREMASKT", [512, 512], BF16), ("WOT_O", [64, 16 * 1024], F32),
    ("WOT_PG", [4096, 1024], F32), ("OUTG", [128, 8], F32),
    ("OUTB", [128, 8], F32), ("W1", [D, 4096], F32), ("B1", [128, 32], F32),
    ("W2", [4096, D], F32), ("B2", [128, 8], F32), ("GAM", [128, 8], F32),
    ("BET", [128, 8], F32), ("WI", [D, D], F32), ("WO", [D, D], F32),
    ("WV", [D, D], F32), ("WA", [D, 16], F32), ("WE", [D, 16], F32),
    ("BA", [16, 1], F32), ("BE", [16, 1], F32), ("HEADIND", [128, 128], F32),
    ("WOUT1", [64, 16 * 1024], F32), ("BETA_P", [1, 16], F32),
    ("IV_P", [1, 16], F32), ("OV_P", [1, 16], F32), ("CHI", [1, 16], F32),
    ("SIGN64", [64, 1], F32), ("FW1T", [D, 4096], F32),
    ("FW1S", [128, 32], F32), ("FW2T", [D, 4096], F32),
    ("FW2S", [128, 32], F32), ("FH1T", [4096, 4096], F32),
    ("FH1S", [128, 32], F32), ("FH2T", [4096, 4096], F32),
    ("FH2S", [128, 32], F32), ("FW3T", [4096, D], F32),
    ("FW3S", [128, 8], F32),
]

SCRATCH = [
    ("XNF", [D, T], F32), ("XNB", [D, T], BF16),
    ("QH", [NH, DH, T], BF16), ("KH", [NH, DH, T], BF16),
    ("VSC", [128, (T // 128) * 1056], BF16), ("PG", [4096, T], BF16),
    ("MH", [NH, T], BF16), ("OH", [NH, DH, T], BF16),
    ("XMID", [D, T], F32), ("HF", [4096, T], BF16), ("X1R", [D, T], F32),
    ("XN1F", [D, T], F32), ("XN1B", [D, T], BF16),
    ("Q1H", [NH, DH, T], BF16), ("K1H", [NH, DH, T], BF16),
    ("VSC1", [128, (T // 128) * 1056], BF16), ("M1H", [NH, T], BF16),
    ("APH", [NH, T], F32), ("O1H", [NH, DH, T], BF16),
    ("XMID1", [D, T], F32), ("G1S", [4096, T], BF16),
    ("H1S", [4096, T], BF16), ("H2S", [4096, T], BF16),
    ("H3S", [4096, T], BF16),
]


def build_fused():
    em = Em("FUSED")
    em.mk_arenas()
    io = {"XF": em.din("XF", [D, T], F32)}
    for nm, shp, dt in SHARED_INPUTS:
        io[nm] = em.din(nm, shp, dt)
    for nm, shp, dt in SCRATCH:
        io[nm] = em.nc.dram_tensor(nm, shp, dt, kind="Internal").ap()
    io["YF"] = em.dout("YF", [D, T], F32)
    Bsh = mk_ln_bufs(em, 512, 56)

    def ph(pfx, base, fn, **kw):
        em.arena_reset(pfx, psum_base=base)
        fn(em=em, io=dict(io, **kw.pop("io_map", {})), **kw)
        em.barrier()

    ph("A:", 3, build_phase_a, B=Bsh)
    ph("B:", 0, build_phase_b,
       io_map=dict(QHD=io["QH"], KHD=io["KH"], VSCD=io["VSC"]))
    ph("C:", 3, build_phase_c, TG=512, B=Bsh,
       io_map=dict(OHD=io["OH"], PGD=io["PG"], XFD=io["XF"]))
    ph("D1:", 0, build_phase_d1, TG=512)
    ph("D2:", 3, build_phase_d2, TG=512, B=Bsh)
    ph("E:", 3, build_phase_e, B=Bsh,
       io_map=dict(X1RD=io["X1R"], WID=io["WI"], WOD=io["WO"], WVD=io["WV"],
                   WAD=io["WA"], WED=io["WE"], BAD=io["BA"], BED=io["BE"]))
    ph("F:", 0, build_phase_f,
       io_map=dict(Q1HD=io["Q1H"], K1HD=io["K1H"], VSC1D=io["VSC1"],
                   M1HD=io["M1H"], APHD=io["APH"]))
    ph("G:", 0, build_phase_g,
       io_map=dict(O1HD=io["O1H"], WOUT1D=io["WOUT1"], X1RD=io["X1R"]))
    ph("H1a:", 0, build_phase_h1a, TG=512)
    ph("H1b:", 0, build_phase_h1b, TG=512)
    em.arena_reset("H2:", 0)
    build_phase_h2("H2x", TG=512, em=em, io=dict(HIN=io["H1S"], WT=io["FH1T"],
                                                 SC=io["FH1S"], HOUT=io["H2S"]))
    em.barrier()
    em.arena_reset("H3:", 0)
    build_phase_h2("H3x", TG=512, em=em, io=dict(HIN=io["H2S"], WT=io["FH2T"],
                                                 SC=io["FH2S"], HOUT=io["H3S"]))
    em.barrier()
    em.arena_reset("H4:", 0)
    build_phase_h4(TG=512, em=em, io=dict(io, XMID1D=io["XMID1"]))
    return em


def kernel(**inputs):
    H = host_prep(inputs)
    NC = 8
    em = build_fused()
    nc = em.finalize()
    shared = {nm: H[nm] for nm, _, _ in SHARED_INPUTS}
    maps = [dict(shared, XF=H["XF"][b]) for b in range(NC)]
    res = run_bass_kernel_spmd(nc, maps, core_ids=list(range(NC)))
    return np.stack([np.ascontiguousarray(np.asarray(res.results[b]["YF"]).T)
                     for b in range(NC)])
